# version 06
# baseline (speedup 1.0000x reference)
import dataclasses
import math
import os
from collections import defaultdict
from contextlib import ExitStack

import numpy as np

import concourse.bass as bass
import concourse.mybir as mybir
from concourse.bass_utils import run_bass_kernel_spmd
from concourse.expressions_rust import RuntimeValue

F32 = mybir.dt.float32
BF16 = mybir.dt.bfloat16
I32 = mybir.dt.int32
U8 = mybir.dt.uint8
AF = mybir.ActivationFunctionType
ALU = mybir.AluOpType
AX = mybir.AxisListType

D = 4096
S = 4096
NBATCH = 4
HD = 128
NH = 32
NKV = 2
QKV_D = 4608
NT = 16
TOK = 2048
NG = 2
GT = 1024
NBLK = 2
NCH = 32
QCH = 36
ER = 256
EPS = 1e-5
NEG = -1e30
ROT = 64

_DEBUG = os.environ.get("MK_DEBUG", "")
ENGINES = ("sync", "gpsimd", "tensor", "vector", "scalar")


def _g_of(par, i):
    return 2 * i + ((i & 1) if par == 0 else 1 - (i & 1))


def _perm_head():
    return np.array(list(range(0, ROT, 2)) + list(range(1, ROT, 2)) +
                    list(range(ROT, HD)), dtype=np.int64)


def _host_prep(inputs):
    x = np.asarray(inputs["L0_attn_hidden_states"], np.float32)
    qkv_w = np.asarray(inputs["L0_attn_qkv_w"], np.float32)
    qkv_b = np.asarray(inputs["L0_attn_qkv_b"], np.float32)
    dense_w = np.asarray(inputs["L0_attn_dense_w"], np.float32)
    W = np.asarray(inputs["L0_ffn_W"], np.float32)
    bias = np.asarray(inputs["L0_ffn_bias"], np.float32)
    A = np.asarray(inputs["L0_ffn_A"], np.float32)
    Bf = np.asarray(inputs["L0_ffn_Bf"], np.float32)
    coeffs = np.asarray(inputs["L0_ffn_coeffs"], np.float32)
    scales = np.asarray(inputs["L0_ffn_scales"], np.float32)

    ph = _perm_head()
    perm_rows = np.arange(QKV_D)
    for h in range(NH):
        perm_rows[h * HD:(h + 1) * HD] = h * HD + ph
    for kv in range(NKV):
        base = D + kv * HD
        perm_rows[base:base + HD] = base + ph
    qkv_w_p = qkv_w[perm_rows]
    qkv_b_p = qkv_b[perm_rows]

    Pm = np.zeros((128, 128), np.float32)
    for m in range(32):
        Pm[m + 32, m] = -1.0
        Pm[m, m + 32] = 1.0

    shared = {
        "wT_qkv": np.ascontiguousarray(qkv_w_p.T),
        "b_qkv": np.ascontiguousarray(qkv_b_p.reshape(QCH, 128).T),
        "wT_ffn": np.ascontiguousarray(W.T),
        "b_ffn": np.ascontiguousarray(bias.reshape(NCH, 128).T),
        "dT": np.ascontiguousarray(dense_w.T),
        "aT": np.ascontiguousarray(A.transpose(2, 0, 1).reshape(D, ER)),
        "bS": np.ascontiguousarray(Bf.transpose(0, 2, 1).reshape(ER, D)),
        "coefr": np.ascontiguousarray(np.repeat(coeffs, 16).reshape(2, 128).T),
        "scalr": np.ascontiguousarray(np.repeat(scales, 16).reshape(2, 128).T),
        "ident": np.eye(128, dtype=np.float32),
        "permm": Pm,
        "ones_col": np.ones((128, 1), np.float32),
        "ones_row": np.ones((1, 128), np.float32),
    }

    inv = 10000.0 ** (-(np.arange(32, dtype=np.float64) / 32.0))
    c_att = 1.0 / math.sqrt(HD)
    rng = np.random.default_rng(int.from_bytes(os.urandom(8), "little"))
    nonce1 = int(rng.integers(1, (1 << 30) - 1))
    nonce2 = int(rng.integers(1, (1 << 30) - 1))

    in_maps = []
    for c in range(8):
        b, par = c // 2, c % 2
        gl = [_g_of(par, i) for i in range(NT)]
        rows = np.concatenate([np.arange(g * 128, (g + 1) * 128) for g in gl])
        m = dict(shared)
        m["xT"] = np.ascontiguousarray(x[b][rows].T)

        pos = rows.astype(np.float64)
        th = pos[None, :] * inv[:, None]
        cos = np.cos(th).astype(np.float32)
        sin = np.sin(th).astype(np.float32)
        ck = np.empty((128, TOK), np.float32)
        sk = np.empty((128, TOK), np.float32)
        ck[0:32] = cos; ck[32:64] = cos; ck[64:128] = 1.0
        sk[0:32] = sin; sk[32:64] = sin; sk[64:128] = 0.0
        m["ck"], m["sk"] = ck, sk
        m["cq"] = (ck * c_att).astype(np.float32)
        m["sq"] = (sk * c_att).astype(np.float32)

        bm = np.zeros((128, NT * 256), np.float32)
        for i in range(NT):
            g = gl[i]
            for sec in range(2):
                gk = _g_of(sec, i)
                blkm = np.zeros((128, 128), np.float32)
                if gk > g:
                    blkm[:] = NEG
                elif gk == g:
                    blkm[np.triu_indices(128, 1)] = NEG
                bm[:, i * 256 + sec * 128: i * 256 + (sec + 1) * 128] = blkm
        m["bmask"] = bm

        kw = par * (NKV * 128 * TOK)
        vw = par * (NT * 128 * ER)
        m["offs"] = np.array(
            [[kw, vw, par * 16, (1 - par) * 16, nonce1, nonce2, 0, 0]], np.int32)
        in_maps.append(m)
    return in_maps


class Plan:
    def __init__(self):
        self.seq = []
        self.cnt = defaultdict(int)

    def add(self, eng, fn, waits=(), inc=None):
        incs = []
        if inc is not None:
            if isinstance(inc, tuple):
                inc = [inc]
            for s, n in inc:
                self.cnt[s] += n
                incs.append((s, n))
        for s, v in waits:
            assert v <= self.cnt[s], f"wait on future: {s} {v} > {self.cnt[s]}"
        self.seq.append((eng, fn, [(s, v) for s, v in waits if v > 0], incs))

    def at(self, s):
        return self.cnt[s]


def _chunks(total, step):
    out, c0 = [], 0
    while c0 < total:
        out.append((c0, min(step, total - c0)))
        c0 += step
    return out


def build_program():
    nc = bass.Bass("TRN2", target_bir_lowering=False)

    d_x = nc.dram_tensor("xT", [D, TOK], F32, kind="ExternalInput").ap()
    d_bm = nc.dram_tensor("bmask", [128, NT * 256], F32, kind="ExternalInput").ap()
    d_cq = nc.dram_tensor("cq", [128, TOK], F32, kind="ExternalInput").ap()
    d_sq = nc.dram_tensor("sq", [128, TOK], F32, kind="ExternalInput").ap()
    d_ck = nc.dram_tensor("ck", [128, TOK], F32, kind="ExternalInput").ap()
    d_sk = nc.dram_tensor("sk", [128, TOK], F32, kind="ExternalInput").ap()
    d_offs = nc.dram_tensor("offs", [1, 8], I32, kind="ExternalInput").ap()
    d_wq = nc.dram_tensor("wT_qkv", [D, QKV_D], F32, kind="ExternalInput").ap()
    d_bq = nc.dram_tensor("b_qkv", [128, QCH], F32, kind="ExternalInput").ap()
    d_wf = nc.dram_tensor("wT_ffn", [D, D], F32, kind="ExternalInput").ap()
    d_bfv = nc.dram_tensor("b_ffn", [128, NCH], F32, kind="ExternalInput").ap()
    d_dT = nc.dram_tensor("dT", [D, D], F32, kind="ExternalInput").ap()
    d_aT = nc.dram_tensor("aT", [D, ER], F32, kind="ExternalInput").ap()
    d_bS = nc.dram_tensor("bS", [ER, D], F32, kind="ExternalInput").ap()
    d_coef = nc.dram_tensor("coefr", [128, 2], F32, kind="ExternalInput").ap()
    d_scal = nc.dram_tensor("scalr", [128, 2], F32, kind="ExternalInput").ap()
    d_id = nc.dram_tensor("ident", [128, 128], F32, kind="ExternalInput").ap()
    d_pm = nc.dram_tensor("permm", [128, 128], F32, kind="ExternalInput").ap()
    d_oc = nc.dram_tensor("ones_col", [128, 1], F32, kind="ExternalInput").ap()
    d_or = nc.dram_tensor("ones_row", [1, 128], F32, kind="ExternalInput").ap()
    d_out = nc.dram_tensor("outT", [D, TOK], F32, kind="ExternalOutput").ap()

    d_q = nc.dram_tensor("q_hbm", [NCH, 128, TOK], BF16, kind="Internal").ap()
    d_xm = nc.dram_tensor("xmid", [NCH, 128, TOK], F32, kind="Internal").ap()
    d_xa = nc.dram_tensor("xatt", [NCH, 128, TOK], F32, kind="Internal").ap()
    d_ctx = nc.dram_tensor("ctx", [NCH, 128, TOK], BF16, kind="Internal").ap()
    d_shk = nc.dram_tensor("sh_k", [2, NKV, 128, TOK], BF16, kind="Internal",
                           addr_space="Shared").ap()
    d_shv = nc.dram_tensor("sh_v", [2, NT, 128, ER], BF16, kind="Internal",
                           addr_space="Shared").ap()
    d_shf = nc.dram_tensor("sh_flag", [2, 16], I32, kind="Internal",
                           addr_space="Shared").ap()
    d_dbg = nc.dram_tensor("dbg", [128, 16384], F32, kind="Internal").ap()

    cstk = ExitStack()

    def sbt(name, shape, dt=F32):
        return cstk.enter_context(nc.sbuf_tensor(name, shape, dt))

    def pst(name, shape, dt=F32):
        return cstk.enter_context(nc.psum_tensor(name, shape, dt))

    identb = sbt("identb", [128, 128], BF16)
    permb = sbt("permb", [128, 128], BF16)
    onescol = sbt("onescol", [128, 1], BF16)
    onesrow = sbt("onesrow", [1, 128], BF16)
    bq_sb = sbt("bq_sb", [128, QCH])
    bfh_sb = sbt("bfh_sb", [128, NCH])
    wvec = sbt("wvec", [128, 2])
    scal_tmp = sbt("scal_tmp", [128, 2])
    offs_sb = sbt("offs_sb", [1, 8], I32)
    aSb = sbt("aSb", [128, NCH * ER], BF16)
    bSb = sbt("bSb", [128, 2 * D], BF16)
    a1row = sbt("a1row", [1, ER], BF16)
    w1f_row = sbt("w1f_row", [1, D], BF16)
    w1q_row = sbt("w1q_row", [1, QKV_D], BF16)
    negm_row = sbt("negm_row", [1, GT], BF16)
    rhi_row = sbt("rhi_row", [1, GT], BF16)
    rlo_row = sbt("rlo_row", [1, GT], BF16)
    mrow = sbt("mrow", [1, GT])
    vrow = sbt("vrow", [1, GT])
    vtmp = sbt("vtmp", [1, GT])
    rb_sb = sbt("rb_sb", [128, GT])
    flagbuf = sbt("flagbuf", [1, 16], I32)

    pA = pst("pA", [128, 512])
    pB = pst("pB", [128, 512])
    pC = pst("pC", [128, 512])
    pD = pst("pD", [128, 512])
    pE = pst("pE", [128, 512])
    pF = pst("pF", [128, 512])
    pR = pst("pR", [128, 512])
    pT = pst("pT", [128, 1024], BF16)

    ARENA = 135488
    arena = nc.alloc_sbuf_tensor("arena", [128, ARENA], U8)
    abase = nc.lookup_mloc(arena).addr

    def at(name, off, shape, dt=F32):
        assert off % 64 == 0
        n = mybir.dt.size(dt)
        for s in shape[1:]:
            n *= s
        assert off + n <= ARENA, f"{name}: arena overflow {off}+{n}"
        return nc.alloc_sbuf_tensor_at(name, shape, dt, offset=abase + off)

    xbig = at("xbig", 0, [128, NCH * GT], BF16)
    tabs = at("tabs", 65536, [128, 8 * 512])
    wslab = [at(f"wslab{j}", 81920 + j * 8192, [128, 4096], BF16) for j in range(2)]
    xf_t = [at(f"xf{j}", 98304 + j * 2048, [128, 512]) for j in range(4)]
    xm_st = [at(f"xmst{j}", 106496 + j * 2048, [128, 512]) for j in range(2)]
    t1_t = [at(f"t1{j}", 110592 + j * 2048, [128, 512]) for j in range(2)]
    qb_t = [at(f"qb{j}", 114688 + j * 1024, [128, 512], BF16) for j in range(2)]
    rq_t = [at(f"rq{j}", 116736 + j * 1024, [128, 512], BF16) for j in range(2)]
    xab = at("xab", 118784, [128, 4 * 512], BF16)
    sq_t = [at(f"sq{j}", 122880 + j * 1024, [128, 512], BF16) for j in range(2)]
    vst_t = [at(f"vst{j}", 124928 + j * 2048, [128, 4 * 256], BF16)
             for j in range(2)]
    ksb = at("ksb", 0, [128, 2 * NKV * TOK], BF16)
    vsb = at("vsb", 16384, [128, 2 * NT * ER], BF16)
    qt_t = [at(f"qt{j}", 32768 + j * 8192, [128, NCH * 128], BF16) for j in range(2)]
    srow_t = [at(f"srow{j}", 49152 + j * 16384, [128, 4096])
              for j in range(2)]
    p4 = at("p4", 81920, [128, 4 * 4096], BF16)
    ptb = at("ptb", 114688, [128, 16 * 512], BF16)
    mt_t = [at(f"mt{j}", 131072 + j * 1024, [128, 256]) for j in range(2)]
    ctxst = [at(f"ctxst{j}", 133120 + j * 1024, [128, 512], BF16) for j in range(2)]
    sums4 = at("sums4", 135168, [128, 8])
    rs4 = at("rs4", 135232, [128, 8])
    negmx = at("negmx", 135296, [128, 8])
    cmax = at("cmax", 135360, [128, 24])
    ctxb_t = [at(f"ctxb{j}", j * 32768, [128, NCH * 512], BF16) for j in range(2)]

    plan = Plan()
    ms = {}
    sems = {}
    _rt_regs = {}

    def add(eng, fn, waits=(), inc=None):
        plan.add(eng, fn, waits, inc)

    def _rt(eng, key):
        idx = {"kw": 0, "vw": 1, "fw": 2, "fr": 3, "n1": 4, "n2": 5}[key]
        name = f"rt_{key}"
        if name not in _rt_regs:
            r = eng.alloc_register(name)
            eng.reg_load(r, offs_sb[0:1, idx:idx + 1])
            _rt_regs[name] = RuntimeValue(val=r, min_val=0, max_val=1 << 30)
        return _rt_regs[name]

    def barrier(k):
        for e in ENGINES:
            add(e, lambda eng: eng.nop(), inc=("bar", 1))
        tgt = 5 * k
        assert plan.at("bar") == tgt
        for e in ENGINES:
            add(e, lambda eng, t=tgt: None, waits=[("bar", tgt)])

    def preloads():
        for t, dsrc in ((identb, d_id), (permb, d_pm), (onescol, d_oc),
                        (onesrow, d_or)):
            add("gpsimd", lambda g, t=t, d_=dsrc: g.dma_start(out=t[:], in_=d_),
                inc=("dCon", 16))
        add("gpsimd", lambda g: g.dma_start(
            out=aSb[:].rearrange("p (i e) -> p i e", e=ER),
            in_=d_aT.rearrange("(i p) e -> p i e", p=128)), inc=("dCon", 16))
        add("gpsimd", lambda g: g.dma_start(
            out=bSb[:].rearrange("p (j o) -> p j o", o=D),
            in_=d_bS.rearrange("(j p) o -> p j o", p=128)), inc=("dCon", 16))
        wconst = ("dCon", plan.at("dCon"))
        for t, dsrc in ((bq_sb, d_bq), (bfh_sb, d_bfv), (wvec, d_coef),
                        (offs_sb, d_offs), (scal_tmp, d_scal)):
            add("sync", lambda s, t=t, d_=dsrc: s.dma_start(out=t[:], in_=d_),
                inc=("dCn2", 16))
        wx = ("dCn2", plan.at("dCn2"))
        add("sync", lambda s: [_rt(s, k) for k in
                               ("kw", "vw", "fw", "fr", "n1", "n2")] and s.nop(),
            waits=[wx])
        add("vector", lambda v: v.tensor_mul(wvec[:], wvec[:], scal_tmp[:]),
            waits=[wx], inc=("cV", 1))
        add("vector", lambda v: v.tensor_scalar_mul(bfh_sb[:], bfh_sb[:], 0.5),
            inc=("cV", 1))
        wv = ("cV", plan.at("cV"))
        for j in range(2):
            add("vector", lambda v, j=j: v.tensor_scalar(
                bSb[:, j * D:(j + 1) * D], bSb[:, j * D:(j + 1) * D],
                wvec[:, j:j + 1], None, ALU.mult),
                waits=[wv, wconst], inc=("cV", 1))
        for i in range(NCH):
            add("tensor", lambda t, i=i: t.matmul(
                pR[0:1, 0:ER], onescol[:], aSb[:, i * ER:(i + 1) * ER],
                start=(i == 0), stop=(i == NCH - 1)),
                waits=[wconst] if i == 0 else (),
                inc=("cT", 1) if i == NCH - 1 else None)
        wt = ("cT", plan.at("cT"))
        add("scalar", lambda s: s.activation(a1row[:], pR[0:1, 0:ER], AF.Copy,
                                             bias=0.0, scale=1.0),
            waits=[wt], inc=("cS", 1))
        ms["a1_done"] = ("cS", plan.at("cS"))

    def ln_finalize(half, tag):
        wt = ("cT", plan.at("cT"))
        for blk in range(NBLK):
            sl = slice(blk * 512, (blk + 1) * 512)
            sbank, qbank = (pC, pD)[blk], (pE, pF)[blk]
            add("vector", lambda v, sl=sl, b=sbank: v.tensor_scalar_mul(
                mrow[0:1, sl], b[0:1, 0:512], 1.0 / D), waits=[wt], inc=("cV", 1))
            add("vector", lambda v, sl=sl, b=qbank: v.tensor_scalar_mul(
                vrow[0:1, sl], b[0:1, 0:512], 1.0 / D), waits=[wt], inc=("cV", 1))
        add("vector", lambda v: v.tensor_mul(vtmp[0:1, :], mrow[0:1, :],
                                             mrow[0:1, :]), inc=("cV", 1))
        add("vector", lambda v: v.tensor_sub(vrow[0:1, :], vrow[0:1, :],
                                             vtmp[0:1, :]), inc=("cV", 1))
        add("vector", lambda v: v.tensor_scalar_mul(negm_row[0:1, :],
                                                    mrow[0:1, :], -1.0),
            inc=("cV", 1))
        add("vector", lambda v: v.tensor_scalar_add(vrow[0:1, :], vrow[0:1, :],
                                                    EPS), inc=("cV", 1))
        wv = ("cV", plan.at("cV"))
        add("scalar", lambda s: s.activation(vtmp[0:1, :], vrow[0:1, :], AF.Ln,
                                             bias=0.0, scale=1.0),
            waits=[wv], inc=("cS", 1))
        add("scalar", lambda s: s.activation(
            vrow[0:1, :], vtmp[0:1, :], AF.Exp, bias=0.0, scale=-0.5),
            inc=("cS", 1))
        wsc = ("cS", plan.at("cS"))
        if half:
            add("vector", lambda v: v.tensor_scalar_mul(vrow[0:1, :],
                                                        vrow[0:1, :], 0.5),
                waits=[wsc], inc=("cV", 1))
            wsc = ("cV", plan.at("cV"))
        add("vector", lambda v: v.tensor_copy(rhi_row[0:1, :], vrow[0:1, :]),
            waits=[wsc], inc=("cV", 1))
        add("vector", lambda v: v.tensor_sub(rlo_row[0:1, :], vrow[0:1, :],
                                             rhi_row[0:1, :]), inc=("cV", 1))
        wv = ("cV", plan.at("cV"))
        for blk in range(NBLK):
            sl = slice(blk * 512, (blk + 1) * 512)
            add("tensor", lambda t, sl=sl: t.matmul(
                pE[:], onesrow[:], rhi_row[0:1, sl], start=True, stop=False),
                waits=[wv, ("cS", plan.at("cS"))])
            add("tensor", lambda t, sl=sl: t.matmul(
                pE[:], onesrow[:], rlo_row[0:1, sl], start=False, stop=True),
                inc=("cT", 1))
            wt2 = ("cT", plan.at("cT"))
            add("scalar", lambda s, sl=sl: s.activation(
                rb_sb[:, sl], pE[:], AF.Copy, bias=0.0, scale=1.0),
                waits=[wt2, wv], inc=("cS", 1))
        ms[f"rdy_{tag}"] = ("cS", plan.at("cS"))

    def stats_ops(u, oo, blk, src_f32, first, last, tag, wsrc):
        cb = qb_t[u % 2]
        wprev = [wsrc] + ([ms[(tag, "stq", u - 2)]] if u >= 2 else [])
        add("vector", lambda v, xm=src_f32, cb=cb: v.tensor_copy(cb[:], xm[:]),
            waits=wprev, inc=("cV", 1))
        wcb = ("cV", plan.at("cV"))
        sqt = sq_t[u % 2]
        add("scalar", lambda s, cb=cb, sqt=sqt: s.activation(
            sqt[:], cb[:], AF.Square, bias=0.0, scale=1.0),
            waits=[wcb], inc=("cS", 1))
        wsq = ("cS", plan.at("cS"))
        sbank, qbank = (pC, pD)[blk], (pE, pF)[blk]
        sw = [wcb]
        if first and ("stats_free" in ms):
            sw.append(ms["stats_free"])
        add("tensor", lambda t, cb=cb, b=sbank, first=first, last=last: t.matmul(
            b[0:1, 0:512], onescol[:], cb[:], start=first, stop=last),
            waits=sw)
        add("tensor", lambda t, sqt=sqt, b=qbank, first=first, last=last: t.matmul(
            b[0:1, 0:512], onescol[:], sqt[:], start=first, stop=last),
            waits=[wsq], inc=("cT", 1))
        ms[(tag, "stq", u)] = ("cT", plan.at("cT"))

    def xa_adapters(tag, wsrc):
        for blk in range(NBLK):
            for j in range(2):
                bank = (pC, pD)[j]
                for i in range(NCH):
                    w0 = []
                    if i == 0:
                        w0 = [ms[f"rdy_{tag}"], wsrc]
                        if blk == 0 and j == 0 and "stats_free" in ms:
                            w0.append(ms["stats_free"])
                    add("tensor", lambda t, b=bank, i=i, j=j, blk=blk: t.matmul(
                        b[:], aSb[:, i * ER + j * 128: i * ER + (j + 1) * 128],
                        xbig[:, i * GT + blk * 512: i * GT + blk * 512 + 512],
                        start=(i == 0), stop=False), waits=w0)
                add("tensor", lambda t, b=bank, j=j, blk=blk: t.matmul(
                    b[:], a1row[0:1, j * 128:(j + 1) * 128],
                    negm_row[0:1, blk * 512: blk * 512 + 512],
                    start=False, stop=True), inc=("cT", 1))
                wt = ("cT", plan.at("cT"))
                add("scalar", lambda s, b=bank, j=j, blk=blk: s.activation(
                    xab[:, (blk * 2 + j) * 512:(blk * 2 + j + 1) * 512], b[:],
                    AF.Copy, bias=0.0, scale=1.0), waits=[wt], inc=("cS", 1))
        ms["xa_rdy"] = ("cS", plan.at("cS"))
        ms["stats_free"] = ms["xa_rdy"]

    def phase1_group(g, stop_after=None):
        gcol = g * GT
        waits = [ms["xbig_free"]] if g > 0 else []
        prev_ev = ("cV", plan.at("cV")) if g > 0 else None
        add("gpsimd", lambda gp: gp.dma_start(
            out=xbig[:].rearrange("p (i t) -> p i t", t=GT),
            in_=d_x[:, gcol:gcol + GT].rearrange("(i p) t -> p i t", p=128)),
            waits=waits, inc=("dXB", 16))
        wxb = ("dXB", plan.at("dXB"))
        tw = [ms["tabs_free"]] if g > 0 else []
        for j, dsrc in enumerate((d_cq, d_sq, d_ck, d_sk)):
            add("sync", lambda s, j=j, d_=dsrc: s.dma_start(
                out=tabs[:, j * 1024:(j + 1) * 1024], in_=d_[:, gcol:gcol + GT]),
                waits=tw, inc=("dTab", 16))
        wtab = ("dTab", plan.at("dTab"))

        for ci in range(NCH):
            for blk in range(NBLK):
                u = ci * NBLK + blk
                xsl = xbig[:, ci * GT + blk * 512: ci * GT + blk * 512 + 512]
                sqt = sq_t[u % 2]
                swq = [wxb] if u < 2 else [ms[("l1", "stq", u - 2)]]
                add("scalar", lambda s, xsl=xsl, sqt=sqt: s.activation(
                    sqt[:], xsl, AF.Square, bias=0.0, scale=1.0),
                    waits=swq, inc=("cS", 1))
                wsq = ("cS", plan.at("cS"))
                sbank, qbank = (pC, pD)[blk], (pE, pF)[blk]
                sw = [wxb]
                if ci == 0 and "stats_free" in ms:
                    sw.append(ms["stats_free"])
                add("tensor", lambda t, xsl=xsl, b=sbank, ci=ci: t.matmul(
                    b[0:1, 0:512], onescol[:], xsl,
                    start=(ci == 0), stop=(ci == NCH - 1)),
                    waits=sw if ci == 0 else ())
                add("tensor", lambda t, sqt=sqt, b=qbank, ci=ci: t.matmul(
                    b[0:1, 0:512], onescol[:], sqt[:],
                    start=(ci == 0), stop=(ci == NCH - 1)),
                    waits=[wsq], inc=("cT", 1))
                ms[("l1", "stq", u)] = ("cT", plan.at("cT"))
        ln_finalize(half=True, tag=f"ln1_{g}")
        ms["stats_free"] = ms[f"rdy_ln1_{g}"]
        if stop_after == "ln1":
            return
        xa_adapters(f"ln1_{g}", wxb)
        if stop_after == "xa":
            return

        def slab_load(mat, o, wprev):
            sem = ("dSlA", "dSlB")[o % 2]
            src = mat[:, o * 128:(o + 1) * 128].rearrange("(i p) c -> p i c", p=128)
            add("gpsimd", lambda gp, o=o, src=src: gp.dma_start(
                out=wslab[o % 2][:].rearrange("p (i c) -> p i c", c=128),
                in_=src), waits=wprev, inc=(sem, 16))
            return (sem, plan.at(sem))

        slw = {0: slab_load(d_wf, 0, []), 1: slab_load(d_wf, 1, [])}
        for oo in range(NCH):
            slab = wslab[oo % 2]
            if g == 0:
                for i in range(NCH):
                    w0 = [slw[oo]]
                    if oo >= 1:
                        w0.append(ms["w1ev"])
                    else:
                        w0.append(ms["a1_done"])
                    add("tensor", lambda t, slab=slab, i=i: t.matmul(
                        pR[0:1, 0:128], onescol[:], slab[:, i * 128:(i + 1) * 128],
                        start=(i == 0), stop=(i == NCH - 1)),
                        waits=w0 if i == 0 else (),
                        inc=("cT", 1) if i == NCH - 1 else None)
                wt = ("cT", plan.at("cT"))
                add("scalar", lambda s, oo=oo: s.activation(
                    w1f_row[0:1, oo * 128:(oo + 1) * 128], pR[0:1, 0:128],
                    AF.Copy, bias=0.0, scale=1.0), waits=[wt], inc=("cS", 1))
                ms["w1ev"] = ("cS", plan.at("cS"))
            ww1 = ms["w1ev"] if g == 0 else None
            for blk in range(NBLK):
                u = oo * NBLK + blk
                bank = (pA, pB)[u % 2]
                xf = xf_t[u % 4]
                xsem = f"dXf{u % 4}"
                xw = [ms[("p1", "res", u - 4)]] if u >= 4 else []
                add("sync", lambda s, oo=oo, blk=blk, xf=xf: s.dma_start(
                    out=xf[:], in_=d_x[oo * 128:(oo + 1) * 128,
                                       gcol + blk * 512: gcol + blk * 512 + 512]),
                    waits=xw, inc=(xsem, 16))
                wxf = (xsem, plan.at(xsem))
                mw = [slw[oo], ms["xa_rdy"]]
                if u >= 2:
                    mw.append(ms[("p1", "t1", u - 2)])
                elif prev_ev is not None:
                    mw.append(prev_ev)
                for i in range(NCH):
                    add("tensor", lambda t, slab=slab, i=i, blk=blk, b=bank: t.matmul(
                        b[:], slab[:, i * 128:(i + 1) * 128],
                        xbig[:, i * GT + blk * 512: i * GT + blk * 512 + 512],
                        start=(i == 0), stop=False), waits=mw if i == 0 else ())
                add("tensor", lambda t, oo=oo, blk=blk, b=bank: t.matmul(
                    b[:], w1f_row[0:1, oo * 128:(oo + 1) * 128],
                    negm_row[0:1, blk * 512: blk * 512 + 512],
                    start=False, stop=False),
                    waits=[ww1] if (g == 0 and blk == 0) else ())
                for j in range(2):
                    add("tensor", lambda t, oo=oo, blk=blk, j=j, b=bank: t.matmul(
                        b[:], bSb[:, j * D + oo * 128: j * D + (oo + 1) * 128],
                        xab[:, (blk * 2 + j) * 512:(blk * 2 + j + 1) * 512],
                        start=False, stop=(j == 1)),
                        inc=("cT", 1) if j == 1 else None)
                wmm = ("cT", plan.at("cT"))
                if blk == NBLK - 1:
                    ms[("p1", "slabdone", oo)] = wmm
                t1 = t1_t[u % 2]
                xm = xm_st[u % 2]
                wt1 = [wmm]
                if u >= 2:
                    wt1.append(ms[("p1", "xmio", u - 2)])
                add("vector", lambda v, b=bank, blk=blk, t1=t1: v.tensor_tensor(
                    t1[:], b[:], rb_sb[:, blk * 512: blk * 512 + 512],
                    op=ALU.mult), waits=wt1, inc=("cV", 1))
                ms[("p1", "t1", u)] = ("cV", plan.at("cV"))
                add("vector", lambda v, t1=t1, xm=xm, oo=oo, xf=xf:
                    v.scalar_tensor_tensor(xm[:], t1[:], bfh_sb[:, oo:oo + 1],
                                           xf[:], op0=ALU.add, op1=ALU.add),
                    waits=[wxf, ("dXm", plan.at("dXm"))], inc=("cV", 1))
                wev = ("cV", plan.at("cV"))
                add("sync", lambda s, oo=oo, blk=blk, xm=xm: s.dma_start(
                    out=d_xm[oo, :, gcol + blk * 512: gcol + blk * 512 + 512],
                    in_=xm[:]), waits=[wev], inc=("dXm", 16))
                ms[("p1", "res", u)] = ("dXm", plan.at("dXm"))
                stats_ops(u, oo, blk, xm, first=(oo == 0), last=(oo == NCH - 1),
                          tag="l2", wsrc=wev)
                ms[("p1", "xmio", u)] = ("cV", plan.at("cV"))
            if oo + 2 < NCH:
                slw[oo + 2] = slab_load(d_wf, oo + 2, [ms[("p1", "slabdone", oo)]])
        ms["ffn_done"] = ("cT", plan.at("cT"))
        ms["xmid_stored"] = ("dXm", plan.at("dXm"))
        if stop_after == "ffn":
            return
        ln_finalize(half=False, tag=f"ln2_{g}")
        if stop_after == "ln2":
            return
        ms["stats_free"] = ms[f"rdy_ln2_{g}"]

        add("gpsimd", lambda gp: gp.dma_start(
            out=xbig[:].rearrange("p (i t) -> p i t", t=GT),
            in_=d_xm[:, :, gcol:gcol + GT].rearrange("i p t -> p i t")),
            waits=[ms["ffn_done"], ms["xmid_stored"]], inc=("dXB", 16))
        wxm = ("dXB", plan.at("dXB"))

        def qslab_load(o, wprev):
            sem = ("dSlA", "dSlB")[o % 2]
            src = d_wq[:, o * 128:(o + 1) * 128].rearrange("(i p) c -> p i c", p=128)
            add("gpsimd", lambda gp, o=o, src=src: gp.dma_start(
                out=wslab[o % 2][:].rearrange("p (i c) -> p i c", c=128),
                in_=src), waits=wprev, inc=(sem, 16))
            return (sem, plan.at(sem))

        qlw = {0: qslab_load(0, []), 1: qslab_load(1, [])}
        for oo in range(QCH):
            slab = wslab[oo % 2]
            if g == 0:
                for i in range(NCH):
                    w0 = [qlw[oo], ms["w1ev"]]
                    add("tensor", lambda t, slab=slab, i=i: t.matmul(
                        pR[0:1, 0:128], onescol[:], slab[:, i * 128:(i + 1) * 128],
                        start=(i == 0), stop=(i == NCH - 1)),
                        waits=w0 if i == 0 else (),
                        inc=("cT", 1) if i == NCH - 1 else None)
                wt = ("cT", plan.at("cT"))
                add("scalar", lambda s, oo=oo: s.activation(
                    w1q_row[0:1, oo * 128:(oo + 1) * 128], pR[0:1, 0:128],
                    AF.Copy, bias=0.0, scale=1.0), waits=[wt], inc=("cS", 1))
                ms["w1ev"] = ("cS", plan.at("cS"))
            for blk in range(NBLK):
                u = oo * NBLK + blk
                bank = (pA, pB)[u % 2]
                mw = [qlw[oo], wxm, ms[f"rdy_ln2_{g}"]]
                if u >= 2:
                    mw.append(ms[("qk", "ev", u - 2)])
                for i in range(NCH):
                    add("tensor", lambda t, slab=slab, i=i, blk=blk, b=bank: t.matmul(
                        b[:], slab[:, i * 128:(i + 1) * 128],
                        xbig[:, i * GT + blk * 512: i * GT + blk * 512 + 512],
                        start=(i == 0), stop=False), waits=mw if i == 0 else ())
                add("tensor", lambda t, oo=oo, blk=blk, b=bank: t.matmul(
                    b[:], w1q_row[0:1, oo * 128:(oo + 1) * 128],
                    negm_row[0:1, blk * 512: blk * 512 + 512],
                    start=False, stop=True),
                    waits=[ms["w1ev"]] if (g == 0 and blk == 0) else (),
                    inc=("cT", 1))
                wmm = ("cT", plan.at("cT"))
                if blk == NBLK - 1:
                    ms[("qk", "slabdone", oo)] = wmm
                t1 = t1_t[u % 2]
                qb = qb_t[u % 2]
                wt1 = [wmm]
                if u >= 2:
                    wt1.append(ms[("qk", "cons", u - 2)])
                add("vector", lambda v, b=bank, blk=blk, t1=t1: v.tensor_tensor(
                    t1[:], b[:], rb_sb[:, blk * 512: blk * 512 + 512],
                    op=ALU.mult), waits=wt1, inc=("cV", 1))
                add("vector", lambda v, t1=t1, qb=qb, oo=oo: v.tensor_scalar(
                    qb[:], t1[:], bq_sb[:, oo:oo + 1], None, ALU.add),
                    inc=("cV", 1))
                ms[("qk", "ev", u)] = ("cV", plan.at("cV"))
                wqb = ms[("qk", "ev", u)]
                if oo < 34:
                    is_q = oo < 32
                    cbase = (0 if is_q else 2) * 1024 + blk * 512
                    sbase = (1 if is_q else 3) * 1024 + blk * 512
                    bankp = (pC, pD)[u % 2]
                    wpp = [wqb]
                    if u >= 2:
                        wpp.append(ms[("qk", "permev", u - 2)])
                    add("tensor", lambda t, qb=qb, bp=bankp: t.matmul(
                        bp[:], permb[:], qb[:], start=True, stop=True),
                        waits=wpp, inc=("cT", 1))
                    wpm = ("cT", plan.at("cT"))
                    t1b = t1_t[u % 2]
                    add("vector", lambda v, bp=bankp, sbase=sbase, t1b=t1b:
                        v.tensor_tensor(t1b[:], bp[:],
                                        tabs[:, sbase:sbase + 512], op=ALU.mult),
                        waits=[wpm, wtab], inc=("cV", 1))
                    ms[("qk", "permev", u)] = ("cV", plan.at("cV"))
                    xm2 = xm_st[u % 2]
                    add("vector", lambda v, qb=qb, cbase=cbase, xm2=xm2:
                        v.tensor_tensor(xm2[:], qb[:],
                                        tabs[:, cbase:cbase + 512], op=ALU.mult),
                        inc=("cV", 1))
                    rq = rq_t[u % 2]
                    wrq_prev = [ms[("qk", "rqio", u - 2)]] if u >= 2 else []
                    add("vector", lambda v, t1b=t1b, xm2=xm2, rq=rq: v.tensor_add(
                        rq[:], t1b[:], xm2[:]), waits=wrq_prev, inc=("cV", 1))
                    wrq = ("cV", plan.at("cV"))
                    if is_q:
                        add("sync", lambda s, oo=oo, blk=blk, rq=rq: s.dma_start(
                            out=d_q[oo, :, gcol + blk * 512: gcol + blk * 512 + 512],
                            in_=rq[:]), waits=[wrq], inc=("dQ", 16))
                        ms[("qk", "rqio", u)] = ("dQ", plan.at("dQ"))
                    else:
                        kvh = oo - 32
                        coff = kvh * (128 * TOK) + gcol + blk * 512
                        add("sync", lambda s, rq=rq, coff=coff: s.dma_start(
                            out=dataclasses.replace(
                                d_shk[0, 0, :, 0:512], offset=_rt(s, "kw") + coff),
                            in_=rq[:]), waits=[wrq], inc=("dKw", 16))
                        ms[("qk", "rqio", u)] = ("dKw", plan.at("dKw"))
                    ms[("qk", "cons", u)] = ("cV", plan.at("cV"))
                else:
                    vh = oo - 34
                    wv0 = [wqb]
                    if ("qk", "vstlast") in ms:
                        wv0.append(ms[("qk", "vstlast")])
                        wv0.append(("dV", plan.at("dV")))
                    vstb = vst_t[blk]
                    for pt in range(4):
                        add("tensor", lambda t, qb=qb, pt=pt: t.transpose(
                            pT[:, pt * 128:(pt + 1) * 128],
                            qb[:, pt * 128:(pt + 1) * 128], identb[:]),
                            waits=wv0 if pt == 0 else (),
                            inc=("cT", 1) if pt == 3 else None)
                    wtr = ("cT", plan.at("cT"))
                    add("scalar", lambda s, vh=vh, vstb=vstb: s.activation(
                        vstb[:].rearrange("p (pt c) -> p pt c", c=ER)[
                            :, :, vh * 128:(vh + 1) * 128],
                        pT[:, 0:512].rearrange("p (pt c) -> p pt c", c=128),
                        AF.Copy, bias=0.0, scale=1.0),
                        waits=[wtr], inc=("cS", 1))
                    wvs = ("cS", plan.at("cS"))
                    ms[("qk", "vstlast")] = wvs
                    if vh == 1:
                        ptg = g * 8 + blk * 4
                        coff = ptg * (128 * ER)
                        add("sync", lambda s, coff=coff, vstb=vstb: s.dma_start(
                            out=dataclasses.replace(
                                d_shv[0, 0:4, :, :].rearrange("l p c -> p l c"),
                                offset=_rt(s, "vw") + coff),
                            in_=vstb[:].rearrange("p (pt c) -> p pt c", c=ER)),
                            waits=[wvs], inc=("dV", 16))
                    ms[("qk", "cons", u)] = ("cS", plan.at("cS"))
            if oo + 2 < QCH:
                qlw[oo + 2] = qslab_load(oo + 2, [ms[("qk", "slabdone", oo)]])
        ms["xbig_free"] = ("cT", plan.at("cT"))
        ms["tabs_free"] = ("cV", plan.at("cV"))
        ms[f"qkv_done_{g}"] = [("dQ", plan.at("dQ")), ("dKw", plan.at("dKw")),
                               ("dV", plan.at("dV"))]

    def flag_and_poll():
        add("sync", lambda s: s.dma_start(
            out=dataclasses.replace(d_shf[0, 0:2], offset=_rt(s, "fw")),
            in_=offs_sb[0:1, 4:6]), waits=ms["qkv_done_1"], inc=("dF", 16))

        def poll(s):
            rflag = s.alloc_register("rflag")
            rcnt = s.alloc_register("rcnt")
            rv_f = RuntimeValue(val=rflag, min_val=0, max_val=(1 << 30) - 1)
            rv_c = RuntimeValue(val=rcnt, min_val=0, max_val=(1 << 30) - 1)
            s.reg_mov(rcnt, 0)
            for widx, nkey in ((0, "n1"), (1, "n2")):
                nv = _rt(s, nkey)
                src = dataclasses.replace(d_shf[0, 0:2], offset=_rt(s, "fr"))
                s.reg_mov(rflag, 0)
                with s.While(lambda: rv_f != nv):
                    s.reg_add(rcnt, rcnt, 16)
                    s.dma_start(out=flagbuf[0:1, 0:2], in_=src).then_inc(
                        sems["pol"], 16)
                    s.wait_ge(sems["pol"], rv_c)
                    s.reg_load(rflag, flagbuf[0:1, widx:widx + 1])
            return s.nop()
        add("sync", poll)

    def phase2(only_tiles=None, only_hg=None, dump=False, only_h4=None):
        for sec in range(2):
            for kvh in range(NKV):
                add("sync", lambda s, sec=sec, kvh=kvh: s.dma_start(
                    out=ksb[:, (sec * NKV + kvh) * TOK:(sec * NKV + kvh + 1) * TOK],
                    in_=d_shk[sec, kvh, :, :]), inc=("dKV", 16))
        add("sync", lambda s: s.dma_start(
            out=vsb[:].rearrange("p (u c) -> p u c", c=ER),
            in_=d_shv.rearrange("h l p c -> p (h l) c")), inc=("dKV", 16))
        wkv = ("dKV", plan.at("dKV"))
        p4_free = None
        last_exp = {0: None, 1: None}
        last_norm = None
        scnt = [0]
        hpar = [0]
        for i in (range(NT) if only_tiles is None else only_tiles):
            seclen = (i + 1) * 128
            qt = qt_t[i % 2]
            qsem = f"dQt{i % 2}"
            qw = [] if i < 2 else [ms[("p2q", i - 2)]]
            add("sync", lambda s, i=i, qt=qt: s.dma_start(
                out=qt[:].rearrange("p (c q) -> p c q", q=128),
                in_=d_q[:, :, i * 128:(i + 1) * 128].rearrange("c p q -> p c q")),
                waits=qw, inc=(qsem, 16))
            wq = (qsem, plan.at(qsem))
            mt = mt_t[i % 2]
            msem = f"dMt{i % 2}"
            mw = [] if i < 2 else [ms[("p2m", i - 2)]]
            add("sync", lambda s, i=i, mt=mt: s.dma_start(
                out=mt[:], in_=d_bm[:, i * 256:(i + 1) * 256]),
                waits=mw, inc=(msem, 16))
            wmt = (msem, plan.at(msem))
            for hg in (range(8) if only_hg is None else only_hg):
                kvh = hg // 4
                otb = (pC, pD)[hg % 2]
                for h4 in (range(4) if only_h4 is None else only_h4):
                    head = hg * 4 + h4
                    par_ = hpar[0] % 2
                    hpar[0] += 1
                    srow = srow_t[par_]
                    col = h4 + 4 * (hg % 2)
                    cm0 = par_ * 12
                    ncm = 0
                    qsl = qt[:, head * 128:(head + 1) * 128]
                    for sec in range(2):
                        kbase = (sec * NKV + kvh) * TOK
                        sb0 = sec * seclen
                        for c0, cw in _chunks(i * 128, 512):
                            nsc = scnt[0]
                            bank = (pA, pB)[nsc % 2]
                            sw = [wkv, wq]
                            if ("p2sev", nsc - 2) in ms:
                                prev = ms[("p2sev", nsc - 2)]
                                sw.extend(prev if isinstance(prev, list) else [prev])
                            add("tensor", lambda t, b=bank, qsl=qsl, kbase=kbase,
                                c0=c0, cw=cw: t.matmul(
                                    b[:, 0:cw], qsl,
                                    ksb[:, kbase + c0:kbase + c0 + cw],
                                    start=True, stop=True),
                                waits=sw, inc=("cT", 1))
                            wsm = ("cT", plan.at("cT"))
                            sev = [wsm]
                            if last_exp[par_] is not None:
                                sev.append(last_exp[par_])
                            add("scalar", lambda s, b=bank, sb0=sb0, c0=c0, cw=cw,
                                srow=srow:
                                s.activation(srow[:, sb0 + c0: sb0 + c0 + cw],
                                             b[:, 0:cw], AF.Copy, bias=0.0,
                                             scale=1.0),
                                waits=sev, inc=("cS", 1))
                            ms[("p2sev", nsc)] = ("cS", plan.at("cS"))
                            scnt[0] += 1
                        nsc = scnt[0]
                        bank = (pA, pB)[nsc % 2]
                        sw = [wkv, wq, wmt]
                        if ("p2sev", nsc - 2) in ms:
                            prev = ms[("p2sev", nsc - 2)]
                            sw.extend(prev if isinstance(prev, list) else [prev])
                        add("tensor", lambda t, b=bank, qsl=qsl, kbase=kbase,
                            sb1=i * 128: t.matmul(
                                b[:, 0:128], qsl,
                                ksb[:, kbase + sb1:kbase + sb1 + 128],
                                start=True, stop=True),
                            waits=sw, inc=("cT", 1))
                        wsm = ("cT", plan.at("cT"))
                        sev = [wsm]
                        if last_exp[par_] is not None:
                            sev.append(last_exp[par_])
                        add("vector", lambda v, b=bank, sec=sec, sb0=sb0,
                            sb1=i * 128, mt=mt, srow=srow: v.scalar_tensor_tensor(
                                srow[:, sb0 + sb1: sb0 + sb1 + 128], b[:, 0:128],
                                1.0, mt[:, sec * 128:(sec + 1) * 128],
                                op0=ALU.mult, op1=ALU.add),
                            waits=sev, inc=("cV", 1))
                        ms[("p2sev", nsc)] = ("cV", plan.at("cV"))
                        scnt[0] += 1
                    rowlen = 2 * seclen
                    wall = [("cS", plan.at("cS")), ("cV", plan.at("cV"))]
                    if ("p2nmfree", col) in ms:
                        wall.append(ms[("p2nmfree", col)])
                    add("vector", lambda v, col=col, rowlen=rowlen, srow=srow:
                        v.reduce_max(
                            negmx[:, col:col + 1], srow[:, 0:rowlen],
                            axis=AX.X, negate=True),
                        waits=wall, inc=("cV", 1))
                    wmx = ("cV", plan.at("cV"))
                    wex = [wmx, ("cS", plan.at("cS"))]
                    if p4_free is not None:
                        wex.append(p4_free)
                    add("scalar", lambda s, h4=h4, rowlen=rowlen, srow=srow,
                        col=col: s.activation(
                            p4[:, h4 * 4096: h4 * 4096 + rowlen], srow[:, 0:rowlen],
                            AF.Exp, bias=negmx[:, col:col + 1], scale=1.0,
                            accum_out=sums4[:, col:col + 1]),
                        waits=wex, inc=("cS", 1))
                    last_exp[par_] = ("cS", plan.at("cS"))
                    ms[("p2nmfree", col)] = last_exp[par_]
                    add("vector", lambda v, col=col: v.reciprocal(
                        rs4[:, col:col + 1], sums4[:, col:col + 1]),
                        waits=[last_exp[par_]], inc=("cV", 1))
                    wrc = ("cV", plan.at("cV"))
                    add("vector", lambda v, h4=h4, rowlen=rowlen, col=col:
                        v.tensor_scalar(
                            p4[:, h4 * 4096: h4 * 4096 + rowlen],
                            p4[:, h4 * 4096: h4 * 4096 + rowlen],
                            rs4[:, col:col + 1], None, ALU.mult),
                        waits=[wrc], inc=("cV", 1))
                    ms[("p2pn", h4)] = ("cV", plan.at("cV"))
                    last_norm = ms[("p2pn", h4)]
                wpn = [ms[("p2pn", h4)] for h4 in
                       (range(4) if only_h4 is None else only_h4)]
                h4list = list(range(4) if only_h4 is None else only_h4)
                nh = len(h4list)
                nkt = 2 * (i + 1)
                for sw0 in range(0, nkt, 16):
                    kts = range(sw0, min(sw0 + 16, nkt))
                    for kt in kts:
                        ptbank = (pT[:], pR[:].bitcast(BF16))[kt % 2]
                        for hx, h4 in enumerate(h4list):
                            tw = []
                            if kt == 0 and hx == 0:
                                tw = list(wpn)
                            if kt >= 2 and hx == 0:
                                tw.append(ms[("p2pte", kt - 2)])
                            if kt >= 16 and hx == 0:
                                tw.append(ms[("p2pv", kt - 16)])
                            add("tensor", lambda t, kt=kt, h4=h4, hx=hx,
                                ptbank=ptbank: t.transpose(
                                    ptbank[:, hx * 128:(hx + 1) * 128],
                                    p4[:, h4 * 4096 + kt * 128:
                                       h4 * 4096 + (kt + 1) * 128],
                                    identb[:]),
                                waits=tw, inc=("cT", 1) if hx == nh - 1 else None)
                        wtr = ("cT", plan.at("cT"))
                        slot = kt % 16
                        add("scalar", lambda s, slot=slot, ptbank=ptbank:
                            s.activation(
                                ptb[:, slot * nh * 128:(slot + 1) * nh * 128],
                                ptbank[:, 0:nh * 128],
                                AF.Copy, bias=0.0, scale=1.0),
                            waits=[wtr], inc=("cS", 1))
                        ms[("p2pte", kt)] = ("cS", plan.at("cS"))
                    for kt in kts:
                        sec = 1 if kt >= (i + 1) else 0
                        lt = kt - (i + 1) * sec
                        vcol = (sec * NT + lt) * ER + kvh * 128
                        slot = kt % 16
                        pw = [ms[("p2pte", kt)]]
                        if hg >= 2 and kt == 0:
                            pw.append(ms[("p2cte", hg - 2)])
                        add("tensor", lambda t, otb=otb, vcol=vcol, slot=slot,
                            kt=kt, last=nkt - 1: t.matmul(
                                otb[:, 0:nh * 128], vsb[:, vcol:vcol + 128],
                                ptb[:, slot * nh * 128:(slot + 1) * nh * 128],
                                start=(kt == 0), stop=(kt == last)),
                            waits=pw,
                            inc=("cT", 1))
                        ms[("p2pv", kt)] = ("cT", plan.at("cT"))
                p4_free = ("cT", plan.at("cT"))
                wpv = p4_free
                cst = ctxst[hg % 2]
                wcs = [wpv]
                if hg >= 2:
                    wcs.append(ms[("p2cio", hg - 2)])
                add("scalar", lambda s, otb=otb, cst=cst: s.activation(
                    cst[:], otb[:], AF.Copy, bias=0.0, scale=1.0),
                    waits=wcs, inc=("cS", 1))
                ms[("p2cte", hg)] = ("cS", plan.at("cS"))
                add("sync", lambda s, hg=hg, i=i, cst=cst: s.dma_start(
                    out=d_ctx[hg * 4:(hg + 1) * 4, :, i * 128:(i + 1) * 128]
                    .rearrange("h p q -> p h q"),
                    in_=cst[:].rearrange("p (h q) -> p h q", q=128)),
                    waits=[ms[("p2cte", hg)]], inc=("dCtx", 16))
                ms[("p2cio", hg)] = ("dCtx", plan.at("dCtx"))
            ms[("p2q", i)] = ("cT", plan.at("cT"))
            ms[("p2m", i)] = ("cV", plan.at("cV"))
            if dump:
                rl = 2 * (i + 1) * 128
                wd = [("cS", plan.at("cS")), ("cV", plan.at("cV")),
                      ("cG", plan.at("cG")), ("cT", plan.at("cT"))]
                add("gpsimd", lambda g, rl=rl: g.dma_start(
                    out=d_dbg[:, 0:rl], in_=srow_t[0][:, 0:rl]), waits=wd,
                    inc=("dDbg", 16))
                add("gpsimd", lambda g, rl=rl: g.dma_start(
                    out=d_dbg[:, 4096:4096 + rl], in_=p4[:, 0:rl]), waits=wd,
                    inc=("dDbg", 16))
                _pw = 2 * (i + 1) * (1 if only_h4 else 4) * 128
                add("gpsimd", lambda g, _pw=_pw: g.dma_start(
                    out=d_dbg[:, 8192:8192 + _pw], in_=ptb[:, 0:_pw]),
                    waits=wd, inc=("dDbg", 16))
                add("gpsimd", lambda g: g.dma_start(
                    out=d_dbg[:, 12288:12289], in_=sums4[:, 0:1],
                    allow_slow_non_contiguous=True), waits=wd,
                    inc=("dDbg", 16))
                add("gpsimd", lambda g: g.dma_start(
                    out=d_dbg[:, 12292:12293], in_=negmx[:, 0:1],
                    allow_slow_non_contiguous=True), waits=wd,
                    inc=("dDbg", 16))
                add("gpsimd", lambda g, i=i: g.dma_start(
                    out=d_dbg[:, 12296:12552], in_=qt_t[i % 2][:, 0:256]),
                    waits=wd, inc=("dDbg", 16))
                add("gpsimd", lambda g, i=i: g.dma_start(
                    out=d_dbg[:, 12552:12808], in_=mt_t[i % 2][:]), waits=wd,
                    inc=("dDbg", 16))
        ms["ctx_stored"] = ("dCtx", plan.at("dCtx"))

    def phase3_group(g):
        gcol = g * GT
        prev_ev3 = ("cV", plan.at("cV")) if g > 0 else None
        for blk in range(NBLK):
            cb = ctxb_t[blk]
            waits = [ms["ctx_stored"]]
            if g > 0:
                waits.append(ms["ffn2_done_0"])
            add("sync", lambda s, blk=blk, cb=cb: s.dma_start(
                out=cb[:].rearrange("p (c q) -> p c q", q=512),
                in_=d_ctx[:, :, gcol + blk * 512: gcol + blk * 512 + 512]
                .rearrange("c p q -> p c q")),
                waits=waits, inc=(f"dCb{blk}", 16))
        wcb = [("dCb0", plan.at("dCb0")), ("dCb1", plan.at("dCb1"))]

        def dslab_load(o, wprev):
            sem = ("dSlA", "dSlB")[o % 2]
            src = d_dT[:, o * 128:(o + 1) * 128].rearrange("(c p) k -> p c k", p=128)
            add("gpsimd", lambda gp, o=o, src=src: gp.dma_start(
                out=wslab[o % 2][:].rearrange("p (c k) -> p c k", k=128),
                in_=src), waits=wprev, inc=(sem, 16))
            return (sem, plan.at(sem))

        dlw = {0: dslab_load(0, [] if g == 0 else [ms["ffn2_done_0"]]),
               1: dslab_load(1, [])}
        for oo in range(NCH):
            slab = wslab[oo % 2]
            for blk in range(NBLK):
                u = oo * NBLK + blk
                bank = (pA, pB)[u % 2]
                xf = xf_t[u % 4]
                xsem = f"dXf{u % 4}"
                xw = [ms[("d3", "res", u - 4)]] if u >= 4 else []
                add("sync", lambda s, oo=oo, blk=blk, xf=xf: s.dma_start(
                    out=xf[:],
                    in_=d_xm[oo, :, gcol + blk * 512: gcol + blk * 512 + 512]),
                    waits=xw, inc=(xsem, 16))
                wxf = (xsem, plan.at(xsem))
                mw = [dlw[oo]] + wcb
                if u >= 2:
                    mw.append(ms[("d3", "t1", u - 2)])
                elif prev_ev3 is not None:
                    mw.append(prev_ev3)
                if u == 0 and "stats_free" in ms:
                    mw.append(ms["stats_free"])
                cb = ctxb_t[blk]
                for c in range(NCH):
                    add("tensor", lambda t, slab=slab, c=c, cb=cb, b=bank: t.matmul(
                        b[:], slab[:, c * 128:(c + 1) * 128],
                        cb[:, c * 512:(c + 1) * 512],
                        start=(c == 0), stop=(c == NCH - 1)),
                        waits=mw if c == 0 else (),
                        inc=("cT", 1) if c == NCH - 1 else None)
                wmm = ("cT", plan.at("cT"))
                if blk == NBLK - 1:
                    ms[("d3", "slabdone", oo)] = wmm
                xm = xm_st[u % 2]
                wt1 = [wmm, wxf, ("dXa", plan.at("dXa"))]
                if u >= 2:
                    wt1.append(ms[("d3", "xmio", u - 2)])
                add("vector", lambda v, b=bank, xf=xf, xm=xm: v.tensor_add(
                    xm[:], b[:], xf[:]), waits=wt1, inc=("cV", 1))
                ms[("d3", "t1", u)] = ("cV", plan.at("cV"))
                wev = ms[("d3", "t1", u)]
                add("sync", lambda s, oo=oo, blk=blk, xm=xm: s.dma_start(
                    out=d_xa[oo, :, gcol + blk * 512: gcol + blk * 512 + 512],
                    in_=xm[:]), waits=[wev], inc=("dXa", 16))
                ms[("d3", "res", u)] = ("dXa", plan.at("dXa"))
                stats_ops(u, oo, blk, xm, first=(oo == 0), last=(oo == NCH - 1),
                          tag="l3", wsrc=wev)
                ms[("d3", "xmio", u)] = ("cV", plan.at("cV"))
            if oo + 2 < NCH:
                dlw[oo + 2] = dslab_load(oo + 2, [ms[("d3", "slabdone", oo)]])
        ms[("d3", "slabdone_last")] = ms[("d3", "slabdone", NCH - 1)]
        ms[f"dense_done_{g}"] = ("cT", plan.at("cT"))
        ms["xatt_stored"] = ("dXa", plan.at("dXa"))
        ln_finalize(half=True, tag=f"ln3_{g}")
        ms["stats_free"] = ms[f"rdy_ln3_{g}"]

        add("gpsimd", lambda gp: gp.dma_start(
            out=xbig[:].rearrange("p (i t) -> p i t", t=GT),
            in_=d_xa[:, :, gcol:gcol + GT].rearrange("i p t -> p i t")),
            waits=[ms[f"dense_done_{g}"], ms["xatt_stored"]], inc=("dXB", 16))
        wxb = ("dXB", plan.at("dXB"))
        xa_adapters(f"ln3_{g}", wxb)

        def fslab_load(o, wprev):
            sem = ("dSlA", "dSlB")[o % 2]
            src = d_wf[:, o * 128:(o + 1) * 128].rearrange("(i p) c -> p i c", p=128)
            add("gpsimd", lambda gp, o=o, src=src: gp.dma_start(
                out=wslab[o % 2][:].rearrange("p (i c) -> p i c", c=128),
                in_=src), waits=wprev, inc=(sem, 16))
            return (sem, plan.at(sem))

        flw = {0: fslab_load(0, [ms[("d3", "slabdone_last")]]),
               1: fslab_load(1, [])}
        for oo in range(NCH):
            slab = wslab[oo % 2]
            for blk in range(NBLK):
                u = oo * NBLK + blk
                bank = (pA, pB)[u % 2]
                xf = xf_t[u % 4]
                xsem = f"dXf{u % 4}"
                xw = [ms[("f2", "out", u - 4)]] if u >= 4 else [ms["xatt_stored"]]
                add("sync", lambda s, oo=oo, blk=blk, xf=xf: s.dma_start(
                    out=xf[:],
                    in_=d_xa[oo, :, gcol + blk * 512: gcol + blk * 512 + 512]),
                    waits=xw, inc=(xsem, 16))
                wxf = (xsem, plan.at(xsem))
                mw = [flw[oo], wxb, ms["xa_rdy"]]
                if u >= 2:
                    mw.append(ms[("f2", "t1", u - 2)])
                for i in range(NCH):
                    add("tensor", lambda t, slab=slab, i=i, blk=blk, b=bank: t.matmul(
                        b[:], slab[:, i * 128:(i + 1) * 128],
                        xbig[:, i * GT + blk * 512: i * GT + blk * 512 + 512],
                        start=(i == 0), stop=False), waits=mw if i == 0 else ())
                add("tensor", lambda t, oo=oo, blk=blk, b=bank: t.matmul(
                    b[:], w1f_row[0:1, oo * 128:(oo + 1) * 128],
                    negm_row[0:1, blk * 512: blk * 512 + 512],
                    start=False, stop=False))
                for j in range(2):
                    add("tensor", lambda t, oo=oo, blk=blk, j=j, b=bank: t.matmul(
                        b[:], bSb[:, j * D + oo * 128: j * D + (oo + 1) * 128],
                        xab[:, (blk * 2 + j) * 512:(blk * 2 + j + 1) * 512],
                        start=False, stop=(j == 1)),
                        inc=("cT", 1) if j == 1 else None)
                wmm = ("cT", plan.at("cT"))
                if blk == NBLK - 1:
                    ms[("f2", "slabdone", oo)] = wmm
                t1 = t1_t[u % 2]
                xm = xm_st[u % 2]
                wt1 = [wmm]
                if u >= 2:
                    wt1.append(ms[("f2", "out", u - 2)])
                add("vector", lambda v, b=bank, blk=blk, t1=t1: v.tensor_tensor(
                    t1[:], b[:], rb_sb[:, blk * 512: blk * 512 + 512],
                    op=ALU.mult), waits=wt1, inc=("cV", 1))
                ms[("f2", "t1", u)] = ("cV", plan.at("cV"))
                add("vector", lambda v, t1=t1, xm=xm, oo=oo, xf=xf:
                    v.scalar_tensor_tensor(xm[:], t1[:], bfh_sb[:, oo:oo + 1],
                                           xf[:], op0=ALU.add, op1=ALU.add),
                    waits=[wxf, ("dSt", plan.at("dSt"))], inc=("cV", 1))
                wev = ("cV", plan.at("cV"))
                add("sync", lambda s, oo=oo, blk=blk, xm=xm: s.dma_start(
                    out=d_out[oo * 128:(oo + 1) * 128,
                              gcol + blk * 512: gcol + blk * 512 + 512],
                    in_=xm[:]), waits=[wev], inc=("dSt", 16))
                ms[("f2", "out", u)] = ("dSt", plan.at("dSt"))
            if oo + 2 < NCH:
                flw[oo + 2] = fslab_load(oo + 2, [ms[("f2", "slabdone", oo)]])
        ms[f"ffn2_done_{g}"] = ("cT", plan.at("cT"))

    nb = [0]
    preloads()
    if _DEBUG == "pre":
        add("sync", lambda s: s.dma_start(
            out=d_out[0:128, 0:QCH], in_=bq_sb[:]),
            waits=[ms["a1_done"]], inc=("dSt", 16))
    elif _DEBUG.startswith("grp0"):
        stop = _DEBUG[5:] or None
        phase1_group(0, stop_after=stop)
        if stop in ("ffn", "ln2"):
            add("sync", lambda s: s.dma_start(
                out=d_out[:, 0:GT].rearrange("(c p) t -> c p t", p=128),
                in_=d_xm[:, :, 0:GT]),
                waits=[ms["xmid_stored"]], inc=("dSt", 16))
        elif stop in ("ln1", "xa"):
            add("sync", lambda s: s.dma_start(
                out=d_out[0:128, 0:GT], in_=rb_sb[:]),
                waits=[ms[f"rdy_ln1_0"]], inc=("dSt", 16))
        else:
            add("sync", lambda s: s.dma_start(
                out=d_out.rearrange("(c p) t -> c p t", p=128),
                in_=d_xm[:, :, :]),
                waits=[ms["xmid_stored"]] + ms["qkv_done_0"], inc=("dSt", 16))
    else:
        for g in range(NG):
            phase1_group(g)
    if _DEBUG == "pre" or _DEBUG.startswith("grp0"):
        pass
    elif _DEBUG == "xmid":
        add("sync", lambda s: s.dma_start(
            out=d_out.rearrange("(c p) t -> c p t", p=128), in_=d_xm[:, :, :]),
            waits=[ms["xmid_stored"]], inc=("dSt", 16))
    elif _DEBUG == "qkv":
        add("gpsimd", lambda g: g.dma_start(
            out=d_out.rearrange("(c p) t -> c p t", p=128), in_=d_q[:, :, :]),
            waits=ms["qkv_done_1"] + [ms["xmid_stored"]], inc=("dSt", 16))
    elif _DEBUG == "poll":
        flag_and_poll()
        nb[0] += 1
        barrier(nb[0])
        add("gpsimd", lambda g: g.dma_start(
            out=d_out[0:1, 0:16], in_=flagbuf[:]), inc=("dDbg", 16))
    elif _DEBUG == "kvload":
        flag_and_poll()
        nb[0] += 1
        barrier(nb[0])
        for sec in range(2):
            for kvh in range(NKV):
                add("sync", lambda s, sec=sec, kvh=kvh: s.dma_start(
                    out=ksb[:, (sec * NKV + kvh) * TOK:(sec * NKV + kvh + 1) * TOK],
                    in_=d_shk[sec, kvh, :, :]), inc=("dKV", 16))
        add("sync", lambda s: s.dma_start(
            out=vsb[:].rearrange("p (u c) -> p u c", c=ER),
            in_=d_shv.rearrange("h l p c -> p (h l) c")), inc=("dKV", 16))
        wkv = ("dKV", plan.at("dKV"))
        lvl = int(os.environ.get("MK_KV", "1"))
        if lvl >= 2:
            add("sync", lambda s: s.dma_start(
                out=qt_t[0][:].rearrange("p (c q) -> p c q", q=128),
                in_=d_q[:, :, 0:128].rearrange("c p q -> p c q")),
                inc=("dQt0", 16))
            add("sync", lambda s: s.dma_start(
                out=mt_t[0][:], in_=d_bm[:, 0:256]), inc=("dMt0", 16))
            wq = ("dQt0", plan.at("dQt0"))
            wmt = ("dMt0", plan.at("dMt0"))
        if lvl >= 3:
            for sec in range(2):
                bank = (pA, pB)[sec]
                add("tensor", lambda t, b=bank, sec=sec: t.matmul(
                    b[:, 0:128], qt_t[0][:, 0:128],
                    ksb[:, (sec * NKV) * TOK:(sec * NKV) * TOK + 128],
                    start=True, stop=True), waits=[wkv, wq, wmt], inc=("cT", 1))
                wsm = ("cT", plan.at("cT"))
                add("vector", lambda v, b=bank, sec=sec: v.scalar_tensor_tensor(
                    srow[:, sec * 128:(sec + 1) * 128], b[:, 0:128], 1.0,
                    mt_t[0][:, sec * 128:(sec + 1) * 128],
                    op0=ALU.mult, op1=ALU.add), waits=[wsm], inc=("cV", 1))
            wse = ("cV", plan.at("cV"))
            add("vector", lambda v: v.reduce_max(
                negmx[:, 0:1], srow[:, 0:256], axis=AX.X, negate=True),
                waits=[wse], inc=("cV", 1))
            wmx = ("cV", plan.at("cV"))
            add("scalar", lambda s: s.activation(
                p4[:, 0:256], srow[:, 0:256], AF.Exp, bias=negmx[:, 0:1],
                scale=1.0, accum_out=sums4[:, 0:1]), waits=[wmx], inc=("cS", 1))
            wex = ("cS", plan.at("cS"))
            add("vector", lambda v: v.reciprocal(rs4[:, 0:1], sums4[:, 0:1]),
                waits=[wex], inc=("cV", 1))
            wrc = ("cV", plan.at("cV"))
            add("gpsimd", lambda g: g.tensor_scalar(
                p4[:, 0:256], p4[:, 0:256], rs4[:, 0:1], None, ALU.mult),
                waits=[wrc], inc=("cG", 1))
            wpn = ("cG", plan.at("cG"))
        if lvl >= 4:
            for kt in range(2):
                for hx in range(1):
                    add("tensor", lambda t, kt=kt: t.transpose(
                        pT[:, (kt % 2) * 128:((kt % 2) + 1) * 128],
                        p4[:, kt * 128:(kt + 1) * 128], identb[:]),
                        waits=[wpn] if kt == 0 else (), inc=("cT", 1))
                wtr = ("cT", plan.at("cT"))
                add("scalar", lambda s, kt=kt: s.activation(
                    ptb[:, kt * 128:(kt + 1) * 128],
                    pT[:, (kt % 2) * 128:((kt % 2) + 1) * 128],
                    AF.Copy, bias=0.0, scale=1.0), waits=[wtr], inc=("cS", 1))
                wpte = ("cS", plan.at("cS"))
                add("tensor", lambda t, kt=kt, wv=wpte: t.matmul(
                    pC[:, 0:128], vsb[:, (kt * NT) * ER: (kt * NT) * ER + 128],
                    ptb[:, kt * 128:(kt + 1) * 128],
                    start=(kt == 0), stop=(kt == 1)),
                    waits=[wpte], inc=("cT", 1) if kt == 1 else None)
            wpv = ("cT", plan.at("cT"))
            add("scalar", lambda s: s.activation(
                ctxst[0][:, 0:128], pC[:, 0:128], AF.Copy, bias=0.0, scale=1.0),
                waits=[wpv], inc=("cS", 1))
            wcte = ("cS", plan.at("cS"))
            add("sync", lambda s: s.dma_start(
                out=d_ctx[0:1, :, 0:128].rearrange("h p q -> p h q"),
                in_=ctxst[0][:, 0:128].rearrange("p (h q) -> p h q", q=128)),
                waits=[wcte], inc=("dCtx", 16))
        add("gpsimd", lambda g: g.dma_start(
            out=d_out[0:128, 0:2048], in_=ksb[:, 0:2048]),
            waits=[wkv, ("dCtx", plan.at("dCtx")), ("cG", plan.at("cG")),
                   ("cS", plan.at("cS"))], inc=("dDbg", 16))
    elif _DEBUG == "att0n":
        flag_and_poll()
        nb[0] += 1
        barrier(nb[0])
        _tl = [int(v) for v in os.environ.get("MK_T", "0").split(",")]
        _hgs = [int(v) for v in os.environ.get("MK_HG", "0").split(",")]
        phase2(only_tiles=_tl, only_hg=_hgs, dump=False)
        add("gpsimd", lambda g: g.dma_start(
            out=d_out[0:512, 0:2048].rearrange("(c p) t -> c p t", p=128),
            in_=d_ctx[0:4, :, :]),
            waits=[ms["ctx_stored"]], inc=("dDbg", 16))
    elif _DEBUG == "att0":
        flag_and_poll()
        nb[0] += 1
        barrier(nb[0])
        import os as _os
        _tl = [int(v) for v in _os.environ.get("MK_T", "0,1").split(",")]
        _h4 = _os.environ.get("MK_H4", "")
        phase2(only_tiles=_tl, only_hg=[0], dump=True,
               only_h4=[int(v) for v in _h4.split(",")] if _h4 else None)
        for a in range(8):
            add("gpsimd", lambda g, a=a: g.dma_start(
                out=d_out[a * 128:(a + 1) * 128, 0:2048],
                in_=d_dbg[:, a * 2048:(a + 1) * 2048]),
                waits=[("dDbg", plan.at("dDbg")), ms["ctx_stored"]],
                inc=("dDbg", 16))
    else:
        flag_and_poll()
        nb[0] += 1
        barrier(nb[0])
        phase2()
        if _DEBUG == "ctx":
            add("gpsimd", lambda g: g.dma_start(
                out=d_out.rearrange("(c p) t -> c p t", p=128), in_=d_ctx[:, :, :]),
                waits=[ms["ctx_stored"]], inc=("dSt", 16))
        else:
            nb[0] += 1
            barrier(nb[0])
            for g in range(NG):
                phase3_group(g)
    fin = plan.at("dSt")
    find = plan.at("dDbg")
    add("sync", lambda s: (s.wait_ge(sems["dDbg"], find) if find else None,
                           s.wait_ge(sems["dSt"], fin))[-1])

    for name in ("dCon", "dCn2", "dXB", "dTab", "dSlA", "dSlB",
                 "dXf0", "dXf1", "dXf2", "dXf3", "dXm", "dQ", "dKw", "dV",
                 "dF", "dKV", "dQt0", "dQt1", "dMt0", "dMt1", "dCtx",
                 "dCb0", "dCb1", "dXa", "dSt", "dDbg", "pol", "bar",
                 "cT", "cV", "cS", "cG"):
        sems[name] = cstk.enter_context(nc.semaphore(name))

    per_eng = {e: [] for e in ENGINES}
    for eng, fn, waits, incs in plan.seq:
        per_eng[eng].append((fn, waits, incs))

    with nc.Block() as block:
        def mk(ops):
            def body(eng):
                for fn, waits, incs in ops:
                    for s, v in waits:
                        eng.wait_ge(sems[s], v)
                    ins = fn(eng)
                    for s, n in incs:
                        tgt = ins if ins is not None else eng.nop()
                        tgt.then_inc(sems[s], n)
            return body
        block.sync(mk(per_eng["sync"]))
        block.gpsimd(mk(per_eng["gpsimd"]))
        block.tensor(mk(per_eng["tensor"]))
        block.vector(mk(per_eng["vector"]))
        block.scalar(mk(per_eng["scalar"]))

    return nc


def kernel(**inputs):
    in_maps = _host_prep(inputs)
    nc = build_program()
    res = run_bass_kernel_spmd(nc, in_maps, core_ids=list(range(8)))
    out = np.empty((NBATCH, S, D), np.float32)
    for c in range(8):
        b, par = c // 2, c % 2
        gl = [_g_of(par, i) for i in range(NT)]
        rows = np.concatenate([np.arange(g * 128, (g + 1) * 128) for g in gl])
        out[b, rows, :] = res.results[c]["outT"].T
    return out


# version 08 (final)
# speedup vs baseline: 1.0755x; 1.0755x over previous
import dataclasses
import math
import os
from collections import defaultdict
from contextlib import ExitStack

import numpy as np

import concourse.bass as bass
import concourse.mybir as mybir
from concourse.bass_utils import run_bass_kernel_spmd
from concourse.expressions_rust import RuntimeValue

F32 = mybir.dt.float32
BF16 = mybir.dt.bfloat16
I32 = mybir.dt.int32
U8 = mybir.dt.uint8
AF = mybir.ActivationFunctionType
ALU = mybir.AluOpType
AX = mybir.AxisListType

D = 4096
S = 4096
NBATCH = 4
HD = 128
NH = 32
NKV = 2
QKV_D = 4608
NT = 16
TOK = 2048
NG = 2
GT = 1024
NBLK = 2
NCH = 32
QCH = 36
ER = 256
EPS = 1e-5
NEG = -1e30
ROT = 64

_DEBUG = os.environ.get("MK_DEBUG", "")
ENGINES = ("sync", "gpsimd", "tensor", "vector", "scalar")


def _g_of(par, i):
    return 2 * i + ((i & 1) if par == 0 else 1 - (i & 1))


def _perm_head():
    return np.array(list(range(0, ROT, 2)) + list(range(1, ROT, 2)) +
                    list(range(ROT, HD)), dtype=np.int64)


def _host_prep(inputs):
    x = np.asarray(inputs["L0_attn_hidden_states"], np.float32)
    qkv_w = np.asarray(inputs["L0_attn_qkv_w"], np.float32)
    qkv_b = np.asarray(inputs["L0_attn_qkv_b"], np.float32)
    dense_w = np.asarray(inputs["L0_attn_dense_w"], np.float32)
    W = np.asarray(inputs["L0_ffn_W"], np.float32)
    bias = np.asarray(inputs["L0_ffn_bias"], np.float32)
    A = np.asarray(inputs["L0_ffn_A"], np.float32)
    Bf = np.asarray(inputs["L0_ffn_Bf"], np.float32)
    coeffs = np.asarray(inputs["L0_ffn_coeffs"], np.float32)
    scales = np.asarray(inputs["L0_ffn_scales"], np.float32)

    ph = _perm_head()
    perm_rows = np.arange(QKV_D)
    for h in range(NH):
        perm_rows[h * HD:(h + 1) * HD] = h * HD + ph
    for kv in range(NKV):
        base = D + kv * HD
        perm_rows[base:base + HD] = base + ph
    qkv_w_p = qkv_w[perm_rows]
    qkv_b_p = qkv_b[perm_rows]

    Pm = np.zeros((128, 128), np.float32)
    for m in range(32):
        Pm[m + 32, m] = -1.0
        Pm[m, m + 32] = 1.0

    shared = {
        "wT_qkv": np.ascontiguousarray(qkv_w_p.T),
        "b_qkv": np.ascontiguousarray(qkv_b_p.reshape(QCH, 128).T),
        "wT_ffn": np.ascontiguousarray(W.T),
        "b_ffn": np.ascontiguousarray(bias.reshape(NCH, 128).T),
        "dT": np.ascontiguousarray(dense_w.T),
        "aT": np.ascontiguousarray(A.transpose(2, 0, 1).reshape(D, ER)),
        "bS": np.ascontiguousarray(Bf.transpose(0, 2, 1).reshape(ER, D)),
        "coefr": np.ascontiguousarray(np.repeat(coeffs, 16).reshape(2, 128).T),
        "scalr": np.ascontiguousarray(np.repeat(scales, 16).reshape(2, 128).T),
        "ident": np.eye(128, dtype=np.float32),
        "permm": Pm,
        "ones_col": np.ones((128, 1), np.float32),
        "ones_row": np.ones((1, 128), np.float32),
    }

    inv = 10000.0 ** (-(np.arange(32, dtype=np.float64) / 32.0))
    c_att = 1.0 / math.sqrt(HD)
    rng = np.random.default_rng(int.from_bytes(os.urandom(8), "little"))
    nonce1 = int(rng.integers(1, (1 << 30) - 1))
    nonce2 = int(rng.integers(1, (1 << 30) - 1))

    in_maps = []
    for c in range(8):
        b, par = c // 2, c % 2
        gl = [_g_of(par, i) for i in range(NT)]
        rows = np.concatenate([np.arange(g * 128, (g + 1) * 128) for g in gl])
        m = dict(shared)
        m["xT"] = np.ascontiguousarray(x[b][rows].T)

        pos = rows.astype(np.float64)
        th = pos[None, :] * inv[:, None]
        cos = np.cos(th).astype(np.float32)
        sin = np.sin(th).astype(np.float32)
        ck = np.empty((128, TOK), np.float32)
        sk = np.empty((128, TOK), np.float32)
        ck[0:32] = cos; ck[32:64] = cos; ck[64:128] = 1.0
        sk[0:32] = sin; sk[32:64] = sin; sk[64:128] = 0.0
        m["ck"], m["sk"] = ck, sk
        m["cq"] = (ck * c_att).astype(np.float32)
        m["sq"] = (sk * c_att).astype(np.float32)

        bm = np.zeros((128, NT * 256), np.float32)
        for i in range(NT):
            g = gl[i]
            for sec in range(2):
                gk = _g_of(sec, i)
                blkm = np.zeros((128, 128), np.float32)
                if gk > g:
                    blkm[:] = NEG
                elif gk == g:
                    blkm[np.triu_indices(128, 1)] = NEG
                bm[:, i * 256 + sec * 128: i * 256 + (sec + 1) * 128] = blkm
        m["bmask"] = bm

        kw = par * (NKV * 128 * TOK)
        vw = par * (NT * 128 * ER)
        m["offs"] = np.array(
            [[kw, vw, par * 16, (1 - par) * 16, nonce1, nonce2, 0, 0]], np.int32)
        in_maps.append(m)
    return in_maps


class Plan:
    def __init__(self):
        self.seq = []
        self.cnt = defaultdict(int)

    def add(self, eng, fn, waits=(), inc=None):
        incs = []
        if inc is not None:
            if isinstance(inc, tuple):
                inc = [inc]
            for s, n in inc:
                self.cnt[s] += n
                incs.append((s, n))
        for s, v in waits:
            assert v <= self.cnt[s], f"wait on future: {s} {v} > {self.cnt[s]}"
        self.seq.append((eng, fn, [(s, v) for s, v in waits if v > 0], incs))

    def at(self, s):
        return self.cnt[s]


def _chunks(total, step):
    out, c0 = [], 0
    while c0 < total:
        out.append((c0, min(step, total - c0)))
        c0 += step
    return out


def build_program():
    nc = bass.Bass("TRN2", target_bir_lowering=False)

    d_x = nc.dram_tensor("xT", [D, TOK], F32, kind="ExternalInput").ap()
    d_bm = nc.dram_tensor("bmask", [128, NT * 256], F32, kind="ExternalInput").ap()
    d_cq = nc.dram_tensor("cq", [128, TOK], F32, kind="ExternalInput").ap()
    d_sq = nc.dram_tensor("sq", [128, TOK], F32, kind="ExternalInput").ap()
    d_ck = nc.dram_tensor("ck", [128, TOK], F32, kind="ExternalInput").ap()
    d_sk = nc.dram_tensor("sk", [128, TOK], F32, kind="ExternalInput").ap()
    d_offs = nc.dram_tensor("offs", [1, 8], I32, kind="ExternalInput").ap()
    d_wq = nc.dram_tensor("wT_qkv", [D, QKV_D], F32, kind="ExternalInput").ap()
    d_bq = nc.dram_tensor("b_qkv", [128, QCH], F32, kind="ExternalInput").ap()
    d_wf = nc.dram_tensor("wT_ffn", [D, D], F32, kind="ExternalInput").ap()
    d_bfv = nc.dram_tensor("b_ffn", [128, NCH], F32, kind="ExternalInput").ap()
    d_dT = nc.dram_tensor("dT", [D, D], F32, kind="ExternalInput").ap()
    d_aT = nc.dram_tensor("aT", [D, ER], F32, kind="ExternalInput").ap()
    d_bS = nc.dram_tensor("bS", [ER, D], F32, kind="ExternalInput").ap()
    d_coef = nc.dram_tensor("coefr", [128, 2], F32, kind="ExternalInput").ap()
    d_scal = nc.dram_tensor("scalr", [128, 2], F32, kind="ExternalInput").ap()
    d_id = nc.dram_tensor("ident", [128, 128], F32, kind="ExternalInput").ap()
    d_pm = nc.dram_tensor("permm", [128, 128], F32, kind="ExternalInput").ap()
    d_oc = nc.dram_tensor("ones_col", [128, 1], F32, kind="ExternalInput").ap()
    d_or = nc.dram_tensor("ones_row", [1, 128], F32, kind="ExternalInput").ap()
    d_out = nc.dram_tensor("outT", [D, TOK], F32, kind="ExternalOutput").ap()

    d_q = nc.dram_tensor("q_hbm", [NCH, 128, TOK], BF16, kind="Internal").ap()
    d_xm = nc.dram_tensor("xmid", [NCH, 128, TOK], F32, kind="Internal").ap()
    d_xa = nc.dram_tensor("xatt", [NCH, 128, TOK], F32, kind="Internal").ap()
    d_ctx = nc.dram_tensor("ctx", [NCH, 128, TOK], BF16, kind="Internal").ap()
    d_shk = nc.dram_tensor("sh_k", [2, NKV, 128, TOK], BF16, kind="Internal",
                           addr_space="Shared").ap()
    d_shv = nc.dram_tensor("sh_v", [2, NT, 128, ER], BF16, kind="Internal",
                           addr_space="Shared").ap()
    d_shf = nc.dram_tensor("sh_flag", [2, 16], I32, kind="Internal",
                           addr_space="Shared").ap()
    d_dbg = nc.dram_tensor("dbg", [128, 16384], F32, kind="Internal").ap()

    cstk = ExitStack()

    def sbt(name, shape, dt=F32):
        return cstk.enter_context(nc.sbuf_tensor(name, shape, dt))

    def pst(name, shape, dt=F32):
        return cstk.enter_context(nc.psum_tensor(name, shape, dt))

    identb = sbt("identb", [128, 128], BF16)
    permb = sbt("permb", [128, 128], BF16)
    onescol = sbt("onescol", [128, 1], BF16)
    onesrow = sbt("onesrow", [1, 128], BF16)
    bq_sb = sbt("bq_sb", [128, QCH])
    bfh_sb = sbt("bfh_sb", [128, NCH])
    wvec = sbt("wvec", [128, 2])
    scal_tmp = sbt("scal_tmp", [128, 2])
    offs_sb = sbt("offs_sb", [1, 8], I32)
    aSb = sbt("aSb", [128, NCH * ER], BF16)
    bSb = sbt("bSb", [128, 2 * D], BF16)
    a1row = sbt("a1row", [1, ER], BF16)
    w1f_row = sbt("w1f_row", [1, D], BF16)
    w1q_row = sbt("w1q_row", [1, QKV_D], BF16)
    negm_row = sbt("negm_row", [1, GT], BF16)
    rhi_row = sbt("rhi_row", [1, GT], BF16)
    rlo_row = sbt("rlo_row", [1, GT], BF16)
    mrow = sbt("mrow", [1, GT])
    vrow = sbt("vrow", [1, GT])
    vtmp = sbt("vtmp", [1, GT])
    rb_sb = sbt("rb_sb", [128, GT])
    flagbuf = sbt("flagbuf", [1, 16], I32)

    pA = pst("pA", [128, 512])
    pB = pst("pB", [128, 512])
    pC = pst("pC", [128, 512])
    pD = pst("pD", [128, 512])
    pE = pst("pE", [128, 512])
    pF = pst("pF", [128, 512])
    pR = pst("pR", [128, 512])
    pT = pst("pT", [128, 1024], BF16)

    ARENA = 135488
    arena = nc.alloc_sbuf_tensor("arena", [128, ARENA], U8)
    abase = nc.lookup_mloc(arena).addr

    def at(name, off, shape, dt=F32):
        assert off % 64 == 0
        n = mybir.dt.size(dt)
        for s in shape[1:]:
            n *= s
        assert off + n <= ARENA, f"{name}: arena overflow {off}+{n}"
        return nc.alloc_sbuf_tensor_at(name, shape, dt, offset=abase + off)

    xbig = at("xbig", 0, [128, NCH * GT], BF16)
    tabs = at("tabs", 65536, [128, 8 * 512])
    wslab = [at(f"wslab{j}", 81920 + j * 8192, [128, 4096], BF16) for j in range(2)]
    xf_t = [at(f"xf{j}", 98304 + j * 2048, [128, 512]) for j in range(4)]
    xm_st = [at(f"xmst{j}", 106496 + j * 2048, [128, 512]) for j in range(2)]
    t1_t = [at(f"t1{j}", 110592 + j * 2048, [128, 512]) for j in range(2)]
    qb_t = [at(f"qb{j}", 114688 + j * 1024, [128, 512], BF16) for j in range(2)]
    rq_t = [at(f"rq{j}", 116736 + j * 1024, [128, 512], BF16) for j in range(2)]
    xab = at("xab", 118784, [128, 4 * 512], BF16)
    sq_t = [at(f"sq{j}", 122880 + j * 1024, [128, 512], BF16) for j in range(2)]
    vst_t = [at(f"vst{j}", 124928 + j * 2048, [128, 4 * 256], BF16)
             for j in range(2)]
    ksb = at("ksb", 0, [128, 2 * NKV * TOK], BF16)
    vsb = at("vsb", 16384, [128, 2 * NT * ER], BF16)
    qt_t = [at(f"qt{j}", 32768 + j * 8192, [128, NCH * 128], BF16) for j in range(2)]
    srow_t = [at(f"srow{j}", 49152 + j * 16384, [128, 4096])
              for j in range(2)]
    p4 = at("p4", 81920, [128, 4 * 4096], BF16)
    ptb = at("ptb", 114688, [128, 16 * 512], BF16)
    mt_t = [at(f"mt{j}", 131072 + j * 1024, [128, 256]) for j in range(2)]
    ctxst = [at(f"ctxst{j}", 133120 + j * 1024, [128, 512], BF16) for j in range(2)]
    sums4 = at("sums4", 135168, [128, 8])
    rs4 = at("rs4", 135232, [128, 8])
    negmx = at("negmx", 135296, [128, 8])
    cmax = at("cmax", 135360, [128, 24])
    ctxb_t = [at(f"ctxb{j}", j * 32768, [128, NCH * 512], BF16) for j in range(2)]

    plan = Plan()
    ms = {}
    sems = {}
    _rt_regs = {}

    def add(eng, fn, waits=(), inc=None):
        plan.add(eng, fn, waits, inc)

    def _rt(eng, key):
        idx = {"kw": 0, "vw": 1, "fw": 2, "fr": 3, "n1": 4, "n2": 5}[key]
        name = f"rt_{key}"
        if name not in _rt_regs:
            r = eng.alloc_register(name)
            eng.reg_load(r, offs_sb[0:1, idx:idx + 1])
            _rt_regs[name] = RuntimeValue(val=r, min_val=0, max_val=1 << 30)
        return _rt_regs[name]

    def barrier(k):
        for e in ENGINES:
            add(e, lambda eng: eng.nop(), inc=("bar", 1))
        tgt = 5 * k
        assert plan.at("bar") == tgt
        for e in ENGINES:
            add(e, lambda eng, t=tgt: None, waits=[("bar", tgt)])

    def preloads():
        for t, dsrc in ((identb, d_id), (permb, d_pm), (onescol, d_oc),
                        (onesrow, d_or)):
            add("gpsimd", lambda g, t=t, d_=dsrc: g.dma_start(out=t[:], in_=d_),
                inc=("dCon", 16))
        add("gpsimd", lambda g: g.dma_start(
            out=aSb[:].rearrange("p (i e) -> p i e", e=ER),
            in_=d_aT.rearrange("(i p) e -> p i e", p=128)), inc=("dCon", 16))
        add("gpsimd", lambda g: g.dma_start(
            out=bSb[:].rearrange("p (j o) -> p j o", o=D),
            in_=d_bS.rearrange("(j p) o -> p j o", p=128)), inc=("dCon", 16))
        wconst = ("dCon", plan.at("dCon"))
        for t, dsrc in ((bq_sb, d_bq), (bfh_sb, d_bfv), (wvec, d_coef),
                        (offs_sb, d_offs), (scal_tmp, d_scal)):
            add("sync", lambda s, t=t, d_=dsrc: s.dma_start(out=t[:], in_=d_),
                inc=("dCn2", 16))
        wx = ("dCn2", plan.at("dCn2"))
        add("sync", lambda s: [_rt(s, k) for k in
                               ("kw", "vw", "fw", "fr", "n1", "n2")] and s.nop(),
            waits=[wx])
        add("vector", lambda v: v.tensor_mul(wvec[:], wvec[:], scal_tmp[:]),
            waits=[wx], inc=("cV", 1))
        add("vector", lambda v: v.tensor_scalar_mul(bfh_sb[:], bfh_sb[:], 0.5),
            inc=("cV", 1))
        wv = ("cV", plan.at("cV"))
        for j in range(2):
            add("vector", lambda v, j=j: v.tensor_scalar(
                bSb[:, j * D:(j + 1) * D], bSb[:, j * D:(j + 1) * D],
                wvec[:, j:j + 1], None, ALU.mult),
                waits=[wv, wconst], inc=("cV", 1))
        for i in range(NCH):
            add("tensor", lambda t, i=i: t.matmul(
                pR[0:1, 0:ER], onescol[:], aSb[:, i * ER:(i + 1) * ER],
                start=(i == 0), stop=(i == NCH - 1)),
                waits=[wconst] if i == 0 else (),
                inc=("cT", 1) if i == NCH - 1 else None)
        wt = ("cT", plan.at("cT"))
        add("scalar", lambda s: s.activation(a1row[:], pR[0:1, 0:ER], AF.Copy,
                                             bias=0.0, scale=1.0),
            waits=[wt], inc=("cS", 1))
        ms["a1_done"] = ("cS", plan.at("cS"))

    def ln_finalize(half, tag):
        wt = ("cT", plan.at("cT"))
        for blk in range(NBLK):
            sl = slice(blk * 512, (blk + 1) * 512)
            sbank, qbank = (pC, pD)[blk], (pE, pF)[blk]
            add("vector", lambda v, sl=sl, b=sbank: v.tensor_scalar_mul(
                mrow[0:1, sl], b[0:1, 0:512], 1.0 / D), waits=[wt], inc=("cV", 1))
            add("vector", lambda v, sl=sl, b=qbank: v.tensor_scalar_mul(
                vrow[0:1, sl], b[0:1, 0:512], 1.0 / D), waits=[wt], inc=("cV", 1))
        add("vector", lambda v: v.tensor_mul(vtmp[0:1, :], mrow[0:1, :],
                                             mrow[0:1, :]), inc=("cV", 1))
        add("vector", lambda v: v.tensor_sub(vrow[0:1, :], vrow[0:1, :],
                                             vtmp[0:1, :]), inc=("cV", 1))
        add("vector", lambda v: v.tensor_scalar_mul(negm_row[0:1, :],
                                                    mrow[0:1, :], -1.0),
            inc=("cV", 1))
        add("vector", lambda v: v.tensor_scalar_add(vrow[0:1, :], vrow[0:1, :],
                                                    EPS), inc=("cV", 1))
        wv = ("cV", plan.at("cV"))
        add("scalar", lambda s: s.activation(vtmp[0:1, :], vrow[0:1, :], AF.Ln,
                                             bias=0.0, scale=1.0),
            waits=[wv], inc=("cS", 1))
        add("scalar", lambda s: s.activation(
            vrow[0:1, :], vtmp[0:1, :], AF.Exp, bias=0.0, scale=-0.5),
            inc=("cS", 1))
        wsc = ("cS", plan.at("cS"))
        if half:
            add("vector", lambda v: v.tensor_scalar_mul(vrow[0:1, :],
                                                        vrow[0:1, :], 0.5),
                waits=[wsc], inc=("cV", 1))
            wsc = ("cV", plan.at("cV"))
        add("vector", lambda v: v.tensor_copy(rhi_row[0:1, :], vrow[0:1, :]),
            waits=[wsc], inc=("cV", 1))
        add("vector", lambda v: v.tensor_sub(rlo_row[0:1, :], vrow[0:1, :],
                                             rhi_row[0:1, :]), inc=("cV", 1))
        wv = ("cV", plan.at("cV"))
        for blk in range(NBLK):
            sl = slice(blk * 512, (blk + 1) * 512)
            add("tensor", lambda t, sl=sl: t.matmul(
                pE[:], onesrow[:], rhi_row[0:1, sl], start=True, stop=False),
                waits=[wv, ("cS", plan.at("cS"))])
            add("tensor", lambda t, sl=sl: t.matmul(
                pE[:], onesrow[:], rlo_row[0:1, sl], start=False, stop=True),
                inc=("cT", 1))
            wt2 = ("cT", plan.at("cT"))
            add("scalar", lambda s, sl=sl: s.activation(
                rb_sb[:, sl], pE[:], AF.Copy, bias=0.0, scale=1.0),
                waits=[wt2, wv], inc=("cS", 1))
        ms[f"rdy_{tag}"] = ("cS", plan.at("cS"))

    stats_pend = []

    def stats_flush(keep=0):
        while len(stats_pend) > keep:
            tagu, ent = stats_pend.pop(0)
            for eng, fn, waits, inc in ent:
                add(eng, fn, waits=waits, inc=inc)
            ms[tagu] = ("cT", plan.at("cT"))

    def stats_ops(u, oo, blk, src_f32, first, last, tag, wsrc):
        stats_flush(keep=1)
        cb = qb_t[u % 2]
        wprev = [wsrc] + ([ms[(tag, "stq", u - 2)]] if u >= 2 else [])
        add("vector", lambda v, xm=src_f32, cb=cb: v.tensor_copy(cb[:], xm[:]),
            waits=wprev, inc=("cV", 1))
        wcb = ("cV", plan.at("cV"))
        sqt = sq_t[u % 2]
        add("scalar", lambda s, cb=cb, sqt=sqt: s.activation(
            sqt[:], cb[:], AF.Square, bias=0.0, scale=1.0),
            waits=[wcb], inc=("cS", 1))
        wsq = ("cS", plan.at("cS"))
        sbank, qbank = (pC, pD)[blk], (pE, pF)[blk]
        sw = [wcb]
        if first and ("stats_free" in ms):
            sw.append(ms["stats_free"])
        ent = [
            ("tensor", (lambda t, cb=cb, b=sbank, first=first, last=last:
                        t.matmul(b[0:1, 0:512], onescol[:], cb[:],
                                 start=first, stop=last)), sw, None),
            ("tensor", (lambda t, sqt=sqt, b=qbank, first=first, last=last:
                        t.matmul(b[0:1, 0:512], onescol[:], sqt[:],
                                 start=first, stop=last)), [wsq], ("cT", 1)),
        ]
        stats_pend.append(((tag, "stq", u), ent))

    def xa_adapters(tag, wsrc):
        for blk in range(NBLK):
            for j in range(2):
                bank = (pC, pD)[j]
                for i in range(NCH):
                    w0 = []
                    if i == 0:
                        w0 = [ms[f"rdy_{tag}"], wsrc]
                        if blk == 0 and j == 0 and "stats_free" in ms:
                            w0.append(ms["stats_free"])
                    add("tensor", lambda t, b=bank, i=i, j=j, blk=blk: t.matmul(
                        b[:], aSb[:, i * ER + j * 128: i * ER + (j + 1) * 128],
                        xbig[:, i * GT + blk * 512: i * GT + blk * 512 + 512],
                        start=(i == 0), stop=False), waits=w0)
                add("tensor", lambda t, b=bank, j=j, blk=blk: t.matmul(
                    b[:], a1row[0:1, j * 128:(j + 1) * 128],
                    negm_row[0:1, blk * 512: blk * 512 + 512],
                    start=False, stop=True), inc=("cT", 1))
                wt = ("cT", plan.at("cT"))
                add("scalar", lambda s, b=bank, j=j, blk=blk: s.activation(
                    xab[:, (blk * 2 + j) * 512:(blk * 2 + j + 1) * 512], b[:],
                    AF.Copy, bias=0.0, scale=1.0), waits=[wt], inc=("cS", 1))
        ms["xa_rdy"] = ("cS", plan.at("cS"))
        ms["stats_free"] = ms["xa_rdy"]

    def phase1_group(g, stop_after=None):
        gcol = g * GT
        waits = [ms["xbig_free"]] if g > 0 else []
        prev_ev = ("cV", plan.at("cV")) if g > 0 else None
        add("gpsimd", lambda gp: gp.dma_start(
            out=xbig[:].rearrange("p (i t) -> p i t", t=GT),
            in_=d_x[:, gcol:gcol + GT].rearrange("(i p) t -> p i t", p=128)),
            waits=waits, inc=("dXB", 16))
        wxb = ("dXB", plan.at("dXB"))
        tw = [ms["tabs_free"]] if g > 0 else []
        for j, dsrc in enumerate((d_cq, d_sq, d_ck, d_sk)):
            add("sync", lambda s, j=j, d_=dsrc: s.dma_start(
                out=tabs[:, j * 1024:(j + 1) * 1024], in_=d_[:, gcol:gcol + GT]),
                waits=tw, inc=("dTab", 16))
        wtab = ("dTab", plan.at("dTab"))

        for ci in range(NCH):
            for blk in range(NBLK):
                u = ci * NBLK + blk
                xsl = xbig[:, ci * GT + blk * 512: ci * GT + blk * 512 + 512]
                sqt = sq_t[u % 2]
                swq = [wxb] if u < 2 else [ms[("l1", "stq", u - 2)]]
                add("scalar", lambda s, xsl=xsl, sqt=sqt: s.activation(
                    sqt[:], xsl, AF.Square, bias=0.0, scale=1.0),
                    waits=swq, inc=("cS", 1))
                wsq = ("cS", plan.at("cS"))
                sbank, qbank = (pC, pD)[blk], (pE, pF)[blk]
                sw = [wxb]
                if ci == 0 and "stats_free" in ms:
                    sw.append(ms["stats_free"])
                add("tensor", lambda t, xsl=xsl, b=sbank, ci=ci: t.matmul(
                    b[0:1, 0:512], onescol[:], xsl,
                    start=(ci == 0), stop=(ci == NCH - 1)),
                    waits=sw if ci == 0 else ())
                add("tensor", lambda t, sqt=sqt, b=qbank, ci=ci: t.matmul(
                    b[0:1, 0:512], onescol[:], sqt[:],
                    start=(ci == 0), stop=(ci == NCH - 1)),
                    waits=[wsq], inc=("cT", 1))
                ms[("l1", "stq", u)] = ("cT", plan.at("cT"))
        ln_finalize(half=True, tag=f"ln1_{g}")
        ms["stats_free"] = ms[f"rdy_ln1_{g}"]
        if stop_after == "ln1":
            return
        xa_adapters(f"ln1_{g}", wxb)
        if stop_after == "xa":
            return

        def slab_load(mat, o, wprev):
            sem = ("dSlA", "dSlB")[o % 2]
            src = mat[:, o * 128:(o + 1) * 128].rearrange("(i p) c -> p i c", p=128)
            add("gpsimd", lambda gp, o=o, src=src: gp.dma_start(
                out=wslab[o % 2][:].rearrange("p (i c) -> p i c", c=128),
                in_=src), waits=wprev, inc=(sem, 16))
            return (sem, plan.at(sem))

        slw = {0: slab_load(d_wf, 0, []), 1: slab_load(d_wf, 1, [])}
        for oo in range(NCH):
            slab = wslab[oo % 2]
            if g == 0:
                for i in range(NCH):
                    w0 = [slw[oo]]
                    if oo >= 1:
                        w0.append(ms["w1ev"])
                    else:
                        w0.append(ms["a1_done"])
                    add("tensor", lambda t, slab=slab, i=i: t.matmul(
                        pR[0:1, 0:128], onescol[:], slab[:, i * 128:(i + 1) * 128],
                        start=(i == 0), stop=(i == NCH - 1)),
                        waits=w0 if i == 0 else (),
                        inc=("cT", 1) if i == NCH - 1 else None)
                wt = ("cT", plan.at("cT"))
                add("scalar", lambda s, oo=oo: s.activation(
                    w1f_row[0:1, oo * 128:(oo + 1) * 128], pR[0:1, 0:128],
                    AF.Copy, bias=0.0, scale=1.0), waits=[wt], inc=("cS", 1))
                ms["w1ev"] = ("cS", plan.at("cS"))
            ww1 = ms["w1ev"] if g == 0 else None
            for blk in range(NBLK):
                u = oo * NBLK + blk
                bank = (pA, pB)[u % 2]
                xf = xf_t[u % 4]
                xsem = f"dXf{u % 4}"
                xw = [ms[("p1", "res", u - 4)]] if u >= 4 else []
                add("sync", lambda s, oo=oo, blk=blk, xf=xf: s.dma_start(
                    out=xf[:], in_=d_x[oo * 128:(oo + 1) * 128,
                                       gcol + blk * 512: gcol + blk * 512 + 512]),
                    waits=xw, inc=(xsem, 16))
                wxf = (xsem, plan.at(xsem))
                mw = [slw[oo], ms["xa_rdy"]]
                if u >= 2:
                    mw.append(ms[("p1", "t1", u - 2)])
                elif prev_ev is not None:
                    mw.append(prev_ev)
                for i in range(NCH):
                    add("tensor", lambda t, slab=slab, i=i, blk=blk, b=bank: t.matmul(
                        b[:], slab[:, i * 128:(i + 1) * 128],
                        xbig[:, i * GT + blk * 512: i * GT + blk * 512 + 512],
                        start=(i == 0), stop=False), waits=mw if i == 0 else ())
                add("tensor", lambda t, oo=oo, blk=blk, b=bank: t.matmul(
                    b[:], w1f_row[0:1, oo * 128:(oo + 1) * 128],
                    negm_row[0:1, blk * 512: blk * 512 + 512],
                    start=False, stop=False),
                    waits=[ww1] if (g == 0 and blk == 0) else ())
                for j in range(2):
                    add("tensor", lambda t, oo=oo, blk=blk, j=j, b=bank: t.matmul(
                        b[:], bSb[:, j * D + oo * 128: j * D + (oo + 1) * 128],
                        xab[:, (blk * 2 + j) * 512:(blk * 2 + j + 1) * 512],
                        start=False, stop=(j == 1)),
                        inc=("cT", 1) if j == 1 else None)
                wmm = ("cT", plan.at("cT"))
                if blk == NBLK - 1:
                    ms[("p1", "slabdone", oo)] = wmm
                t1 = t1_t[u % 2]
                xm = xm_st[u % 2]
                wt1 = [wmm]
                if u >= 2:
                    wt1.append(ms[("p1", "xmio", u - 2)])
                add("vector", lambda v, b=bank, blk=blk, t1=t1: v.tensor_tensor(
                    t1[:], b[:], rb_sb[:, blk * 512: blk * 512 + 512],
                    op=ALU.mult), waits=wt1, inc=("cV", 1))
                ms[("p1", "t1", u)] = ("cV", plan.at("cV"))
                add("vector", lambda v, t1=t1, xm=xm, oo=oo, xf=xf:
                    v.scalar_tensor_tensor(xm[:], t1[:], bfh_sb[:, oo:oo + 1],
                                           xf[:], op0=ALU.add, op1=ALU.add),
                    waits=[wxf, ("dXm", plan.at("dXm"))], inc=("cV", 1))
                wev = ("cV", plan.at("cV"))
                add("sync", lambda s, oo=oo, blk=blk, xm=xm: s.dma_start(
                    out=d_xm[oo, :, gcol + blk * 512: gcol + blk * 512 + 512],
                    in_=xm[:]), waits=[wev], inc=("dXm", 16))
                ms[("p1", "res", u)] = ("dXm", plan.at("dXm"))
                stats_ops(u, oo, blk, xm, first=(oo == 0), last=(oo == NCH - 1),
                          tag="l2", wsrc=wev)
                ms[("p1", "xmio", u)] = ("cV", plan.at("cV"))
            if oo + 2 < NCH:
                slw[oo + 2] = slab_load(d_wf, oo + 2, [ms[("p1", "slabdone", oo)]])
        stats_flush()
        ms["ffn_done"] = ("cT", plan.at("cT"))
        ms["xmid_stored"] = ("dXm", plan.at("dXm"))
        if stop_after == "ffn":
            return
        ln_finalize(half=False, tag=f"ln2_{g}")
        if stop_after == "ln2":
            return
        ms["stats_free"] = ms[f"rdy_ln2_{g}"]

        add("gpsimd", lambda gp: gp.dma_start(
            out=xbig[:].rearrange("p (i t) -> p i t", t=GT),
            in_=d_xm[:, :, gcol:gcol + GT].rearrange("i p t -> p i t")),
            waits=[ms["ffn_done"], ms["xmid_stored"]], inc=("dXB", 16))
        wxm = ("dXB", plan.at("dXB"))

        def qslab_load(o, wprev):
            sem = ("dSlA", "dSlB")[o % 2]
            src = d_wq[:, o * 128:(o + 1) * 128].rearrange("(i p) c -> p i c", p=128)
            add("gpsimd", lambda gp, o=o, src=src: gp.dma_start(
                out=wslab[o % 2][:].rearrange("p (i c) -> p i c", c=128),
                in_=src), waits=wprev, inc=(sem, 16))
            return (sem, plan.at(sem))

        qlw = {0: qslab_load(0, []), 1: qslab_load(1, [])}
        for oo in range(QCH):
            slab = wslab[oo % 2]
            if g == 0:
                for i in range(NCH):
                    w0 = [qlw[oo], ms["w1ev"]]
                    add("tensor", lambda t, slab=slab, i=i: t.matmul(
                        pR[0:1, 0:128], onescol[:], slab[:, i * 128:(i + 1) * 128],
                        start=(i == 0), stop=(i == NCH - 1)),
                        waits=w0 if i == 0 else (),
                        inc=("cT", 1) if i == NCH - 1 else None)
                wt = ("cT", plan.at("cT"))
                add("scalar", lambda s, oo=oo: s.activation(
                    w1q_row[0:1, oo * 128:(oo + 1) * 128], pR[0:1, 0:128],
                    AF.Copy, bias=0.0, scale=1.0), waits=[wt], inc=("cS", 1))
                ms["w1ev"] = ("cS", plan.at("cS"))
            for blk in range(NBLK):
                u = oo * NBLK + blk
                bank = (pA, pB)[u % 2]
                mw = [qlw[oo], wxm, ms[f"rdy_ln2_{g}"]]
                if u >= 2:
                    mw.append(ms[("qk", "ev", u - 2)])
                for i in range(NCH):
                    add("tensor", lambda t, slab=slab, i=i, blk=blk, b=bank: t.matmul(
                        b[:], slab[:, i * 128:(i + 1) * 128],
                        xbig[:, i * GT + blk * 512: i * GT + blk * 512 + 512],
                        start=(i == 0), stop=False), waits=mw if i == 0 else ())
                add("tensor", lambda t, oo=oo, blk=blk, b=bank: t.matmul(
                    b[:], w1q_row[0:1, oo * 128:(oo + 1) * 128],
                    negm_row[0:1, blk * 512: blk * 512 + 512],
                    start=False, stop=True),
                    waits=[ms["w1ev"]] if (g == 0 and blk == 0) else (),
                    inc=("cT", 1))
                wmm = ("cT", plan.at("cT"))
                if blk == NBLK - 1:
                    ms[("qk", "slabdone", oo)] = wmm
                t1 = t1_t[u % 2]
                qb = qb_t[u % 2]
                wt1 = [wmm]
                if u >= 2:
                    wt1.append(ms[("qk", "cons", u - 2)])
                add("vector", lambda v, b=bank, blk=blk, t1=t1: v.tensor_tensor(
                    t1[:], b[:], rb_sb[:, blk * 512: blk * 512 + 512],
                    op=ALU.mult), waits=wt1, inc=("cV", 1))
                add("vector", lambda v, t1=t1, qb=qb, oo=oo: v.tensor_scalar(
                    qb[:], t1[:], bq_sb[:, oo:oo + 1], None, ALU.add),
                    inc=("cV", 1))
                ms[("qk", "ev", u)] = ("cV", plan.at("cV"))
                wqb = ms[("qk", "ev", u)]
                if oo < 34:
                    is_q = oo < 32
                    cbase = (0 if is_q else 2) * 1024 + blk * 512
                    sbase = (1 if is_q else 3) * 1024 + blk * 512
                    bankp = (pC, pD)[u % 2]
                    wpp = [wqb]
                    if u >= 2:
                        wpp.append(ms[("qk", "permev", u - 2)])
                    add("tensor", lambda t, qb=qb, bp=bankp: t.matmul(
                        bp[:], permb[:], qb[:], start=True, stop=True),
                        waits=wpp, inc=("cT", 1))
                    wpm = ("cT", plan.at("cT"))
                    t1b = t1_t[u % 2]
                    add("vector", lambda v, bp=bankp, sbase=sbase, t1b=t1b:
                        v.tensor_tensor(t1b[:], bp[:],
                                        tabs[:, sbase:sbase + 512], op=ALU.mult),
                        waits=[wpm, wtab], inc=("cV", 1))
                    ms[("qk", "permev", u)] = ("cV", plan.at("cV"))
                    xm2 = xm_st[u % 2]
                    add("vector", lambda v, qb=qb, cbase=cbase, xm2=xm2:
                        v.tensor_tensor(xm2[:], qb[:],
                                        tabs[:, cbase:cbase + 512], op=ALU.mult),
                        inc=("cV", 1))
                    rq = rq_t[u % 2]
                    wrq_prev = [ms[("qk", "rqio", u - 2)]] if u >= 2 else []
                    add("vector", lambda v, t1b=t1b, xm2=xm2, rq=rq: v.tensor_add(
                        rq[:], t1b[:], xm2[:]), waits=wrq_prev, inc=("cV", 1))
                    wrq = ("cV", plan.at("cV"))
                    if is_q:
                        add("sync", lambda s, oo=oo, blk=blk, rq=rq: s.dma_start(
                            out=d_q[oo, :, gcol + blk * 512: gcol + blk * 512 + 512],
                            in_=rq[:]), waits=[wrq], inc=("dQ", 16))
                        ms[("qk", "rqio", u)] = ("dQ", plan.at("dQ"))
                    else:
                        kvh = oo - 32
                        coff = kvh * (128 * TOK) + gcol + blk * 512
                        add("sync", lambda s, rq=rq, coff=coff: s.dma_start(
                            out=dataclasses.replace(
                                d_shk[0, 0, :, 0:512], offset=_rt(s, "kw") + coff),
                            in_=rq[:]), waits=[wrq], inc=("dKw", 16))
                        ms[("qk", "rqio", u)] = ("dKw", plan.at("dKw"))
                    ms[("qk", "cons", u)] = ("cV", plan.at("cV"))
                else:
                    vh = oo - 34
                    wv0 = [wqb]
                    if ("qk", "vstlast") in ms:
                        wv0.append(ms[("qk", "vstlast")])
                        wv0.append(("dV", plan.at("dV")))
                    vstb = vst_t[blk]
                    for pt in range(4):
                        add("tensor", lambda t, qb=qb, pt=pt: t.transpose(
                            pT[:, pt * 128:(pt + 1) * 128],
                            qb[:, pt * 128:(pt + 1) * 128], identb[:]),
                            waits=wv0 if pt == 0 else (),
                            inc=("cT", 1) if pt == 3 else None)
                    wtr = ("cT", plan.at("cT"))
                    add("scalar", lambda s, vh=vh, vstb=vstb: s.activation(
                        vstb[:].rearrange("p (pt c) -> p pt c", c=ER)[
                            :, :, vh * 128:(vh + 1) * 128],
                        pT[:, 0:512].rearrange("p (pt c) -> p pt c", c=128),
                        AF.Copy, bias=0.0, scale=1.0),
                        waits=[wtr], inc=("cS", 1))
                    wvs = ("cS", plan.at("cS"))
                    ms[("qk", "vstlast")] = wvs
                    if vh == 1:
                        ptg = g * 8 + blk * 4
                        coff = ptg * (128 * ER)
                        add("sync", lambda s, coff=coff, vstb=vstb: s.dma_start(
                            out=dataclasses.replace(
                                d_shv[0, 0:4, :, :].rearrange("l p c -> p l c"),
                                offset=_rt(s, "vw") + coff),
                            in_=vstb[:].rearrange("p (pt c) -> p pt c", c=ER)),
                            waits=[wvs], inc=("dV", 16))
                    ms[("qk", "cons", u)] = ("cS", plan.at("cS"))
            if oo + 2 < QCH:
                qlw[oo + 2] = qslab_load(oo + 2, [ms[("qk", "slabdone", oo)]])
        ms["xbig_free"] = ("cT", plan.at("cT"))
        ms["tabs_free"] = ("cV", plan.at("cV"))
        ms[f"qkv_done_{g}"] = [("dQ", plan.at("dQ")), ("dKw", plan.at("dKw")),
                               ("dV", plan.at("dV"))]

    def flag_and_poll():
        add("sync", lambda s: s.dma_start(
            out=dataclasses.replace(d_shf[0, 0:2], offset=_rt(s, "fw")),
            in_=offs_sb[0:1, 4:6]), waits=ms["qkv_done_1"], inc=("dF", 16))

        def poll(s):
            rflag = s.alloc_register("rflag")
            rcnt = s.alloc_register("rcnt")
            rv_f = RuntimeValue(val=rflag, min_val=0, max_val=(1 << 30) - 1)
            rv_c = RuntimeValue(val=rcnt, min_val=0, max_val=(1 << 30) - 1)
            s.reg_mov(rcnt, 0)
            for widx, nkey in ((0, "n1"), (1, "n2")):
                nv = _rt(s, nkey)
                src = dataclasses.replace(d_shf[0, 0:2], offset=_rt(s, "fr"))
                s.reg_mov(rflag, 0)
                with s.While(lambda: rv_f != nv):
                    s.reg_add(rcnt, rcnt, 16)
                    s.dma_start(out=flagbuf[0:1, 0:2], in_=src).then_inc(
                        sems["pol"], 16)
                    s.wait_ge(sems["pol"], rv_c)
                    s.reg_load(rflag, flagbuf[0:1, widx:widx + 1])
            return s.nop()
        add("sync", poll)

    def phase2(only_tiles=None, only_hg=None, dump=False, only_h4=None):
        for sec in range(2):
            for kvh in range(NKV):
                add("sync", lambda s, sec=sec, kvh=kvh: s.dma_start(
                    out=ksb[:, (sec * NKV + kvh) * TOK:(sec * NKV + kvh + 1) * TOK],
                    in_=d_shk[sec, kvh, :, :]), inc=("dKV", 16))
        add("sync", lambda s: s.dma_start(
            out=vsb[:].rearrange("p (u c) -> p u c", c=ER),
            in_=d_shv.rearrange("h l p c -> p (h l) c")), inc=("dKV", 16))
        wkv = ("dKV", plan.at("dKV"))
        p4_free = None
        last_exp = {0: None, 1: None}
        last_norm = None
        scnt = [0]
        hpar = [0]
        for i in (range(NT) if only_tiles is None else only_tiles):
            seclen = (i + 1) * 128
            qt = qt_t[i % 2]
            qsem = f"dQt{i % 2}"
            qw = [] if i < 2 else [ms[("p2q", i - 2)]]
            add("sync", lambda s, i=i, qt=qt: s.dma_start(
                out=qt[:].rearrange("p (c q) -> p c q", q=128),
                in_=d_q[:, :, i * 128:(i + 1) * 128].rearrange("c p q -> p c q")),
                waits=qw, inc=(qsem, 16))
            wq = (qsem, plan.at(qsem))
            mt = mt_t[i % 2]
            msem = f"dMt{i % 2}"
            mw = [] if i < 2 else [ms[("p2m", i - 2)]]
            add("sync", lambda s, i=i, mt=mt: s.dma_start(
                out=mt[:], in_=d_bm[:, i * 256:(i + 1) * 256]),
                waits=mw, inc=(msem, 16))
            wmt = (msem, plan.at(msem))
            for hg in (range(8) if only_hg is None else only_hg):
                kvh = hg // 4
                otb = (pC, pD)[hg % 2]
                for h4 in (range(4) if only_h4 is None else only_h4):
                    head = hg * 4 + h4
                    par_ = hpar[0] % 2
                    hpar[0] += 1
                    srow = srow_t[par_]
                    col = h4 + 4 * (hg % 2)
                    cm0 = par_ * 12
                    ncm = 0
                    qsl = qt[:, head * 128:(head + 1) * 128]
                    for sec in range(2):
                        kbase = (sec * NKV + kvh) * TOK
                        sb0 = sec * seclen
                        for c0, cw in _chunks(i * 128, 512):
                            nsc = scnt[0]
                            bank = (pA, pB)[nsc % 2]
                            sw = [wkv, wq]
                            if ("p2sev", nsc - 2) in ms:
                                prev = ms[("p2sev", nsc - 2)]
                                sw.extend(prev if isinstance(prev, list) else [prev])
                            add("tensor", lambda t, b=bank, qsl=qsl, kbase=kbase,
                                c0=c0, cw=cw: t.matmul(
                                    b[:, 0:cw], qsl,
                                    ksb[:, kbase + c0:kbase + c0 + cw],
                                    start=True, stop=True),
                                waits=sw, inc=("cT", 1))
                            wsm = ("cT", plan.at("cT"))
                            sev = [wsm]
                            if last_exp[par_] is not None:
                                sev.append(last_exp[par_])
                            if nsc % 4 == 3:
                                add("vector", lambda v, b=bank, sb0=sb0, c0=c0,
                                    cw=cw, srow=srow: v.tensor_copy(
                                        srow[:, sb0 + c0: sb0 + c0 + cw],
                                        b[:, 0:cw]),
                                    waits=sev, inc=("cV", 1))
                                ms[("p2sev", nsc)] = ("cV", plan.at("cV"))
                            else:
                                add("scalar", lambda s, b=bank, sb0=sb0, c0=c0,
                                    cw=cw, srow=srow:
                                    s.activation(srow[:, sb0 + c0: sb0 + c0 + cw],
                                                 b[:, 0:cw], AF.Copy, bias=0.0,
                                                 scale=1.0),
                                    waits=sev, inc=("cS", 1))
                                ms[("p2sev", nsc)] = ("cS", plan.at("cS"))
                            scnt[0] += 1
                        nsc = scnt[0]
                        bank = (pA, pB)[nsc % 2]
                        sw = [wkv, wq, wmt]
                        if ("p2sev", nsc - 2) in ms:
                            prev = ms[("p2sev", nsc - 2)]
                            sw.extend(prev if isinstance(prev, list) else [prev])
                        add("tensor", lambda t, b=bank, qsl=qsl, kbase=kbase,
                            sb1=i * 128: t.matmul(
                                b[:, 0:128], qsl,
                                ksb[:, kbase + sb1:kbase + sb1 + 128],
                                start=True, stop=True),
                            waits=sw, inc=("cT", 1))
                        wsm = ("cT", plan.at("cT"))
                        sev = [wsm]
                        if last_exp[par_] is not None:
                            sev.append(last_exp[par_])
                        add("vector", lambda v, b=bank, sec=sec, sb0=sb0,
                            sb1=i * 128, mt=mt, srow=srow: v.scalar_tensor_tensor(
                                srow[:, sb0 + sb1: sb0 + sb1 + 128], b[:, 0:128],
                                1.0, mt[:, sec * 128:(sec + 1) * 128],
                                op0=ALU.mult, op1=ALU.add),
                            waits=sev, inc=("cV", 1))
                        ms[("p2sev", nsc)] = ("cV", plan.at("cV"))
                        scnt[0] += 1
                    rowlen = 2 * seclen
                    wall = [("cS", plan.at("cS")), ("cV", plan.at("cV"))]
                    if ("p2nmfree", col) in ms:
                        wall.append(ms[("p2nmfree", col)])
                    add("vector", lambda v, col=col, rowlen=rowlen, srow=srow:
                        v.reduce_max(
                            negmx[:, col:col + 1], srow[:, 0:rowlen],
                            axis=AX.X, negate=True),
                        waits=wall, inc=("cV", 1))
                    wmx = ("cV", plan.at("cV"))
                    wex = [wmx, ("cS", plan.at("cS"))]
                    if p4_free is not None:
                        wex.append(p4_free)
                    add("scalar", lambda s, h4=h4, rowlen=rowlen, srow=srow,
                        col=col: s.activation(
                            p4[:, h4 * 4096: h4 * 4096 + rowlen], srow[:, 0:rowlen],
                            AF.Exp, bias=negmx[:, col:col + 1], scale=1.0,
                            accum_out=sums4[:, col:col + 1]),
                        waits=wex, inc=("cS", 1))
                    last_exp[par_] = ("cS", plan.at("cS"))
                    ms[("p2nmfree", col)] = last_exp[par_]
                    add("vector", lambda v, col=col: v.reciprocal(
                        rs4[:, col:col + 1], sums4[:, col:col + 1]),
                        waits=[last_exp[par_]], inc=("cV", 1))
                    wrc = ("cV", plan.at("cV"))
                    add("vector", lambda v, h4=h4, rowlen=rowlen, col=col:
                        v.tensor_scalar(
                            p4[:, h4 * 4096: h4 * 4096 + rowlen],
                            p4[:, h4 * 4096: h4 * 4096 + rowlen],
                            rs4[:, col:col + 1], None, ALU.mult),
                        waits=[wrc], inc=("cV", 1))
                    ms[("p2pn", h4)] = ("cV", plan.at("cV"))
                    last_norm = ms[("p2pn", h4)]
                wpn = [ms[("p2pn", h4)] for h4 in
                       (range(4) if only_h4 is None else only_h4)]
                h4list = list(range(4) if only_h4 is None else only_h4)
                nh = len(h4list)
                nkt = 2 * (i + 1)
                for sw0 in range(0, nkt, 16):
                    kts = range(sw0, min(sw0 + 16, nkt))
                    for kt in kts:
                        ptbank = (pT[:], pR[:].bitcast(BF16))[kt % 2]
                        for hx, h4 in enumerate(h4list):
                            tw = []
                            if kt == 0 and hx == 0:
                                tw = list(wpn)
                            if kt >= 2 and hx == 0:
                                tw.append(ms[("p2pte", kt - 2)])
                            if kt >= 16 and hx == 0:
                                tw.append(ms[("p2pv", kt - 16)])
                            add("tensor", lambda t, kt=kt, h4=h4, hx=hx,
                                ptbank=ptbank: t.transpose(
                                    ptbank[:, hx * 128:(hx + 1) * 128],
                                    p4[:, h4 * 4096 + kt * 128:
                                       h4 * 4096 + (kt + 1) * 128],
                                    identb[:]),
                                waits=tw, inc=("cT", 1) if hx == nh - 1 else None)
                        wtr = ("cT", plan.at("cT"))
                        slot = kt % 16
                        if kt % 4 == 1:
                            add("vector", lambda v, slot=slot, ptbank=ptbank:
                                v.tensor_copy(
                                    ptb[:, slot * nh * 128:(slot + 1) * nh * 128],
                                    ptbank[:, 0:nh * 128]),
                                waits=[wtr], inc=("cV", 1))
                            ms[("p2pte", kt)] = ("cV", plan.at("cV"))
                        else:
                            add("scalar", lambda s, slot=slot, ptbank=ptbank:
                                s.activation(
                                    ptb[:, slot * nh * 128:(slot + 1) * nh * 128],
                                    ptbank[:, 0:nh * 128],
                                    AF.Copy, bias=0.0, scale=1.0),
                                waits=[wtr], inc=("cS", 1))
                            ms[("p2pte", kt)] = ("cS", plan.at("cS"))
                    for kt in kts:
                        sec = 1 if kt >= (i + 1) else 0
                        lt = kt - (i + 1) * sec
                        vcol = (sec * NT + lt) * ER + kvh * 128
                        slot = kt % 16
                        pw = [ms[("p2pte", kt)]]
                        if hg >= 2 and kt == 0:
                            pw.append(ms[("p2cte", hg - 2)])
                        add("tensor", lambda t, otb=otb, vcol=vcol, slot=slot,
                            kt=kt, last=nkt - 1: t.matmul(
                                otb[:, 0:nh * 128], vsb[:, vcol:vcol + 128],
                                ptb[:, slot * nh * 128:(slot + 1) * nh * 128],
                                start=(kt == 0), stop=(kt == last)),
                            waits=pw,
                            inc=("cT", 1))
                        ms[("p2pv", kt)] = ("cT", plan.at("cT"))
                p4_free = ("cT", plan.at("cT"))
                wpv = p4_free
                cst = ctxst[hg % 2]
                wcs = [wpv]
                if hg >= 2:
                    wcs.append(ms[("p2cio", hg - 2)])
                add("scalar", lambda s, otb=otb, cst=cst: s.activation(
                    cst[:], otb[:], AF.Copy, bias=0.0, scale=1.0),
                    waits=wcs, inc=("cS", 1))
                ms[("p2cte", hg)] = ("cS", plan.at("cS"))
                add("sync", lambda s, hg=hg, i=i, cst=cst: s.dma_start(
                    out=d_ctx[hg * 4:(hg + 1) * 4, :, i * 128:(i + 1) * 128]
                    .rearrange("h p q -> p h q"),
                    in_=cst[:].rearrange("p (h q) -> p h q", q=128)),
                    waits=[ms[("p2cte", hg)]], inc=("dCtx", 16))
                ms[("p2cio", hg)] = ("dCtx", plan.at("dCtx"))
            ms[("p2q", i)] = ("cT", plan.at("cT"))
            ms[("p2m", i)] = ("cV", plan.at("cV"))
            if dump:
                rl = 2 * (i + 1) * 128
                wd = [("cS", plan.at("cS")), ("cV", plan.at("cV")),
                      ("cG", plan.at("cG")), ("cT", plan.at("cT"))]
                add("gpsimd", lambda g, rl=rl: g.dma_start(
                    out=d_dbg[:, 0:rl], in_=srow_t[0][:, 0:rl]), waits=wd,
                    inc=("dDbg", 16))
                add("gpsimd", lambda g, rl=rl: g.dma_start(
                    out=d_dbg[:, 4096:4096 + rl], in_=p4[:, 0:rl]), waits=wd,
                    inc=("dDbg", 16))
                _pw = 2 * (i + 1) * (1 if only_h4 else 4) * 128
                add("gpsimd", lambda g, _pw=_pw: g.dma_start(
                    out=d_dbg[:, 8192:8192 + _pw], in_=ptb[:, 0:_pw]),
                    waits=wd, inc=("dDbg", 16))
                add("gpsimd", lambda g: g.dma_start(
                    out=d_dbg[:, 12288:12289], in_=sums4[:, 0:1],
                    allow_slow_non_contiguous=True), waits=wd,
                    inc=("dDbg", 16))
                add("gpsimd", lambda g: g.dma_start(
                    out=d_dbg[:, 12292:12293], in_=negmx[:, 0:1],
                    allow_slow_non_contiguous=True), waits=wd,
                    inc=("dDbg", 16))
                add("gpsimd", lambda g, i=i: g.dma_start(
                    out=d_dbg[:, 12296:12552], in_=qt_t[i % 2][:, 0:256]),
                    waits=wd, inc=("dDbg", 16))
                add("gpsimd", lambda g, i=i: g.dma_start(
                    out=d_dbg[:, 12552:12808], in_=mt_t[i % 2][:]), waits=wd,
                    inc=("dDbg", 16))
        ms["ctx_stored"] = ("dCtx", plan.at("dCtx"))

    def phase3_group(g):
        gcol = g * GT
        prev_ev3 = ("cV", plan.at("cV")) if g > 0 else None
        for blk in range(NBLK):
            cb = ctxb_t[blk]
            waits = [ms["ctx_stored"]]
            if g > 0:
                waits.append(ms["ffn2_done_0"])
            add("sync", lambda s, blk=blk, cb=cb: s.dma_start(
                out=cb[:].rearrange("p (c q) -> p c q", q=512),
                in_=d_ctx[:, :, gcol + blk * 512: gcol + blk * 512 + 512]
                .rearrange("c p q -> p c q")),
                waits=waits, inc=(f"dCb{blk}", 16))
        wcb = [("dCb0", plan.at("dCb0")), ("dCb1", plan.at("dCb1"))]

        def dslab_load(o, wprev):
            sem = ("dSlA", "dSlB")[o % 2]
            src = d_dT[:, o * 128:(o + 1) * 128].rearrange("(c p) k -> p c k", p=128)
            add("gpsimd", lambda gp, o=o, src=src: gp.dma_start(
                out=wslab[o % 2][:].rearrange("p (c k) -> p c k", k=128),
                in_=src), waits=wprev, inc=(sem, 16))
            return (sem, plan.at(sem))

        dlw = {0: dslab_load(0, [] if g == 0 else [ms["ffn2_done_0"]]),
               1: dslab_load(1, [])}
        for oo in range(NCH):
            slab = wslab[oo % 2]
            for blk in range(NBLK):
                u = oo * NBLK + blk
                bank = (pA, pB)[u % 2]
                xf = xf_t[u % 4]
                xsem = f"dXf{u % 4}"
                xw = [ms[("d3", "res", u - 4)]] if u >= 4 else []
                add("sync", lambda s, oo=oo, blk=blk, xf=xf: s.dma_start(
                    out=xf[:],
                    in_=d_xm[oo, :, gcol + blk * 512: gcol + blk * 512 + 512]),
                    waits=xw, inc=(xsem, 16))
                wxf = (xsem, plan.at(xsem))
                mw = [dlw[oo]] + wcb
                if u >= 2:
                    mw.append(ms[("d3", "t1", u - 2)])
                elif prev_ev3 is not None:
                    mw.append(prev_ev3)
                if u == 0 and "stats_free" in ms:
                    mw.append(ms["stats_free"])
                cb = ctxb_t[blk]
                for c in range(NCH):
                    add("tensor", lambda t, slab=slab, c=c, cb=cb, b=bank: t.matmul(
                        b[:], slab[:, c * 128:(c + 1) * 128],
                        cb[:, c * 512:(c + 1) * 512],
                        start=(c == 0), stop=(c == NCH - 1)),
                        waits=mw if c == 0 else (),
                        inc=("cT", 1) if c == NCH - 1 else None)
                wmm = ("cT", plan.at("cT"))
                if blk == NBLK - 1:
                    ms[("d3", "slabdone", oo)] = wmm
                xm = xm_st[u % 2]
                wt1 = [wmm, wxf, ("dXa", plan.at("dXa"))]
                if u >= 2:
                    wt1.append(ms[("d3", "xmio", u - 2)])
                add("vector", lambda v, b=bank, xf=xf, xm=xm: v.tensor_add(
                    xm[:], b[:], xf[:]), waits=wt1, inc=("cV", 1))
                ms[("d3", "t1", u)] = ("cV", plan.at("cV"))
                wev = ms[("d3", "t1", u)]
                add("sync", lambda s, oo=oo, blk=blk, xm=xm: s.dma_start(
                    out=d_xa[oo, :, gcol + blk * 512: gcol + blk * 512 + 512],
                    in_=xm[:]), waits=[wev], inc=("dXa", 16))
                ms[("d3", "res", u)] = ("dXa", plan.at("dXa"))
                stats_ops(u, oo, blk, xm, first=(oo == 0), last=(oo == NCH - 1),
                          tag="l3", wsrc=wev)
                ms[("d3", "xmio", u)] = ("cV", plan.at("cV"))
            if oo + 2 < NCH:
                dlw[oo + 2] = dslab_load(oo + 2, [ms[("d3", "slabdone", oo)]])
        ms[("d3", "slabdone_last")] = ms[("d3", "slabdone", NCH - 1)]
        stats_flush()
        ms[f"dense_done_{g}"] = ("cT", plan.at("cT"))
        ms["xatt_stored"] = ("dXa", plan.at("dXa"))
        ln_finalize(half=True, tag=f"ln3_{g}")
        ms["stats_free"] = ms[f"rdy_ln3_{g}"]

        add("gpsimd", lambda gp: gp.dma_start(
            out=xbig[:].rearrange("p (i t) -> p i t", t=GT),
            in_=d_xa[:, :, gcol:gcol + GT].rearrange("i p t -> p i t")),
            waits=[ms[f"dense_done_{g}"], ms["xatt_stored"]], inc=("dXB", 16))
        wxb = ("dXB", plan.at("dXB"))
        xa_adapters(f"ln3_{g}", wxb)

        def fslab_load(o, wprev):
            sem = ("dSlA", "dSlB")[o % 2]
            src = d_wf[:, o * 128:(o + 1) * 128].rearrange("(i p) c -> p i c", p=128)
            add("gpsimd", lambda gp, o=o, src=src: gp.dma_start(
                out=wslab[o % 2][:].rearrange("p (i c) -> p i c", c=128),
                in_=src), waits=wprev, inc=(sem, 16))
            return (sem, plan.at(sem))

        flw = {0: fslab_load(0, [ms[("d3", "slabdone_last")]]),
               1: fslab_load(1, [])}
        for oo in range(NCH):
            slab = wslab[oo % 2]
            for blk in range(NBLK):
                u = oo * NBLK + blk
                bank = (pA, pB)[u % 2]
                xf = xf_t[u % 4]
                xsem = f"dXf{u % 4}"
                xw = [ms[("f2", "out", u - 4)]] if u >= 4 else [ms["xatt_stored"]]
                add("sync", lambda s, oo=oo, blk=blk, xf=xf: s.dma_start(
                    out=xf[:],
                    in_=d_xa[oo, :, gcol + blk * 512: gcol + blk * 512 + 512]),
                    waits=xw, inc=(xsem, 16))
                wxf = (xsem, plan.at(xsem))
                mw = [flw[oo], wxb, ms["xa_rdy"]]
                if u >= 2:
                    mw.append(ms[("f2", "t1", u - 2)])
                for i in range(NCH):
                    add("tensor", lambda t, slab=slab, i=i, blk=blk, b=bank: t.matmul(
                        b[:], slab[:, i * 128:(i + 1) * 128],
                        xbig[:, i * GT + blk * 512: i * GT + blk * 512 + 512],
                        start=(i == 0), stop=False), waits=mw if i == 0 else ())
                add("tensor", lambda t, oo=oo, blk=blk, b=bank: t.matmul(
                    b[:], w1f_row[0:1, oo * 128:(oo + 1) * 128],
                    negm_row[0:1, blk * 512: blk * 512 + 512],
                    start=False, stop=False))
                for j in range(2):
                    add("tensor", lambda t, oo=oo, blk=blk, j=j, b=bank: t.matmul(
                        b[:], bSb[:, j * D + oo * 128: j * D + (oo + 1) * 128],
                        xab[:, (blk * 2 + j) * 512:(blk * 2 + j + 1) * 512],
                        start=False, stop=(j == 1)),
                        inc=("cT", 1) if j == 1 else None)
                wmm = ("cT", plan.at("cT"))
                if blk == NBLK - 1:
                    ms[("f2", "slabdone", oo)] = wmm
                t1 = t1_t[u % 2]
                xm = xm_st[u % 2]
                wt1 = [wmm]
                if u >= 2:
                    wt1.append(ms[("f2", "out", u - 2)])
                add("vector", lambda v, b=bank, blk=blk, t1=t1: v.tensor_tensor(
                    t1[:], b[:], rb_sb[:, blk * 512: blk * 512 + 512],
                    op=ALU.mult), waits=wt1, inc=("cV", 1))
                ms[("f2", "t1", u)] = ("cV", plan.at("cV"))
                add("vector", lambda v, t1=t1, xm=xm, oo=oo, xf=xf:
                    v.scalar_tensor_tensor(xm[:], t1[:], bfh_sb[:, oo:oo + 1],
                                           xf[:], op0=ALU.add, op1=ALU.add),
                    waits=[wxf, ("dSt", plan.at("dSt"))], inc=("cV", 1))
                wev = ("cV", plan.at("cV"))
                add("sync", lambda s, oo=oo, blk=blk, xm=xm: s.dma_start(
                    out=d_out[oo * 128:(oo + 1) * 128,
                              gcol + blk * 512: gcol + blk * 512 + 512],
                    in_=xm[:]), waits=[wev], inc=("dSt", 16))
                ms[("f2", "out", u)] = ("dSt", plan.at("dSt"))
            if oo + 2 < NCH:
                flw[oo + 2] = fslab_load(oo + 2, [ms[("f2", "slabdone", oo)]])
        ms[f"ffn2_done_{g}"] = ("cT", plan.at("cT"))

    nb = [0]
    preloads()
    if _DEBUG == "pre":
        add("sync", lambda s: s.dma_start(
            out=d_out[0:128, 0:QCH], in_=bq_sb[:]),
            waits=[ms["a1_done"]], inc=("dSt", 16))
    elif _DEBUG.startswith("grp0"):
        stop = _DEBUG[5:] or None
        phase1_group(0, stop_after=stop)
        if stop in ("ffn", "ln2"):
            add("sync", lambda s: s.dma_start(
                out=d_out[:, 0:GT].rearrange("(c p) t -> c p t", p=128),
                in_=d_xm[:, :, 0:GT]),
                waits=[ms["xmid_stored"]], inc=("dSt", 16))
        elif stop in ("ln1", "xa"):
            add("sync", lambda s: s.dma_start(
                out=d_out[0:128, 0:GT], in_=rb_sb[:]),
                waits=[ms[f"rdy_ln1_0"]], inc=("dSt", 16))
        else:
            add("sync", lambda s: s.dma_start(
                out=d_out.rearrange("(c p) t -> c p t", p=128),
                in_=d_xm[:, :, :]),
                waits=[ms["xmid_stored"]] + ms["qkv_done_0"], inc=("dSt", 16))
    else:
        for g in range(NG):
            phase1_group(g)
    if _DEBUG == "pre" or _DEBUG.startswith("grp0"):
        pass
    elif _DEBUG == "xmid":
        add("sync", lambda s: s.dma_start(
            out=d_out.rearrange("(c p) t -> c p t", p=128), in_=d_xm[:, :, :]),
            waits=[ms["xmid_stored"]], inc=("dSt", 16))
    elif _DEBUG == "qkv":
        add("gpsimd", lambda g: g.dma_start(
            out=d_out.rearrange("(c p) t -> c p t", p=128), in_=d_q[:, :, :]),
            waits=ms["qkv_done_1"] + [ms["xmid_stored"]], inc=("dSt", 16))
    elif _DEBUG == "poll":
        flag_and_poll()
        nb[0] += 1
        barrier(nb[0])
        add("gpsimd", lambda g: g.dma_start(
            out=d_out[0:1, 0:16], in_=flagbuf[:]), inc=("dDbg", 16))
    elif _DEBUG == "kvload":
        flag_and_poll()
        nb[0] += 1
        barrier(nb[0])
        for sec in range(2):
            for kvh in range(NKV):
                add("sync", lambda s, sec=sec, kvh=kvh: s.dma_start(
                    out=ksb[:, (sec * NKV + kvh) * TOK:(sec * NKV + kvh + 1) * TOK],
                    in_=d_shk[sec, kvh, :, :]), inc=("dKV", 16))
        add("sync", lambda s: s.dma_start(
            out=vsb[:].rearrange("p (u c) -> p u c", c=ER),
            in_=d_shv.rearrange("h l p c -> p (h l) c")), inc=("dKV", 16))
        wkv = ("dKV", plan.at("dKV"))
        lvl = int(os.environ.get("MK_KV", "1"))
        if lvl >= 2:
            add("sync", lambda s: s.dma_start(
                out=qt_t[0][:].rearrange("p (c q) -> p c q", q=128),
                in_=d_q[:, :, 0:128].rearrange("c p q -> p c q")),
                inc=("dQt0", 16))
            add("sync", lambda s: s.dma_start(
                out=mt_t[0][:], in_=d_bm[:, 0:256]), inc=("dMt0", 16))
            wq = ("dQt0", plan.at("dQt0"))
            wmt = ("dMt0", plan.at("dMt0"))
        if lvl >= 3:
            for sec in range(2):
                bank = (pA, pB)[sec]
                add("tensor", lambda t, b=bank, sec=sec: t.matmul(
                    b[:, 0:128], qt_t[0][:, 0:128],
                    ksb[:, (sec * NKV) * TOK:(sec * NKV) * TOK + 128],
                    start=True, stop=True), waits=[wkv, wq, wmt], inc=("cT", 1))
                wsm = ("cT", plan.at("cT"))
                add("vector", lambda v, b=bank, sec=sec: v.scalar_tensor_tensor(
                    srow[:, sec * 128:(sec + 1) * 128], b[:, 0:128], 1.0,
                    mt_t[0][:, sec * 128:(sec + 1) * 128],
                    op0=ALU.mult, op1=ALU.add), waits=[wsm], inc=("cV", 1))
            wse = ("cV", plan.at("cV"))
            add("vector", lambda v: v.reduce_max(
                negmx[:, 0:1], srow[:, 0:256], axis=AX.X, negate=True),
                waits=[wse], inc=("cV", 1))
            wmx = ("cV", plan.at("cV"))
            add("scalar", lambda s: s.activation(
                p4[:, 0:256], srow[:, 0:256], AF.Exp, bias=negmx[:, 0:1],
                scale=1.0, accum_out=sums4[:, 0:1]), waits=[wmx], inc=("cS", 1))
            wex = ("cS", plan.at("cS"))
            add("vector", lambda v: v.reciprocal(rs4[:, 0:1], sums4[:, 0:1]),
                waits=[wex], inc=("cV", 1))
            wrc = ("cV", plan.at("cV"))
            add("gpsimd", lambda g: g.tensor_scalar(
                p4[:, 0:256], p4[:, 0:256], rs4[:, 0:1], None, ALU.mult),
                waits=[wrc], inc=("cG", 1))
            wpn = ("cG", plan.at("cG"))
        if lvl >= 4:
            for kt in range(2):
                for hx in range(1):
                    add("tensor", lambda t, kt=kt: t.transpose(
                        pT[:, (kt % 2) * 128:((kt % 2) + 1) * 128],
                        p4[:, kt * 128:(kt + 1) * 128], identb[:]),
                        waits=[wpn] if kt == 0 else (), inc=("cT", 1))
                wtr = ("cT", plan.at("cT"))
                add("scalar", lambda s, kt=kt: s.activation(
                    ptb[:, kt * 128:(kt + 1) * 128],
                    pT[:, (kt % 2) * 128:((kt % 2) + 1) * 128],
                    AF.Copy, bias=0.0, scale=1.0), waits=[wtr], inc=("cS", 1))
                wpte = ("cS", plan.at("cS"))
                add("tensor", lambda t, kt=kt, wv=wpte: t.matmul(
                    pC[:, 0:128], vsb[:, (kt * NT) * ER: (kt * NT) * ER + 128],
                    ptb[:, kt * 128:(kt + 1) * 128],
                    start=(kt == 0), stop=(kt == 1)),
                    waits=[wpte], inc=("cT", 1) if kt == 1 else None)
            wpv = ("cT", plan.at("cT"))
            add("scalar", lambda s: s.activation(
                ctxst[0][:, 0:128], pC[:, 0:128], AF.Copy, bias=0.0, scale=1.0),
                waits=[wpv], inc=("cS", 1))
            wcte = ("cS", plan.at("cS"))
            add("sync", lambda s: s.dma_start(
                out=d_ctx[0:1, :, 0:128].rearrange("h p q -> p h q"),
                in_=ctxst[0][:, 0:128].rearrange("p (h q) -> p h q", q=128)),
                waits=[wcte], inc=("dCtx", 16))
        add("gpsimd", lambda g: g.dma_start(
            out=d_out[0:128, 0:2048], in_=ksb[:, 0:2048]),
            waits=[wkv, ("dCtx", plan.at("dCtx")), ("cG", plan.at("cG")),
                   ("cS", plan.at("cS"))], inc=("dDbg", 16))
    elif _DEBUG == "att0n":
        flag_and_poll()
        nb[0] += 1
        barrier(nb[0])
        _tl = [int(v) for v in os.environ.get("MK_T", "0").split(",")]
        _hgs = [int(v) for v in os.environ.get("MK_HG", "0").split(",")]
        phase2(only_tiles=_tl, only_hg=_hgs, dump=False)
        add("gpsimd", lambda g: g.dma_start(
            out=d_out[0:512, 0:2048].rearrange("(c p) t -> c p t", p=128),
            in_=d_ctx[0:4, :, :]),
            waits=[ms["ctx_stored"]], inc=("dDbg", 16))
    elif _DEBUG == "att0":
        flag_and_poll()
        nb[0] += 1
        barrier(nb[0])
        import os as _os
        _tl = [int(v) for v in _os.environ.get("MK_T", "0,1").split(",")]
        _h4 = _os.environ.get("MK_H4", "")
        phase2(only_tiles=_tl, only_hg=[0], dump=True,
               only_h4=[int(v) for v in _h4.split(",")] if _h4 else None)
        for a in range(8):
            add("gpsimd", lambda g, a=a: g.dma_start(
                out=d_out[a * 128:(a + 1) * 128, 0:2048],
                in_=d_dbg[:, a * 2048:(a + 1) * 2048]),
                waits=[("dDbg", plan.at("dDbg")), ms["ctx_stored"]],
                inc=("dDbg", 16))
    else:
        flag_and_poll()
        nb[0] += 1
        barrier(nb[0])
        phase2()
        if _DEBUG == "ctx":
            add("gpsimd", lambda g: g.dma_start(
                out=d_out.rearrange("(c p) t -> c p t", p=128), in_=d_ctx[:, :, :]),
                waits=[ms["ctx_stored"]], inc=("dSt", 16))
        else:
            nb[0] += 1
            barrier(nb[0])
            for g in range(NG):
                phase3_group(g)
    fin = plan.at("dSt")
    find = plan.at("dDbg")
    add("sync", lambda s: (s.wait_ge(sems["dDbg"], find) if find else None,
                           s.wait_ge(sems["dSt"], fin))[-1])

    for name in ("dCon", "dCn2", "dXB", "dTab", "dSlA", "dSlB",
                 "dXf0", "dXf1", "dXf2", "dXf3", "dXm", "dQ", "dKw", "dV",
                 "dF", "dKV", "dQt0", "dQt1", "dMt0", "dMt1", "dCtx",
                 "dCb0", "dCb1", "dXa", "dSt", "dDbg", "pol", "bar",
                 "cT", "cV", "cS", "cG"):
        sems[name] = cstk.enter_context(nc.semaphore(name))

    per_eng = {e: [] for e in ENGINES}
    for eng, fn, waits, incs in plan.seq:
        per_eng[eng].append((fn, waits, incs))

    with nc.Block() as block:
        def mk(ops):
            def body(eng):
                for fn, waits, incs in ops:
                    for s, v in waits:
                        eng.wait_ge(sems[s], v)
                    ins = fn(eng)
                    for s, n in incs:
                        tgt = ins if ins is not None else eng.nop()
                        tgt.then_inc(sems[s], n)
            return body
        block.sync(mk(per_eng["sync"]))
        block.gpsimd(mk(per_eng["gpsimd"]))
        block.tensor(mk(per_eng["tensor"]))
        block.vector(mk(per_eng["vector"]))
        block.scalar(mk(per_eng["scalar"]))

    return nc


def kernel(**inputs):
    in_maps = _host_prep(inputs)
    nc = build_program()
    res = run_bass_kernel_spmd(nc, in_maps, core_ids=list(range(8)))
    out = np.empty((NBATCH, S, D), np.float32)
    for c in range(8):
        b, par = c // 2, c % 2
        gl = [_g_of(par, i) for i in range(NT)]
        rows = np.concatenate([np.arange(g * 128, (g + 1) * 128) for g in gl])
        out[b, rows, :] = res.results[c]["outT"].T
    return out


# version 09
# speedup vs baseline: 1.0845x; 1.0083x over previous
import dataclasses
import math
import os
from collections import defaultdict
from contextlib import ExitStack

import numpy as np

import concourse.bass as bass
import concourse.mybir as mybir
from concourse.bass_utils import run_bass_kernel_spmd
from concourse.expressions_rust import RuntimeValue

F32 = mybir.dt.float32
BF16 = mybir.dt.bfloat16
I32 = mybir.dt.int32
U8 = mybir.dt.uint8
AF = mybir.ActivationFunctionType
ALU = mybir.AluOpType
AX = mybir.AxisListType

D = 4096
S = 4096
NBATCH = 4
HD = 128
NH = 32
NKV = 2
QKV_D = 4608
NT = 16
TOK = 2048
NG = 2
GT = 1024
NBLK = 2
NCH = 32
QCH = 36
ER = 256
EPS = 1e-5
NEG = -1e30
ROT = 64

_DEBUG = os.environ.get("MK_DEBUG", "")
ENGINES = ("sync", "gpsimd", "tensor", "vector", "scalar")


def _g_of(par, i):
    return 2 * i + ((i & 1) if par == 0 else 1 - (i & 1))


def _perm_head():
    return np.array(list(range(0, ROT, 2)) + list(range(1, ROT, 2)) +
                    list(range(ROT, HD)), dtype=np.int64)


def _host_prep(inputs):
    x = np.asarray(inputs["L0_attn_hidden_states"], np.float32)
    qkv_w = np.asarray(inputs["L0_attn_qkv_w"], np.float32)
    qkv_b = np.asarray(inputs["L0_attn_qkv_b"], np.float32)
    dense_w = np.asarray(inputs["L0_attn_dense_w"], np.float32)
    W = np.asarray(inputs["L0_ffn_W"], np.float32)
    bias = np.asarray(inputs["L0_ffn_bias"], np.float32)
    A = np.asarray(inputs["L0_ffn_A"], np.float32)
    Bf = np.asarray(inputs["L0_ffn_Bf"], np.float32)
    coeffs = np.asarray(inputs["L0_ffn_coeffs"], np.float32)
    scales = np.asarray(inputs["L0_ffn_scales"], np.float32)

    ph = _perm_head()
    perm_rows = np.arange(QKV_D)
    for h in range(NH):
        perm_rows[h * HD:(h + 1) * HD] = h * HD + ph
    for kv in range(NKV):
        base = D + kv * HD
        perm_rows[base:base + HD] = base + ph
    qkv_w_p = qkv_w[perm_rows]
    qkv_b_p = qkv_b[perm_rows]

    Pm = np.zeros((128, 128), np.float32)
    for m in range(32):
        Pm[m + 32, m] = -1.0
        Pm[m, m + 32] = 1.0

    shared = {
        "wT_qkv": np.ascontiguousarray(qkv_w_p.T),
        "b_qkv": np.ascontiguousarray(qkv_b_p.reshape(QCH, 128).T),
        "wT_ffn": np.ascontiguousarray(W.T),
        "b_ffn": np.ascontiguousarray(bias.reshape(NCH, 128).T),
        "dT": np.ascontiguousarray(dense_w.T),
        "aT": np.ascontiguousarray(A.transpose(2, 0, 1).reshape(D, ER)),
        "bS": np.ascontiguousarray(Bf.transpose(0, 2, 1).reshape(ER, D)),
        "coefr": np.ascontiguousarray(np.repeat(coeffs, 16).reshape(2, 128).T),
        "scalr": np.ascontiguousarray(np.repeat(scales, 16).reshape(2, 128).T),
        "ident": np.eye(128, dtype=np.float32),
        "permm": Pm,
        "ones_col": np.ones((128, 1), np.float32),
        "ones_row": np.ones((1, 128), np.float32),
    }

    inv = 10000.0 ** (-(np.arange(32, dtype=np.float64) / 32.0))
    c_att = 1.0 / math.sqrt(HD)
    rng = np.random.default_rng(int.from_bytes(os.urandom(8), "little"))
    nonce1 = int(rng.integers(1, (1 << 30) - 1))
    nonce2 = int(rng.integers(1, (1 << 30) - 1))

    in_maps = []
    for c in range(8):
        b, par = c // 2, c % 2
        gl = [_g_of(par, i) for i in range(NT)]
        rows = np.concatenate([np.arange(g * 128, (g + 1) * 128) for g in gl])
        m = dict(shared)
        m["xT"] = np.ascontiguousarray(x[b][rows].T)

        pos = rows.astype(np.float64)
        th = pos[None, :] * inv[:, None]
        cos = np.cos(th).astype(np.float32)
        sin = np.sin(th).astype(np.float32)
        ck = np.empty((128, TOK), np.float32)
        sk = np.empty((128, TOK), np.float32)
        ck[0:32] = cos; ck[32:64] = cos; ck[64:128] = 1.0
        sk[0:32] = sin; sk[32:64] = sin; sk[64:128] = 0.0
        m["ck"], m["sk"] = ck, sk
        m["cq"] = (ck * c_att).astype(np.float32)
        m["sq"] = (sk * c_att).astype(np.float32)

        bm = np.zeros((128, NT * 256), np.float32)
        for i in range(NT):
            g = gl[i]
            for sec in range(2):
                gk = _g_of(sec, i)
                blkm = np.zeros((128, 128), np.float32)
                if gk > g:
                    blkm[:] = NEG
                elif gk == g:
                    blkm[np.triu_indices(128, 1)] = NEG
                bm[:, i * 256 + sec * 128: i * 256 + (sec + 1) * 128] = blkm
        m["bmask"] = bm

        kw = par * (NKV * 128 * TOK)
        vw = par * (NT * 128 * ER)
        m["offs"] = np.array(
            [[kw, vw, par * 16, (1 - par) * 16, nonce1, nonce2, 0, 0]], np.int32)
        in_maps.append(m)
    return in_maps


class Plan:
    def __init__(self):
        self.seq = []
        self.cnt = defaultdict(int)

    def add(self, eng, fn, waits=(), inc=None):
        incs = []
        if inc is not None:
            if isinstance(inc, tuple):
                inc = [inc]
            for s, n in inc:
                self.cnt[s] += n
                incs.append((s, n))
        for s, v in waits:
            assert v <= self.cnt[s], f"wait on future: {s} {v} > {self.cnt[s]}"
        self.seq.append((eng, fn, [(s, v) for s, v in waits if v > 0], incs))

    def at(self, s):
        return self.cnt[s]


def _chunks(total, step):
    out, c0 = [], 0
    while c0 < total:
        out.append((c0, min(step, total - c0)))
        c0 += step
    return out


def build_program():
    nc = bass.Bass("TRN2", target_bir_lowering=False)

    d_x = nc.dram_tensor("xT", [D, TOK], F32, kind="ExternalInput").ap()
    d_bm = nc.dram_tensor("bmask", [128, NT * 256], F32, kind="ExternalInput").ap()
    d_cq = nc.dram_tensor("cq", [128, TOK], F32, kind="ExternalInput").ap()
    d_sq = nc.dram_tensor("sq", [128, TOK], F32, kind="ExternalInput").ap()
    d_ck = nc.dram_tensor("ck", [128, TOK], F32, kind="ExternalInput").ap()
    d_sk = nc.dram_tensor("sk", [128, TOK], F32, kind="ExternalInput").ap()
    d_offs = nc.dram_tensor("offs", [1, 8], I32, kind="ExternalInput").ap()
    d_wq = nc.dram_tensor("wT_qkv", [D, QKV_D], F32, kind="ExternalInput").ap()
    d_bq = nc.dram_tensor("b_qkv", [128, QCH], F32, kind="ExternalInput").ap()
    d_wf = nc.dram_tensor("wT_ffn", [D, D], F32, kind="ExternalInput").ap()
    d_bfv = nc.dram_tensor("b_ffn", [128, NCH], F32, kind="ExternalInput").ap()
    d_dT = nc.dram_tensor("dT", [D, D], F32, kind="ExternalInput").ap()
    d_aT = nc.dram_tensor("aT", [D, ER], F32, kind="ExternalInput").ap()
    d_bS = nc.dram_tensor("bS", [ER, D], F32, kind="ExternalInput").ap()
    d_coef = nc.dram_tensor("coefr", [128, 2], F32, kind="ExternalInput").ap()
    d_scal = nc.dram_tensor("scalr", [128, 2], F32, kind="ExternalInput").ap()
    d_id = nc.dram_tensor("ident", [128, 128], F32, kind="ExternalInput").ap()
    d_pm = nc.dram_tensor("permm", [128, 128], F32, kind="ExternalInput").ap()
    d_oc = nc.dram_tensor("ones_col", [128, 1], F32, kind="ExternalInput").ap()
    d_or = nc.dram_tensor("ones_row", [1, 128], F32, kind="ExternalInput").ap()
    d_out = nc.dram_tensor("outT", [D, TOK], F32, kind="ExternalOutput").ap()

    d_q = nc.dram_tensor("q_hbm", [NCH, 128, TOK], BF16, kind="Internal").ap()
    d_xm = nc.dram_tensor("xmid", [NCH, 128, TOK], F32, kind="Internal").ap()
    d_xa = nc.dram_tensor("xatt", [NCH, 128, TOK], F32, kind="Internal").ap()
    d_ctx = nc.dram_tensor("ctx", [NCH, 128, TOK], BF16, kind="Internal").ap()
    d_shk = nc.dram_tensor("sh_k", [2, NKV, 128, TOK], BF16, kind="Internal",
                           addr_space="Shared").ap()
    d_shv = nc.dram_tensor("sh_v", [2, NT, 128, ER], BF16, kind="Internal",
                           addr_space="Shared").ap()
    d_shf = nc.dram_tensor("sh_flag", [2, 16], I32, kind="Internal",
                           addr_space="Shared").ap()
    d_dbg = nc.dram_tensor("dbg", [128, 16384], F32, kind="Internal").ap()

    cstk = ExitStack()

    def sbt(name, shape, dt=F32):
        return cstk.enter_context(nc.sbuf_tensor(name, shape, dt))

    def pst(name, shape, dt=F32):
        return cstk.enter_context(nc.psum_tensor(name, shape, dt))

    identb = sbt("identb", [128, 128], BF16)
    permb = sbt("permb", [128, 128], BF16)
    onescol = sbt("onescol", [128, 1], BF16)
    onesrow = sbt("onesrow", [1, 128], BF16)
    bq_sb = sbt("bq_sb", [128, QCH])
    bfh_sb = sbt("bfh_sb", [128, NCH])
    wvec = sbt("wvec", [128, 2])
    scal_tmp = sbt("scal_tmp", [128, 2])
    offs_sb = sbt("offs_sb", [1, 8], I32)
    aSb = sbt("aSb", [128, NCH * ER], BF16)
    bSb = sbt("bSb", [128, 2 * D], BF16)
    a1row = sbt("a1row", [1, ER], BF16)
    w1f_row = sbt("w1f_row", [1, D], BF16)
    w1q_row = sbt("w1q_row", [1, QKV_D], BF16)
    negm_row = sbt("negm_row", [1, GT], BF16)
    rhi_row = sbt("rhi_row", [1, GT], BF16)
    rlo_row = sbt("rlo_row", [1, GT], BF16)
    mrow = sbt("mrow", [1, GT])
    vrow = sbt("vrow", [1, GT])
    vtmp = sbt("vtmp", [1, GT])
    rb_sb = sbt("rb_sb", [128, GT])
    flagbuf = sbt("flagbuf", [1, 16], I32)

    pA = pst("pA", [128, 512])
    pB = pst("pB", [128, 512])
    pC = pst("pC", [128, 512])
    pD = pst("pD", [128, 512])
    pE = pst("pE", [128, 512])
    pF = pst("pF", [128, 512])
    pR = pst("pR", [128, 512])
    pT = pst("pT", [128, 1024], BF16)

    ARENA = 135488
    arena = nc.alloc_sbuf_tensor("arena", [128, ARENA], U8)
    abase = nc.lookup_mloc(arena).addr

    def at(name, off, shape, dt=F32):
        assert off % 64 == 0
        n = mybir.dt.size(dt)
        for s in shape[1:]:
            n *= s
        assert off + n <= ARENA, f"{name}: arena overflow {off}+{n}"
        return nc.alloc_sbuf_tensor_at(name, shape, dt, offset=abase + off)

    xbig = at("xbig", 0, [128, NCH * GT], BF16)
    tabs = at("tabs", 65536, [128, 8 * 512])
    wslab = [at(f"wslab{j}", 81920 + j * 8192, [128, 4096], BF16) for j in range(2)]
    xf_t = [at(f"xf{j}", 98304 + j * 2048, [128, 512]) for j in range(4)]
    xm_st = [at(f"xmst{j}", 106496 + j * 2048, [128, 512]) for j in range(2)]
    t1_t = [at(f"t1{j}", 110592 + j * 2048, [128, 512]) for j in range(2)]
    qb_t = [at(f"qb{j}", 114688 + j * 1024, [128, 512], BF16) for j in range(2)]
    rq_t = [at(f"rq{j}", 116736 + j * 1024, [128, 512], BF16) for j in range(2)]
    xab = at("xab", 118784, [128, 4 * 512], BF16)
    sq_t = [at(f"sq{j}", 122880 + j * 1024, [128, 512], BF16) for j in range(2)]
    vst_t = [at(f"vst{j}", 124928 + j * 2048, [128, 4 * 256], BF16)
             for j in range(2)]
    ksb = at("ksb", 0, [128, 2 * NKV * TOK], BF16)
    vsb = at("vsb", 16384, [128, 2 * NT * ER], BF16)
    qt_t = [at(f"qt{j}", 32768 + j * 8192, [128, NCH * 128], BF16) for j in range(2)]
    srow_t = [at(f"srow{j}", 49152 + j * 16384, [128, 4096])
              for j in range(2)]
    p4 = at("p4", 81920, [128, 4 * 4096], BF16)
    ptb = at("ptb", 114688, [128, 16 * 512], BF16)
    mt_t = [at(f"mt{j}", 131072 + j * 1024, [128, 256]) for j in range(2)]
    ctxst = [at(f"ctxst{j}", 133120 + j * 1024, [128, 512], BF16) for j in range(2)]
    sums4 = at("sums4", 135168, [128, 8])
    rs4 = at("rs4", 135232, [128, 8])
    negmx = at("negmx", 135296, [128, 8])
    cmax = at("cmax", 135360, [128, 24])
    ctxb_t = [at(f"ctxb{j}", j * 32768, [128, NCH * 512], BF16) for j in range(2)]

    plan = Plan()
    ms = {}
    sems = {}
    _rt_regs = {}

    def add(eng, fn, waits=(), inc=None):
        plan.add(eng, fn, waits, inc)

    def _rt(eng, key):
        idx = {"kw": 0, "vw": 1, "fw": 2, "fr": 3, "n1": 4, "n2": 5}[key]
        name = f"rt_{key}"
        if name not in _rt_regs:
            r = eng.alloc_register(name)
            eng.reg_load(r, offs_sb[0:1, idx:idx + 1])
            _rt_regs[name] = RuntimeValue(val=r, min_val=0, max_val=1 << 30)
        return _rt_regs[name]

    def barrier(k):
        for e in ENGINES:
            add(e, lambda eng: eng.nop(), inc=("bar", 1))
        tgt = 5 * k
        assert plan.at("bar") == tgt
        for e in ENGINES:
            add(e, lambda eng, t=tgt: None, waits=[("bar", tgt)])

    def preloads():
        for t, dsrc in ((identb, d_id), (permb, d_pm), (onescol, d_oc),
                        (onesrow, d_or)):
            add("gpsimd", lambda g, t=t, d_=dsrc: g.dma_start(out=t[:], in_=d_),
                inc=("dCon", 16))
        add("gpsimd", lambda g: g.dma_start(
            out=aSb[:].rearrange("p (i e) -> p i e", e=ER),
            in_=d_aT.rearrange("(i p) e -> p i e", p=128)), inc=("dCon", 16))
        add("gpsimd", lambda g: g.dma_start(
            out=bSb[:].rearrange("p (j o) -> p j o", o=D),
            in_=d_bS.rearrange("(j p) o -> p j o", p=128)), inc=("dCon", 16))
        wconst = ("dCon", plan.at("dCon"))
        for t, dsrc in ((bq_sb, d_bq), (bfh_sb, d_bfv), (wvec, d_coef),
                        (offs_sb, d_offs), (scal_tmp, d_scal)):
            add("sync", lambda s, t=t, d_=dsrc: s.dma_start(out=t[:], in_=d_),
                inc=("dCn2", 16))
        wx = ("dCn2", plan.at("dCn2"))
        add("sync", lambda s: [_rt(s, k) for k in
                               ("kw", "vw", "fw", "fr", "n1", "n2")] and s.nop(),
            waits=[wx])
        add("vector", lambda v: v.tensor_mul(wvec[:], wvec[:], scal_tmp[:]),
            waits=[wx], inc=("cV", 1))
        add("vector", lambda v: v.tensor_scalar_mul(bfh_sb[:], bfh_sb[:], 0.5),
            inc=("cV", 1))
        wv = ("cV", plan.at("cV"))
        for j in range(2):
            add("vector", lambda v, j=j: v.tensor_scalar(
                bSb[:, j * D:(j + 1) * D], bSb[:, j * D:(j + 1) * D],
                wvec[:, j:j + 1], None, ALU.mult),
                waits=[wv, wconst], inc=("cV", 1))
        for i in range(NCH):
            add("tensor", lambda t, i=i: t.matmul(
                pR[0:1, 0:ER], onescol[:], aSb[:, i * ER:(i + 1) * ER],
                start=(i == 0), stop=(i == NCH - 1)),
                waits=[wconst] if i == 0 else (),
                inc=("cT", 1) if i == NCH - 1 else None)
        wt = ("cT", plan.at("cT"))
        add("scalar", lambda s: s.activation(a1row[:], pR[0:1, 0:ER], AF.Copy,
                                             bias=0.0, scale=1.0),
            waits=[wt], inc=("cS", 1))
        ms["a1_done"] = ("cS", plan.at("cS"))

    def ln_finalize(half, tag):
        wt = ("cT", plan.at("cT"))
        for blk in range(NBLK):
            sl = slice(blk * 512, (blk + 1) * 512)
            sbank, qbank = (pC, pD)[blk], (pE, pF)[blk]
            add("vector", lambda v, sl=sl, b=sbank: v.tensor_scalar_mul(
                mrow[0:1, sl], b[0:1, 0:512], 1.0 / D), waits=[wt], inc=("cV", 1))
            add("vector", lambda v, sl=sl, b=qbank: v.tensor_scalar_mul(
                vrow[0:1, sl], b[0:1, 0:512], 1.0 / D), waits=[wt], inc=("cV", 1))
        add("vector", lambda v: v.tensor_mul(vtmp[0:1, :], mrow[0:1, :],
                                             mrow[0:1, :]), inc=("cV", 1))
        add("vector", lambda v: v.tensor_sub(vrow[0:1, :], vrow[0:1, :],
                                             vtmp[0:1, :]), inc=("cV", 1))
        add("vector", lambda v: v.tensor_scalar_mul(negm_row[0:1, :],
                                                    mrow[0:1, :], -1.0),
            inc=("cV", 1))
        add("vector", lambda v: v.tensor_scalar_add(vrow[0:1, :], vrow[0:1, :],
                                                    EPS), inc=("cV", 1))
        wv = ("cV", plan.at("cV"))
        add("scalar", lambda s: s.activation(vtmp[0:1, :], vrow[0:1, :], AF.Ln,
                                             bias=0.0, scale=1.0),
            waits=[wv], inc=("cS", 1))
        add("scalar", lambda s: s.activation(
            vrow[0:1, :], vtmp[0:1, :], AF.Exp, bias=0.0, scale=-0.5),
            inc=("cS", 1))
        wsc = ("cS", plan.at("cS"))
        if half:
            add("vector", lambda v: v.tensor_scalar_mul(vrow[0:1, :],
                                                        vrow[0:1, :], 0.5),
                waits=[wsc], inc=("cV", 1))
            wsc = ("cV", plan.at("cV"))
        add("vector", lambda v: v.tensor_copy(rhi_row[0:1, :], vrow[0:1, :]),
            waits=[wsc], inc=("cV", 1))
        add("vector", lambda v: v.tensor_sub(rlo_row[0:1, :], vrow[0:1, :],
                                             rhi_row[0:1, :]), inc=("cV", 1))
        wv = ("cV", plan.at("cV"))
        for blk in range(NBLK):
            sl = slice(blk * 512, (blk + 1) * 512)
            add("tensor", lambda t, sl=sl: t.matmul(
                pE[:], onesrow[:], rhi_row[0:1, sl], start=True, stop=False),
                waits=[wv, ("cS", plan.at("cS"))])
            add("tensor", lambda t, sl=sl: t.matmul(
                pE[:], onesrow[:], rlo_row[0:1, sl], start=False, stop=True),
                inc=("cT", 1))
            wt2 = ("cT", plan.at("cT"))
            add("scalar", lambda s, sl=sl: s.activation(
                rb_sb[:, sl], pE[:], AF.Copy, bias=0.0, scale=1.0),
                waits=[wt2, wv], inc=("cS", 1))
        ms[f"rdy_{tag}"] = ("cS", plan.at("cS"))

    stats_pend = []

    def stats_flush(keep=0):
        while len(stats_pend) > keep:
            tagu, ent = stats_pend.pop(0)
            for eng, fn, waits, inc in ent:
                add(eng, fn, waits=waits, inc=inc)
            ms[tagu] = ("cT", plan.at("cT"))

    def stats_ops(u, oo, blk, src_f32, first, last, tag, wsrc):
        stats_flush(keep=1)
        cb = qb_t[u % 2]
        wprev = [wsrc] + ([ms[(tag, "stq", u - 2)]] if u >= 2 else [])
        add("vector", lambda v, xm=src_f32, cb=cb: v.tensor_copy(cb[:], xm[:]),
            waits=wprev, inc=("cV", 1))
        wcb = ("cV", plan.at("cV"))
        sqt = sq_t[u % 2]
        add("scalar", lambda s, cb=cb, sqt=sqt: s.activation(
            sqt[:], cb[:], AF.Square, bias=0.0, scale=1.0),
            waits=[wcb], inc=("cS", 1))
        wsq = ("cS", plan.at("cS"))
        sbank, qbank = (pC, pD)[blk], (pE, pF)[blk]
        sw = [wcb]
        if first and ("stats_free" in ms):
            sw.append(ms["stats_free"])
        ent = [
            ("tensor", (lambda t, cb=cb, b=sbank, first=first, last=last:
                        t.matmul(b[0:1, 0:512], onescol[:], cb[:],
                                 start=first, stop=last)), sw, None),
            ("tensor", (lambda t, sqt=sqt, b=qbank, first=first, last=last:
                        t.matmul(b[0:1, 0:512], onescol[:], sqt[:],
                                 start=first, stop=last)), [wsq], ("cT", 1)),
        ]
        stats_pend.append(((tag, "stq", u), ent))

    def xa_adapters(tag, wsrc):
        for blk in range(NBLK):
            for j in range(2):
                bank = (pC, pD)[j]
                for i in range(NCH):
                    w0 = []
                    if i == 0:
                        w0 = [ms[f"rdy_{tag}"], wsrc]
                        if blk == 0 and j == 0 and "stats_free" in ms:
                            w0.append(ms["stats_free"])
                    add("tensor", lambda t, b=bank, i=i, j=j, blk=blk: t.matmul(
                        b[:], aSb[:, i * ER + j * 128: i * ER + (j + 1) * 128],
                        xbig[:, i * GT + blk * 512: i * GT + blk * 512 + 512],
                        start=(i == 0), stop=False), waits=w0)
                add("tensor", lambda t, b=bank, j=j, blk=blk: t.matmul(
                    b[:], a1row[0:1, j * 128:(j + 1) * 128],
                    negm_row[0:1, blk * 512: blk * 512 + 512],
                    start=False, stop=True), inc=("cT", 1))
                wt = ("cT", plan.at("cT"))
                add("scalar", lambda s, b=bank, j=j, blk=blk: s.activation(
                    xab[:, (blk * 2 + j) * 512:(blk * 2 + j + 1) * 512], b[:],
                    AF.Copy, bias=0.0, scale=1.0), waits=[wt], inc=("cS", 1))
        ms["xa_rdy"] = ("cS", plan.at("cS"))
        ms["stats_free"] = ms["xa_rdy"]

    def phase1_group(g, stop_after=None):
        gcol = g * GT
        waits = [ms["xbig_free"]] if g > 0 else []
        prev_ev = ("cV", plan.at("cV")) if g > 0 else None
        add("gpsimd", lambda gp: gp.dma_start(
            out=xbig[:].rearrange("p (i t) -> p i t", t=GT),
            in_=d_x[:, gcol:gcol + GT].rearrange("(i p) t -> p i t", p=128)),
            waits=waits, inc=("dXB", 16))
        wxb = ("dXB", plan.at("dXB"))
        tw = [ms["tabs_free"]] if g > 0 else []
        for j, dsrc in enumerate((d_cq, d_sq, d_ck, d_sk)):
            add("sync", lambda s, j=j, d_=dsrc: s.dma_start(
                out=tabs[:, j * 1024:(j + 1) * 1024], in_=d_[:, gcol:gcol + GT]),
                waits=tw, inc=("dTab", 16))
        wtab = ("dTab", plan.at("dTab"))

        for ci in range(NCH):
            for blk in range(NBLK):
                u = ci * NBLK + blk
                xsl = xbig[:, ci * GT + blk * 512: ci * GT + blk * 512 + 512]
                sqt = sq_t[u % 2]
                swq = [wxb] if u < 2 else [ms[("l1", "stq", u - 2)]]
                add("scalar", lambda s, xsl=xsl, sqt=sqt: s.activation(
                    sqt[:], xsl, AF.Square, bias=0.0, scale=1.0),
                    waits=swq, inc=("cS", 1))
                wsq = ("cS", plan.at("cS"))
                sbank, qbank = (pC, pD)[blk], (pE, pF)[blk]
                sw = [wxb]
                if ci == 0 and "stats_free" in ms:
                    sw.append(ms["stats_free"])
                add("tensor", lambda t, xsl=xsl, b=sbank, ci=ci: t.matmul(
                    b[0:1, 0:512], onescol[:], xsl,
                    start=(ci == 0), stop=(ci == NCH - 1)),
                    waits=sw if ci == 0 else ())
                add("tensor", lambda t, sqt=sqt, b=qbank, ci=ci: t.matmul(
                    b[0:1, 0:512], onescol[:], sqt[:],
                    start=(ci == 0), stop=(ci == NCH - 1)),
                    waits=[wsq], inc=("cT", 1))
                ms[("l1", "stq", u)] = ("cT", plan.at("cT"))
        ln_finalize(half=True, tag=f"ln1_{g}")
        ms["stats_free"] = ms[f"rdy_ln1_{g}"]
        if stop_after == "ln1":
            return
        xa_adapters(f"ln1_{g}", wxb)
        if stop_after == "xa":
            return

        def slab_load(mat, o, wprev):
            sem = ("dSlA", "dSlB")[o % 2]
            src = mat[:, o * 128:(o + 1) * 128].rearrange("(i p) c -> p i c", p=128)
            add("gpsimd", lambda gp, o=o, src=src: gp.dma_start(
                out=wslab[o % 2][:].rearrange("p (i c) -> p i c", c=128),
                in_=src), waits=wprev, inc=(sem, 16))
            return (sem, plan.at(sem))

        slw = {0: slab_load(d_wf, 0, []), 1: slab_load(d_wf, 1, [])}
        for oo in range(NCH):
            slab = wslab[oo % 2]
            if g == 0:
                for i in range(NCH):
                    w0 = [slw[oo]]
                    if oo >= 1:
                        w0.append(ms["w1ev"])
                    else:
                        w0.append(ms["a1_done"])
                    add("tensor", lambda t, slab=slab, i=i: t.matmul(
                        pR[0:1, 0:128], onescol[:], slab[:, i * 128:(i + 1) * 128],
                        start=(i == 0), stop=(i == NCH - 1)),
                        waits=w0 if i == 0 else (),
                        inc=("cT", 1) if i == NCH - 1 else None)
                wt = ("cT", plan.at("cT"))
                add("scalar", lambda s, oo=oo: s.activation(
                    w1f_row[0:1, oo * 128:(oo + 1) * 128], pR[0:1, 0:128],
                    AF.Copy, bias=0.0, scale=1.0), waits=[wt], inc=("cS", 1))
                ms["w1ev"] = ("cS", plan.at("cS"))
            ww1 = ms["w1ev"] if g == 0 else None
            for blk in range(NBLK):
                u = oo * NBLK + blk
                bank = (pA, pB)[u % 2]
                xf = xf_t[u % 4]
                xsem = f"dXf{u % 4}"
                xw = [ms[("p1", "res", u - 4)]] if u >= 4 else []
                add("sync", lambda s, oo=oo, blk=blk, xf=xf: s.dma_start(
                    out=xf[:], in_=d_x[oo * 128:(oo + 1) * 128,
                                       gcol + blk * 512: gcol + blk * 512 + 512]),
                    waits=xw, inc=(xsem, 16))
                wxf = (xsem, plan.at(xsem))
                mw = [slw[oo], ms["xa_rdy"]]
                if u >= 2:
                    mw.append(ms[("p1", "t1", u - 2)])
                elif prev_ev is not None:
                    mw.append(prev_ev)
                for i in range(NCH):
                    add("tensor", lambda t, slab=slab, i=i, blk=blk, b=bank: t.matmul(
                        b[:], slab[:, i * 128:(i + 1) * 128],
                        xbig[:, i * GT + blk * 512: i * GT + blk * 512 + 512],
                        start=(i == 0), stop=False), waits=mw if i == 0 else ())
                add("tensor", lambda t, oo=oo, blk=blk, b=bank: t.matmul(
                    b[:], w1f_row[0:1, oo * 128:(oo + 1) * 128],
                    negm_row[0:1, blk * 512: blk * 512 + 512],
                    start=False, stop=False),
                    waits=[ww1] if (g == 0 and blk == 0) else ())
                for j in range(2):
                    add("tensor", lambda t, oo=oo, blk=blk, j=j, b=bank: t.matmul(
                        b[:], bSb[:, j * D + oo * 128: j * D + (oo + 1) * 128],
                        xab[:, (blk * 2 + j) * 512:(blk * 2 + j + 1) * 512],
                        start=False, stop=(j == 1)),
                        inc=("cT", 1) if j == 1 else None)
                wmm = ("cT", plan.at("cT"))
                if blk == NBLK - 1:
                    ms[("p1", "slabdone", oo)] = wmm
                t1 = t1_t[u % 2]
                xm = xm_st[u % 2]
                wt1 = [wmm]
                if u >= 2:
                    wt1.append(ms[("p1", "xmio", u - 2)])
                add("vector", lambda v, b=bank, blk=blk, t1=t1: v.tensor_tensor(
                    t1[:], b[:], rb_sb[:, blk * 512: blk * 512 + 512],
                    op=ALU.mult), waits=wt1, inc=("cV", 1))
                ms[("p1", "t1", u)] = ("cV", plan.at("cV"))
                add("vector", lambda v, t1=t1, xm=xm, oo=oo, xf=xf:
                    v.scalar_tensor_tensor(xm[:], t1[:], bfh_sb[:, oo:oo + 1],
                                           xf[:], op0=ALU.add, op1=ALU.add),
                    waits=[wxf, ("dXm", plan.at("dXm"))], inc=("cV", 1))
                wev = ("cV", plan.at("cV"))
                add("sync", lambda s, oo=oo, blk=blk, xm=xm: s.dma_start(
                    out=d_xm[oo, :, gcol + blk * 512: gcol + blk * 512 + 512],
                    in_=xm[:]), waits=[wev], inc=("dXm", 16))
                ms[("p1", "res", u)] = ("dXm", plan.at("dXm"))
                stats_ops(u, oo, blk, xm, first=(oo == 0), last=(oo == NCH - 1),
                          tag="l2", wsrc=wev)
                ms[("p1", "xmio", u)] = ("cV", plan.at("cV"))
            if oo + 2 < NCH:
                slw[oo + 2] = slab_load(d_wf, oo + 2, [ms[("p1", "slabdone", oo)]])
        stats_flush()
        ms["ffn_done"] = ("cT", plan.at("cT"))
        ms["xmid_stored"] = ("dXm", plan.at("dXm"))
        if stop_after == "ffn":
            return
        ln_finalize(half=False, tag=f"ln2_{g}")
        if stop_after == "ln2":
            return
        ms["stats_free"] = ms[f"rdy_ln2_{g}"]

        add("gpsimd", lambda gp: gp.dma_start(
            out=xbig[:].rearrange("p (i t) -> p i t", t=GT),
            in_=d_xm[:, :, gcol:gcol + GT].rearrange("i p t -> p i t")),
            waits=[ms["ffn_done"], ms["xmid_stored"]], inc=("dXB", 16))
        wxm = ("dXB", plan.at("dXB"))

        def qslab_load(o, wprev):
            sem = ("dSlA", "dSlB")[o % 2]
            src = d_wq[:, o * 128:(o + 1) * 128].rearrange("(i p) c -> p i c", p=128)
            add("gpsimd", lambda gp, o=o, src=src: gp.dma_start(
                out=wslab[o % 2][:].rearrange("p (i c) -> p i c", c=128),
                in_=src), waits=wprev, inc=(sem, 16))
            return (sem, plan.at(sem))

        qlw = {0: qslab_load(0, []), 1: qslab_load(1, [])}
        for oo in range(QCH):
            slab = wslab[oo % 2]
            if g == 0:
                for i in range(NCH):
                    w0 = [qlw[oo], ms["w1ev"]]
                    add("tensor", lambda t, slab=slab, i=i: t.matmul(
                        pR[0:1, 0:128], onescol[:], slab[:, i * 128:(i + 1) * 128],
                        start=(i == 0), stop=(i == NCH - 1)),
                        waits=w0 if i == 0 else (),
                        inc=("cT", 1) if i == NCH - 1 else None)
                wt = ("cT", plan.at("cT"))
                add("scalar", lambda s, oo=oo: s.activation(
                    w1q_row[0:1, oo * 128:(oo + 1) * 128], pR[0:1, 0:128],
                    AF.Copy, bias=0.0, scale=1.0), waits=[wt], inc=("cS", 1))
                ms["w1ev"] = ("cS", plan.at("cS"))
            for blk in range(NBLK):
                u = oo * NBLK + blk
                bank = (pA, pB)[u % 2]
                mw = [qlw[oo], wxm, ms[f"rdy_ln2_{g}"]]
                if u >= 2:
                    mw.append(ms[("qk", "ev", u - 2)])
                for i in range(NCH):
                    add("tensor", lambda t, slab=slab, i=i, blk=blk, b=bank: t.matmul(
                        b[:], slab[:, i * 128:(i + 1) * 128],
                        xbig[:, i * GT + blk * 512: i * GT + blk * 512 + 512],
                        start=(i == 0), stop=False), waits=mw if i == 0 else ())
                add("tensor", lambda t, oo=oo, blk=blk, b=bank: t.matmul(
                    b[:], w1q_row[0:1, oo * 128:(oo + 1) * 128],
                    negm_row[0:1, blk * 512: blk * 512 + 512],
                    start=False, stop=True),
                    waits=[ms["w1ev"]] if (g == 0 and blk == 0) else (),
                    inc=("cT", 1))
                wmm = ("cT", plan.at("cT"))
                if blk == NBLK - 1:
                    ms[("qk", "slabdone", oo)] = wmm
                t1 = t1_t[u % 2]
                qb = qb_t[u % 2]
                wt1 = [wmm]
                if u >= 2:
                    wt1.append(ms[("qk", "cons", u - 2)])
                add("vector", lambda v, b=bank, blk=blk, t1=t1: v.tensor_tensor(
                    t1[:], b[:], rb_sb[:, blk * 512: blk * 512 + 512],
                    op=ALU.mult), waits=wt1, inc=("cV", 1))
                add("vector", lambda v, t1=t1, qb=qb, oo=oo: v.tensor_scalar(
                    qb[:], t1[:], bq_sb[:, oo:oo + 1], None, ALU.add),
                    inc=("cV", 1))
                ms[("qk", "ev", u)] = ("cV", plan.at("cV"))
                wqb = ms[("qk", "ev", u)]
                if oo < 34:
                    is_q = oo < 32
                    cbase = (0 if is_q else 2) * 1024 + blk * 512
                    sbase = (1 if is_q else 3) * 1024 + blk * 512
                    bankp = (pC, pD)[u % 2]
                    wpp = [wqb]
                    if u >= 2:
                        wpp.append(ms[("qk", "permev", u - 2)])
                    add("tensor", lambda t, qb=qb, bp=bankp: t.matmul(
                        bp[:], permb[:], qb[:], start=True, stop=True),
                        waits=wpp, inc=("cT", 1))
                    wpm = ("cT", plan.at("cT"))
                    t1b = t1_t[u % 2]
                    add("vector", lambda v, bp=bankp, sbase=sbase, t1b=t1b:
                        v.tensor_tensor(t1b[:], bp[:],
                                        tabs[:, sbase:sbase + 512], op=ALU.mult),
                        waits=[wpm, wtab], inc=("cV", 1))
                    ms[("qk", "permev", u)] = ("cV", plan.at("cV"))
                    xm2 = xm_st[u % 2]
                    add("vector", lambda v, qb=qb, cbase=cbase, xm2=xm2:
                        v.tensor_tensor(xm2[:], qb[:],
                                        tabs[:, cbase:cbase + 512], op=ALU.mult),
                        inc=("cV", 1))
                    rq = rq_t[u % 2]
                    wrq_prev = [ms[("qk", "rqio", u - 2)]] if u >= 2 else []
                    add("vector", lambda v, t1b=t1b, xm2=xm2, rq=rq: v.tensor_add(
                        rq[:], t1b[:], xm2[:]), waits=wrq_prev, inc=("cV", 1))
                    wrq = ("cV", plan.at("cV"))
                    if is_q:
                        add("sync", lambda s, oo=oo, blk=blk, rq=rq: s.dma_start(
                            out=d_q[oo, :, gcol + blk * 512: gcol + blk * 512 + 512],
                            in_=rq[:]), waits=[wrq], inc=("dQ", 16))
                        ms[("qk", "rqio", u)] = ("dQ", plan.at("dQ"))
                    else:
                        kvh = oo - 32
                        coff = kvh * (128 * TOK) + gcol + blk * 512
                        add("sync", lambda s, rq=rq, coff=coff: s.dma_start(
                            out=dataclasses.replace(
                                d_shk[0, 0, :, 0:512], offset=_rt(s, "kw") + coff),
                            in_=rq[:]), waits=[wrq], inc=("dKw", 16))
                        ms[("qk", "rqio", u)] = ("dKw", plan.at("dKw"))
                    ms[("qk", "cons", u)] = ("cV", plan.at("cV"))
                else:
                    vh = oo - 34
                    wv0 = [wqb]
                    if ("qk", "vstlast") in ms:
                        wv0.append(ms[("qk", "vstlast")])
                        wv0.append(("dV", plan.at("dV")))
                    vstb = vst_t[blk]
                    for pt in range(4):
                        add("tensor", lambda t, qb=qb, pt=pt: t.transpose(
                            pT[:, pt * 128:(pt + 1) * 128],
                            qb[:, pt * 128:(pt + 1) * 128], identb[:]),
                            waits=wv0 if pt == 0 else (),
                            inc=("cT", 1) if pt == 3 else None)
                    wtr = ("cT", plan.at("cT"))
                    add("scalar", lambda s, vh=vh, vstb=vstb: s.activation(
                        vstb[:].rearrange("p (pt c) -> p pt c", c=ER)[
                            :, :, vh * 128:(vh + 1) * 128],
                        pT[:, 0:512].rearrange("p (pt c) -> p pt c", c=128),
                        AF.Copy, bias=0.0, scale=1.0),
                        waits=[wtr], inc=("cS", 1))
                    wvs = ("cS", plan.at("cS"))
                    ms[("qk", "vstlast")] = wvs
                    if vh == 1:
                        ptg = g * 8 + blk * 4
                        coff = ptg * (128 * ER)
                        add("sync", lambda s, coff=coff, vstb=vstb: s.dma_start(
                            out=dataclasses.replace(
                                d_shv[0, 0:4, :, :].rearrange("l p c -> p l c"),
                                offset=_rt(s, "vw") + coff),
                            in_=vstb[:].rearrange("p (pt c) -> p pt c", c=ER)),
                            waits=[wvs], inc=("dV", 16))
                    ms[("qk", "cons", u)] = ("cS", plan.at("cS"))
            if oo + 2 < QCH:
                qlw[oo + 2] = qslab_load(oo + 2, [ms[("qk", "slabdone", oo)]])
        ms["xbig_free"] = ("cT", plan.at("cT"))
        ms["tabs_free"] = ("cV", plan.at("cV"))
        ms[f"qkv_done_{g}"] = [("dQ", plan.at("dQ")), ("dKw", plan.at("dKw")),
                               ("dV", plan.at("dV"))]

    def flag_and_poll():
        add("sync", lambda s: s.dma_start(
            out=dataclasses.replace(d_shf[0, 0:2], offset=_rt(s, "fw")),
            in_=offs_sb[0:1, 4:6]), waits=ms["qkv_done_1"], inc=("dF", 16))

        def poll(s):
            rflag = s.alloc_register("rflag")
            rcnt = s.alloc_register("rcnt")
            rv_f = RuntimeValue(val=rflag, min_val=0, max_val=(1 << 30) - 1)
            rv_c = RuntimeValue(val=rcnt, min_val=0, max_val=(1 << 30) - 1)
            s.reg_mov(rcnt, 0)
            for widx, nkey in ((0, "n1"), (1, "n2")):
                nv = _rt(s, nkey)
                src = dataclasses.replace(d_shf[0, 0:2], offset=_rt(s, "fr"))
                s.reg_mov(rflag, 0)
                with s.While(lambda: rv_f != nv):
                    s.reg_add(rcnt, rcnt, 16)
                    s.dma_start(out=flagbuf[0:1, 0:2], in_=src).then_inc(
                        sems["pol"], 16)
                    s.wait_ge(sems["pol"], rv_c)
                    s.reg_load(rflag, flagbuf[0:1, widx:widx + 1])
            return s.nop()
        add("sync", poll)

    def phase2(only_tiles=None, only_hg=None, dump=False, only_h4=None):
        for sec in range(2):
            for kvh in range(NKV):
                add("sync", lambda s, sec=sec, kvh=kvh: s.dma_start(
                    out=ksb[:, (sec * NKV + kvh) * TOK:(sec * NKV + kvh + 1) * TOK],
                    in_=d_shk[sec, kvh, :, :]), inc=("dKV", 16))
        add("sync", lambda s: s.dma_start(
            out=vsb[:].rearrange("p (u c) -> p u c", c=ER),
            in_=d_shv.rearrange("h l p c -> p (h l) c")), inc=("dKV", 16))
        wkv = ("dKV", plan.at("dKV"))
        p4_free = None
        last_exp = {0: None, 1: None}
        last_norm = None
        scnt = [0]
        hpar = [0]
        for i in (range(NT) if only_tiles is None else only_tiles):
            seclen = (i + 1) * 128
            qt = qt_t[i % 2]
            qsem = f"dQt{i % 2}"
            qw = [] if i < 2 else [ms[("p2q", i - 2)]]
            add("sync", lambda s, i=i, qt=qt: s.dma_start(
                out=qt[:].rearrange("p (c q) -> p c q", q=128),
                in_=d_q[:, :, i * 128:(i + 1) * 128].rearrange("c p q -> p c q")),
                waits=qw, inc=(qsem, 16))
            wq = (qsem, plan.at(qsem))
            mt = mt_t[i % 2]
            msem = f"dMt{i % 2}"
            mw = [] if i < 2 else [ms[("p2m", i - 2)]]
            add("sync", lambda s, i=i, mt=mt: s.dma_start(
                out=mt[:], in_=d_bm[:, i * 256:(i + 1) * 256]),
                waits=mw, inc=(msem, 16))
            wmt = (msem, plan.at(msem))
            for hg in (range(8) if only_hg is None else only_hg):
                kvh = hg // 4
                otb = (pC, pD)[hg % 2]
                for h4 in (range(4) if only_h4 is None else only_h4):
                    head = hg * 4 + h4
                    par_ = hpar[0] % 2
                    hpar[0] += 1
                    srow = srow_t[par_]
                    col = h4 + 4 * (hg % 2)
                    cm0 = par_ * 12
                    ncm = 0
                    qsl = qt[:, head * 128:(head + 1) * 128]
                    for sec in range(2):
                        kbase = (sec * NKV + kvh) * TOK
                        sb0 = sec * seclen
                        for c0, cw in _chunks(i * 128, 512):
                            nsc = scnt[0]
                            bank = (pA, pB)[nsc % 2]
                            sw = [wkv, wq]
                            if ("p2sev", nsc - 2) in ms:
                                prev = ms[("p2sev", nsc - 2)]
                                sw.extend(prev if isinstance(prev, list) else [prev])
                            add("tensor", lambda t, b=bank, qsl=qsl, kbase=kbase,
                                c0=c0, cw=cw: t.matmul(
                                    b[:, 0:cw], qsl,
                                    ksb[:, kbase + c0:kbase + c0 + cw],
                                    start=True, stop=True),
                                waits=sw, inc=("cT", 1))
                            wsm = ("cT", plan.at("cT"))
                            sev = [wsm]
                            if last_exp[par_] is not None:
                                sev.append(last_exp[par_])
                            if nsc % 4 == 3:
                                add("vector", lambda v, b=bank, sb0=sb0, c0=c0,
                                    cw=cw, srow=srow: v.tensor_copy(
                                        srow[:, sb0 + c0: sb0 + c0 + cw],
                                        b[:, 0:cw]),
                                    waits=sev, inc=("cV", 1))
                                ms[("p2sev", nsc)] = ("cV", plan.at("cV"))
                            else:
                                add("scalar", lambda s, b=bank, sb0=sb0, c0=c0,
                                    cw=cw, srow=srow:
                                    s.activation(srow[:, sb0 + c0: sb0 + c0 + cw],
                                                 b[:, 0:cw], AF.Copy, bias=0.0,
                                                 scale=1.0),
                                    waits=sev, inc=("cS", 1))
                                ms[("p2sev", nsc)] = ("cS", plan.at("cS"))
                            scnt[0] += 1
                        nsc = scnt[0]
                        bank = (pA, pB)[nsc % 2]
                        sw = [wkv, wq, wmt]
                        if ("p2sev", nsc - 2) in ms:
                            prev = ms[("p2sev", nsc - 2)]
                            sw.extend(prev if isinstance(prev, list) else [prev])
                        add("tensor", lambda t, b=bank, qsl=qsl, kbase=kbase,
                            sb1=i * 128: t.matmul(
                                b[:, 0:128], qsl,
                                ksb[:, kbase + sb1:kbase + sb1 + 128],
                                start=True, stop=True),
                            waits=sw, inc=("cT", 1))
                        wsm = ("cT", plan.at("cT"))
                        sev = [wsm]
                        if last_exp[par_] is not None:
                            sev.append(last_exp[par_])
                        add("vector", lambda v, b=bank, sec=sec, sb0=sb0,
                            sb1=i * 128, mt=mt, srow=srow: v.scalar_tensor_tensor(
                                srow[:, sb0 + sb1: sb0 + sb1 + 128], b[:, 0:128],
                                1.0, mt[:, sec * 128:(sec + 1) * 128],
                                op0=ALU.mult, op1=ALU.add),
                            waits=sev, inc=("cV", 1))
                        ms[("p2sev", nsc)] = ("cV", plan.at("cV"))
                        scnt[0] += 1
                    rowlen = 2 * seclen
                    wall = [("cS", plan.at("cS")), ("cV", plan.at("cV"))]
                    if ("p2nmfree", col) in ms:
                        wall.append(ms[("p2nmfree", col)])
                    add("vector", lambda v, col=col, rowlen=rowlen, srow=srow:
                        v.reduce_max(
                            negmx[:, col:col + 1], srow[:, 0:rowlen],
                            axis=AX.X, negate=True),
                        waits=wall, inc=("cV", 1))
                    wmx = ("cV", plan.at("cV"))
                    wex = [wmx, ("cS", plan.at("cS"))]
                    if p4_free is not None:
                        wex.append(p4_free)
                    add("scalar", lambda s, h4=h4, rowlen=rowlen, srow=srow,
                        col=col: s.activation(
                            p4[:, h4 * 4096: h4 * 4096 + rowlen], srow[:, 0:rowlen],
                            AF.Exp, bias=negmx[:, col:col + 1], scale=1.0,
                            accum_out=sums4[:, col:col + 1]),
                        waits=wex, inc=("cS", 1))
                    last_exp[par_] = ("cS", plan.at("cS"))
                    ms[("p2nmfree", col)] = last_exp[par_]
                    add("vector", lambda v, col=col: v.reciprocal(
                        rs4[:, col:col + 1], sums4[:, col:col + 1]),
                        waits=[last_exp[par_]], inc=("cV", 1))
                    wrc = ("cV", plan.at("cV"))
                    add("vector", lambda v, h4=h4, rowlen=rowlen, col=col:
                        v.tensor_scalar(
                            p4[:, h4 * 4096: h4 * 4096 + rowlen],
                            p4[:, h4 * 4096: h4 * 4096 + rowlen],
                            rs4[:, col:col + 1], None, ALU.mult),
                        waits=[wrc], inc=("cV", 1))
                    ms[("p2pn", h4)] = ("cV", plan.at("cV"))
                    last_norm = ms[("p2pn", h4)]
                wpn = [ms[("p2pn", h4)] for h4 in
                       (range(4) if only_h4 is None else only_h4)]
                h4list = list(range(4) if only_h4 is None else only_h4)
                nh = len(h4list)
                nkt = 2 * (i + 1)
                p4f_cand = None
                for sw0 in range(0, nkt, 16):
                    kts = range(sw0, min(sw0 + 16, nkt))
                    for kt in kts:
                        ptbank = (pT[:], pR[:].bitcast(BF16))[kt % 2]
                        for hx, h4 in enumerate(h4list):
                            tw = []
                            if kt == 0 and hx == 0:
                                tw = list(wpn)
                            if kt >= 2 and hx == 0:
                                tw.append(ms[("p2pte", kt - 2)])
                            if kt >= 16 and hx == 0:
                                tw.append(ms[("p2pv", kt - 16)])
                            add("tensor", lambda t, kt=kt, h4=h4, hx=hx,
                                ptbank=ptbank: t.transpose(
                                    ptbank[:, hx * 128:(hx + 1) * 128],
                                    p4[:, h4 * 4096 + kt * 128:
                                       h4 * 4096 + (kt + 1) * 128],
                                    identb[:]),
                                waits=tw, inc=("cT", 1) if hx == nh - 1 else None)
                        wtr = ("cT", plan.at("cT"))
                        slot = kt % 16
                        if kt % 4 == 1:
                            add("vector", lambda v, slot=slot, ptbank=ptbank:
                                v.tensor_copy(
                                    ptb[:, slot * nh * 128:(slot + 1) * nh * 128],
                                    ptbank[:, 0:nh * 128]),
                                waits=[wtr], inc=("cV", 1))
                            ms[("p2pte", kt)] = ("cV", plan.at("cV"))
                        else:
                            add("scalar", lambda s, slot=slot, ptbank=ptbank:
                                s.activation(
                                    ptb[:, slot * nh * 128:(slot + 1) * nh * 128],
                                    ptbank[:, 0:nh * 128],
                                    AF.Copy, bias=0.0, scale=1.0),
                                waits=[wtr], inc=("cS", 1))
                            ms[("p2pte", kt)] = ("cS", plan.at("cS"))
                    p4f_cand = ("cT", plan.at("cT"))
                    for kt in kts:
                        sec = 1 if kt >= (i + 1) else 0
                        lt = kt - (i + 1) * sec
                        vcol = (sec * NT + lt) * ER + kvh * 128
                        slot = kt % 16
                        pw = [ms[("p2pte", kt)]]
                        if hg >= 2 and kt == 0:
                            pw.append(ms[("p2cte", hg - 2)])
                        add("tensor", lambda t, otb=otb, vcol=vcol, slot=slot,
                            kt=kt, last=nkt - 1: t.matmul(
                                otb[:, 0:nh * 128], vsb[:, vcol:vcol + 128],
                                ptb[:, slot * nh * 128:(slot + 1) * nh * 128],
                                start=(kt == 0), stop=(kt == last)),
                            waits=pw,
                            inc=("cT", 1))
                        ms[("p2pv", kt)] = ("cT", plan.at("cT"))
                p4_free = p4f_cand
                wpv = ("cT", plan.at("cT"))
                cst = ctxst[hg % 2]
                wcs = [wpv]
                if hg >= 2:
                    wcs.append(ms[("p2cio", hg - 2)])
                add("scalar", lambda s, otb=otb, cst=cst: s.activation(
                    cst[:], otb[:], AF.Copy, bias=0.0, scale=1.0),
                    waits=wcs, inc=("cS", 1))
                ms[("p2cte", hg)] = ("cS", plan.at("cS"))
                add("sync", lambda s, hg=hg, i=i, cst=cst: s.dma_start(
                    out=d_ctx[hg * 4:(hg + 1) * 4, :, i * 128:(i + 1) * 128]
                    .rearrange("h p q -> p h q"),
                    in_=cst[:].rearrange("p (h q) -> p h q", q=128)),
                    waits=[ms[("p2cte", hg)]], inc=("dCtx", 16))
                ms[("p2cio", hg)] = ("dCtx", plan.at("dCtx"))
            ms[("p2q", i)] = ("cT", plan.at("cT"))
            ms[("p2m", i)] = ("cV", plan.at("cV"))
            if dump:
                rl = 2 * (i + 1) * 128
                wd = [("cS", plan.at("cS")), ("cV", plan.at("cV")),
                      ("cG", plan.at("cG")), ("cT", plan.at("cT"))]
                add("gpsimd", lambda g, rl=rl: g.dma_start(
                    out=d_dbg[:, 0:rl], in_=srow_t[0][:, 0:rl]), waits=wd,
                    inc=("dDbg", 16))
                add("gpsimd", lambda g, rl=rl: g.dma_start(
                    out=d_dbg[:, 4096:4096 + rl], in_=p4[:, 0:rl]), waits=wd,
                    inc=("dDbg", 16))
                _pw = 2 * (i + 1) * (1 if only_h4 else 4) * 128
                add("gpsimd", lambda g, _pw=_pw: g.dma_start(
                    out=d_dbg[:, 8192:8192 + _pw], in_=ptb[:, 0:_pw]),
                    waits=wd, inc=("dDbg", 16))
                add("gpsimd", lambda g: g.dma_start(
                    out=d_dbg[:, 12288:12289], in_=sums4[:, 0:1],
                    allow_slow_non_contiguous=True), waits=wd,
                    inc=("dDbg", 16))
                add("gpsimd", lambda g: g.dma_start(
                    out=d_dbg[:, 12292:12293], in_=negmx[:, 0:1],
                    allow_slow_non_contiguous=True), waits=wd,
                    inc=("dDbg", 16))
                add("gpsimd", lambda g, i=i: g.dma_start(
                    out=d_dbg[:, 12296:12552], in_=qt_t[i % 2][:, 0:256]),
                    waits=wd, inc=("dDbg", 16))
                add("gpsimd", lambda g, i=i: g.dma_start(
                    out=d_dbg[:, 12552:12808], in_=mt_t[i % 2][:]), waits=wd,
                    inc=("dDbg", 16))
        ms["ctx_stored"] = ("dCtx", plan.at("dCtx"))

    def phase3_group(g):
        gcol = g * GT
        prev_ev3 = ("cV", plan.at("cV")) if g > 0 else None
        for blk in range(NBLK):
            cb = ctxb_t[blk]
            waits = [ms["ctx_stored"]]
            if g > 0:
                waits.append(ms["ffn2_done_0"])
            add("sync", lambda s, blk=blk, cb=cb: s.dma_start(
                out=cb[:].rearrange("p (c q) -> p c q", q=512),
                in_=d_ctx[:, :, gcol + blk * 512: gcol + blk * 512 + 512]
                .rearrange("c p q -> p c q")),
                waits=waits, inc=(f"dCb{blk}", 16))
        wcb = [("dCb0", plan.at("dCb0")), ("dCb1", plan.at("dCb1"))]

        def dslab_load(o, wprev):
            sem = ("dSlA", "dSlB")[o % 2]
            src = d_dT[:, o * 128:(o + 1) * 128].rearrange("(c p) k -> p c k", p=128)
            add("gpsimd", lambda gp, o=o, src=src: gp.dma_start(
                out=wslab[o % 2][:].rearrange("p (c k) -> p c k", k=128),
                in_=src), waits=wprev, inc=(sem, 16))
            return (sem, plan.at(sem))

        dlw = {0: dslab_load(0, [] if g == 0 else [ms["ffn2_done_0"]]),
               1: dslab_load(1, [])}
        for oo in range(NCH):
            slab = wslab[oo % 2]
            for blk in range(NBLK):
                u = oo * NBLK + blk
                bank = (pA, pB)[u % 2]
                xf = xf_t[u % 4]
                xsem = f"dXf{u % 4}"
                xw = [ms[("d3", "res", u - 4)]] if u >= 4 else []
                add("sync", lambda s, oo=oo, blk=blk, xf=xf: s.dma_start(
                    out=xf[:],
                    in_=d_xm[oo, :, gcol + blk * 512: gcol + blk * 512 + 512]),
                    waits=xw, inc=(xsem, 16))
                wxf = (xsem, plan.at(xsem))
                mw = [dlw[oo]] + wcb
                if u >= 2:
                    mw.append(ms[("d3", "t1", u - 2)])
                elif prev_ev3 is not None:
                    mw.append(prev_ev3)
                if u == 0 and "stats_free" in ms:
                    mw.append(ms["stats_free"])
                cb = ctxb_t[blk]
                for c in range(NCH):
                    add("tensor", lambda t, slab=slab, c=c, cb=cb, b=bank: t.matmul(
                        b[:], slab[:, c * 128:(c + 1) * 128],
                        cb[:, c * 512:(c + 1) * 512],
                        start=(c == 0), stop=(c == NCH - 1)),
                        waits=mw if c == 0 else (),
                        inc=("cT", 1) if c == NCH - 1 else None)
                wmm = ("cT", plan.at("cT"))
                if blk == NBLK - 1:
                    ms[("d3", "slabdone", oo)] = wmm
                xm = xm_st[u % 2]
                wt1 = [wmm, wxf, ("dXa", plan.at("dXa"))]
                if u >= 2:
                    wt1.append(ms[("d3", "xmio", u - 2)])
                add("vector", lambda v, b=bank, xf=xf, xm=xm: v.tensor_add(
                    xm[:], b[:], xf[:]), waits=wt1, inc=("cV", 1))
                ms[("d3", "t1", u)] = ("cV", plan.at("cV"))
                wev = ms[("d3", "t1", u)]
                add("sync", lambda s, oo=oo, blk=blk, xm=xm: s.dma_start(
                    out=d_xa[oo, :, gcol + blk * 512: gcol + blk * 512 + 512],
                    in_=xm[:]), waits=[wev], inc=("dXa", 16))
                ms[("d3", "res", u)] = ("dXa", plan.at("dXa"))
                stats_ops(u, oo, blk, xm, first=(oo == 0), last=(oo == NCH - 1),
                          tag="l3", wsrc=wev)
                ms[("d3", "xmio", u)] = ("cV", plan.at("cV"))
            if oo + 2 < NCH:
                dlw[oo + 2] = dslab_load(oo + 2, [ms[("d3", "slabdone", oo)]])
        ms[("d3", "slabdone_last")] = ms[("d3", "slabdone", NCH - 1)]
        stats_flush()
        ms[f"dense_done_{g}"] = ("cT", plan.at("cT"))
        ms["xatt_stored"] = ("dXa", plan.at("dXa"))
        ln_finalize(half=True, tag=f"ln3_{g}")
        ms["stats_free"] = ms[f"rdy_ln3_{g}"]

        add("gpsimd", lambda gp: gp.dma_start(
            out=xbig[:].rearrange("p (i t) -> p i t", t=GT),
            in_=d_xa[:, :, gcol:gcol + GT].rearrange("i p t -> p i t")),
            waits=[ms[f"dense_done_{g}"], ms["xatt_stored"]], inc=("dXB", 16))
        wxb = ("dXB", plan.at("dXB"))
        xa_adapters(f"ln3_{g}", wxb)

        def fslab_load(o, wprev):
            sem = ("dSlA", "dSlB")[o % 2]
            src = d_wf[:, o * 128:(o + 1) * 128].rearrange("(i p) c -> p i c", p=128)
            add("gpsimd", lambda gp, o=o, src=src: gp.dma_start(
                out=wslab[o % 2][:].rearrange("p (i c) -> p i c", c=128),
                in_=src), waits=wprev, inc=(sem, 16))
            return (sem, plan.at(sem))

        flw = {0: fslab_load(0, [ms[("d3", "slabdone_last")]]),
               1: fslab_load(1, [])}
        for oo in range(NCH):
            slab = wslab[oo % 2]
            for blk in range(NBLK):
                u = oo * NBLK + blk
                bank = (pA, pB)[u % 2]
                xf = xf_t[u % 4]
                xsem = f"dXf{u % 4}"
                xw = [ms[("f2", "out", u - 4)]] if u >= 4 else [ms["xatt_stored"]]
                add("sync", lambda s, oo=oo, blk=blk, xf=xf: s.dma_start(
                    out=xf[:],
                    in_=d_xa[oo, :, gcol + blk * 512: gcol + blk * 512 + 512]),
                    waits=xw, inc=(xsem, 16))
                wxf = (xsem, plan.at(xsem))
                mw = [flw[oo], wxb, ms["xa_rdy"]]
                if u >= 2:
                    mw.append(ms[("f2", "t1", u - 2)])
                for i in range(NCH):
                    add("tensor", lambda t, slab=slab, i=i, blk=blk, b=bank: t.matmul(
                        b[:], slab[:, i * 128:(i + 1) * 128],
                        xbig[:, i * GT + blk * 512: i * GT + blk * 512 + 512],
                        start=(i == 0), stop=False), waits=mw if i == 0 else ())
                add("tensor", lambda t, oo=oo, blk=blk, b=bank: t.matmul(
                    b[:], w1f_row[0:1, oo * 128:(oo + 1) * 128],
                    negm_row[0:1, blk * 512: blk * 512 + 512],
                    start=False, stop=False))
                for j in range(2):
                    add("tensor", lambda t, oo=oo, blk=blk, j=j, b=bank: t.matmul(
                        b[:], bSb[:, j * D + oo * 128: j * D + (oo + 1) * 128],
                        xab[:, (blk * 2 + j) * 512:(blk * 2 + j + 1) * 512],
                        start=False, stop=(j == 1)),
                        inc=("cT", 1) if j == 1 else None)
                wmm = ("cT", plan.at("cT"))
                if blk == NBLK - 1:
                    ms[("f2", "slabdone", oo)] = wmm
                t1 = t1_t[u % 2]
                xm = xm_st[u % 2]
                wt1 = [wmm]
                if u >= 2:
                    wt1.append(ms[("f2", "out", u - 2)])
                add("vector", lambda v, b=bank, blk=blk, t1=t1: v.tensor_tensor(
                    t1[:], b[:], rb_sb[:, blk * 512: blk * 512 + 512],
                    op=ALU.mult), waits=wt1, inc=("cV", 1))
                ms[("f2", "t1", u)] = ("cV", plan.at("cV"))
                add("vector", lambda v, t1=t1, xm=xm, oo=oo, xf=xf:
                    v.scalar_tensor_tensor(xm[:], t1[:], bfh_sb[:, oo:oo + 1],
                                           xf[:], op0=ALU.add, op1=ALU.add),
                    waits=[wxf, ("dSt", plan.at("dSt"))], inc=("cV", 1))
                wev = ("cV", plan.at("cV"))
                add("sync", lambda s, oo=oo, blk=blk, xm=xm: s.dma_start(
                    out=d_out[oo * 128:(oo + 1) * 128,
                              gcol + blk * 512: gcol + blk * 512 + 512],
                    in_=xm[:]), waits=[wev], inc=("dSt", 16))
                ms[("f2", "out", u)] = ("dSt", plan.at("dSt"))
            if oo + 2 < NCH:
                flw[oo + 2] = fslab_load(oo + 2, [ms[("f2", "slabdone", oo)]])
        ms[f"ffn2_done_{g}"] = ("cT", plan.at("cT"))

    nb = [0]
    preloads()
    if _DEBUG == "pre":
        add("sync", lambda s: s.dma_start(
            out=d_out[0:128, 0:QCH], in_=bq_sb[:]),
            waits=[ms["a1_done"]], inc=("dSt", 16))
    elif _DEBUG.startswith("grp0"):
        stop = _DEBUG[5:] or None
        phase1_group(0, stop_after=stop)
        if stop in ("ffn", "ln2"):
            add("sync", lambda s: s.dma_start(
                out=d_out[:, 0:GT].rearrange("(c p) t -> c p t", p=128),
                in_=d_xm[:, :, 0:GT]),
                waits=[ms["xmid_stored"]], inc=("dSt", 16))
        elif stop in ("ln1", "xa"):
            add("sync", lambda s: s.dma_start(
                out=d_out[0:128, 0:GT], in_=rb_sb[:]),
                waits=[ms[f"rdy_ln1_0"]], inc=("dSt", 16))
        else:
            add("sync", lambda s: s.dma_start(
                out=d_out.rearrange("(c p) t -> c p t", p=128),
                in_=d_xm[:, :, :]),
                waits=[ms["xmid_stored"]] + ms["qkv_done_0"], inc=("dSt", 16))
    else:
        for g in range(NG):
            phase1_group(g)
    if _DEBUG == "pre" or _DEBUG.startswith("grp0"):
        pass
    elif _DEBUG == "xmid":
        add("sync", lambda s: s.dma_start(
            out=d_out.rearrange("(c p) t -> c p t", p=128), in_=d_xm[:, :, :]),
            waits=[ms["xmid_stored"]], inc=("dSt", 16))
    elif _DEBUG == "qkv":
        add("gpsimd", lambda g: g.dma_start(
            out=d_out.rearrange("(c p) t -> c p t", p=128), in_=d_q[:, :, :]),
            waits=ms["qkv_done_1"] + [ms["xmid_stored"]], inc=("dSt", 16))
    elif _DEBUG == "poll":
        flag_and_poll()
        nb[0] += 1
        barrier(nb[0])
        add("gpsimd", lambda g: g.dma_start(
            out=d_out[0:1, 0:16], in_=flagbuf[:]), inc=("dDbg", 16))
    elif _DEBUG == "kvload":
        flag_and_poll()
        nb[0] += 1
        barrier(nb[0])
        for sec in range(2):
            for kvh in range(NKV):
                add("sync", lambda s, sec=sec, kvh=kvh: s.dma_start(
                    out=ksb[:, (sec * NKV + kvh) * TOK:(sec * NKV + kvh + 1) * TOK],
                    in_=d_shk[sec, kvh, :, :]), inc=("dKV", 16))
        add("sync", lambda s: s.dma_start(
            out=vsb[:].rearrange("p (u c) -> p u c", c=ER),
            in_=d_shv.rearrange("h l p c -> p (h l) c")), inc=("dKV", 16))
        wkv = ("dKV", plan.at("dKV"))
        lvl = int(os.environ.get("MK_KV", "1"))
        if lvl >= 2:
            add("sync", lambda s: s.dma_start(
                out=qt_t[0][:].rearrange("p (c q) -> p c q", q=128),
                in_=d_q[:, :, 0:128].rearrange("c p q -> p c q")),
                inc=("dQt0", 16))
            add("sync", lambda s: s.dma_start(
                out=mt_t[0][:], in_=d_bm[:, 0:256]), inc=("dMt0", 16))
            wq = ("dQt0", plan.at("dQt0"))
            wmt = ("dMt0", plan.at("dMt0"))
        if lvl >= 3:
            for sec in range(2):
                bank = (pA, pB)[sec]
                add("tensor", lambda t, b=bank, sec=sec: t.matmul(
                    b[:, 0:128], qt_t[0][:, 0:128],
                    ksb[:, (sec * NKV) * TOK:(sec * NKV) * TOK + 128],
                    start=True, stop=True), waits=[wkv, wq, wmt], inc=("cT", 1))
                wsm = ("cT", plan.at("cT"))
                add("vector", lambda v, b=bank, sec=sec: v.scalar_tensor_tensor(
                    srow[:, sec * 128:(sec + 1) * 128], b[:, 0:128], 1.0,
                    mt_t[0][:, sec * 128:(sec + 1) * 128],
                    op0=ALU.mult, op1=ALU.add), waits=[wsm], inc=("cV", 1))
            wse = ("cV", plan.at("cV"))
            add("vector", lambda v: v.reduce_max(
                negmx[:, 0:1], srow[:, 0:256], axis=AX.X, negate=True),
                waits=[wse], inc=("cV", 1))
            wmx = ("cV", plan.at("cV"))
            add("scalar", lambda s: s.activation(
                p4[:, 0:256], srow[:, 0:256], AF.Exp, bias=negmx[:, 0:1],
                scale=1.0, accum_out=sums4[:, 0:1]), waits=[wmx], inc=("cS", 1))
            wex = ("cS", plan.at("cS"))
            add("vector", lambda v: v.reciprocal(rs4[:, 0:1], sums4[:, 0:1]),
                waits=[wex], inc=("cV", 1))
            wrc = ("cV", plan.at("cV"))
            add("gpsimd", lambda g: g.tensor_scalar(
                p4[:, 0:256], p4[:, 0:256], rs4[:, 0:1], None, ALU.mult),
                waits=[wrc], inc=("cG", 1))
            wpn = ("cG", plan.at("cG"))
        if lvl >= 4:
            for kt in range(2):
                for hx in range(1):
                    add("tensor", lambda t, kt=kt: t.transpose(
                        pT[:, (kt % 2) * 128:((kt % 2) + 1) * 128],
                        p4[:, kt * 128:(kt + 1) * 128], identb[:]),
                        waits=[wpn] if kt == 0 else (), inc=("cT", 1))
                wtr = ("cT", plan.at("cT"))
                add("scalar", lambda s, kt=kt: s.activation(
                    ptb[:, kt * 128:(kt + 1) * 128],
                    pT[:, (kt % 2) * 128:((kt % 2) + 1) * 128],
                    AF.Copy, bias=0.0, scale=1.0), waits=[wtr], inc=("cS", 1))
                wpte = ("cS", plan.at("cS"))
                add("tensor", lambda t, kt=kt, wv=wpte: t.matmul(
                    pC[:, 0:128], vsb[:, (kt * NT) * ER: (kt * NT) * ER + 128],
                    ptb[:, kt * 128:(kt + 1) * 128],
                    start=(kt == 0), stop=(kt == 1)),
                    waits=[wpte], inc=("cT", 1) if kt == 1 else None)
            wpv = ("cT", plan.at("cT"))
            add("scalar", lambda s: s.activation(
                ctxst[0][:, 0:128], pC[:, 0:128], AF.Copy, bias=0.0, scale=1.0),
                waits=[wpv], inc=("cS", 1))
            wcte = ("cS", plan.at("cS"))
            add("sync", lambda s: s.dma_start(
                out=d_ctx[0:1, :, 0:128].rearrange("h p q -> p h q"),
                in_=ctxst[0][:, 0:128].rearrange("p (h q) -> p h q", q=128)),
                waits=[wcte], inc=("dCtx", 16))
        add("gpsimd", lambda g: g.dma_start(
            out=d_out[0:128, 0:2048], in_=ksb[:, 0:2048]),
            waits=[wkv, ("dCtx", plan.at("dCtx")), ("cG", plan.at("cG")),
                   ("cS", plan.at("cS"))], inc=("dDbg", 16))
    elif _DEBUG == "att0n":
        flag_and_poll()
        nb[0] += 1
        barrier(nb[0])
        _tl = [int(v) for v in os.environ.get("MK_T", "0").split(",")]
        _hgs = [int(v) for v in os.environ.get("MK_HG", "0").split(",")]
        phase2(only_tiles=_tl, only_hg=_hgs, dump=False)
        add("gpsimd", lambda g: g.dma_start(
            out=d_out[0:512, 0:2048].rearrange("(c p) t -> c p t", p=128),
            in_=d_ctx[0:4, :, :]),
            waits=[ms["ctx_stored"]], inc=("dDbg", 16))
    elif _DEBUG == "att0":
        flag_and_poll()
        nb[0] += 1
        barrier(nb[0])
        import os as _os
        _tl = [int(v) for v in _os.environ.get("MK_T", "0,1").split(",")]
        _h4 = _os.environ.get("MK_H4", "")
        phase2(only_tiles=_tl, only_hg=[0], dump=True,
               only_h4=[int(v) for v in _h4.split(",")] if _h4 else None)
        for a in range(8):
            add("gpsimd", lambda g, a=a: g.dma_start(
                out=d_out[a * 128:(a + 1) * 128, 0:2048],
                in_=d_dbg[:, a * 2048:(a + 1) * 2048]),
                waits=[("dDbg", plan.at("dDbg")), ms["ctx_stored"]],
                inc=("dDbg", 16))
    else:
        flag_and_poll()
        nb[0] += 1
        barrier(nb[0])
        phase2()
        if _DEBUG == "ctx":
            add("gpsimd", lambda g: g.dma_start(
                out=d_out.rearrange("(c p) t -> c p t", p=128), in_=d_ctx[:, :, :]),
                waits=[ms["ctx_stored"]], inc=("dSt", 16))
        else:
            nb[0] += 1
            barrier(nb[0])
            for g in range(NG):
                phase3_group(g)
    fin = plan.at("dSt")
    find = plan.at("dDbg")
    add("sync", lambda s: (s.wait_ge(sems["dDbg"], find) if find else None,
                           s.wait_ge(sems["dSt"], fin))[-1])

    for name in ("dCon", "dCn2", "dXB", "dTab", "dSlA", "dSlB",
                 "dXf0", "dXf1", "dXf2", "dXf3", "dXm", "dQ", "dKw", "dV",
                 "dF", "dKV", "dQt0", "dQt1", "dMt0", "dMt1", "dCtx",
                 "dCb0", "dCb1", "dXa", "dSt", "dDbg", "pol", "bar",
                 "cT", "cV", "cS", "cG"):
        sems[name] = cstk.enter_context(nc.semaphore(name))

    per_eng = {e: [] for e in ENGINES}
    for eng, fn, waits, incs in plan.seq:
        per_eng[eng].append((fn, waits, incs))

    with nc.Block() as block:
        def mk(ops):
            def body(eng):
                for fn, waits, incs in ops:
                    for s, v in waits:
                        eng.wait_ge(sems[s], v)
                    ins = fn(eng)
                    for s, n in incs:
                        tgt = ins if ins is not None else eng.nop()
                        tgt.then_inc(sems[s], n)
            return body
        block.sync(mk(per_eng["sync"]))
        block.gpsimd(mk(per_eng["gpsimd"]))
        block.tensor(mk(per_eng["tensor"]))
        block.vector(mk(per_eng["vector"]))
        block.scalar(mk(per_eng["scalar"]))

    return nc


def kernel(**inputs):
    in_maps = _host_prep(inputs)
    nc = build_program()
    res = run_bass_kernel_spmd(nc, in_maps, core_ids=list(range(8)))
    out = np.empty((NBATCH, S, D), np.float32)
    for c in range(8):
        b, par = c // 2, c % 2
        gl = [_g_of(par, i) for i in range(NT)]
        rows = np.concatenate([np.arange(g * 128, (g + 1) * 128) for g in gl])
        out[b, rows, :] = res.results[c]["outT"].T
    return out


# version 10
# speedup vs baseline: 1.1534x; 1.0636x over previous
import dataclasses
import math
import os
from collections import defaultdict
from contextlib import ExitStack

import numpy as np

import concourse.bass as bass
import concourse.mybir as mybir
from concourse.bass_utils import run_bass_kernel_spmd
from concourse.expressions_rust import RuntimeValue

F32 = mybir.dt.float32
BF16 = mybir.dt.bfloat16
I32 = mybir.dt.int32
U8 = mybir.dt.uint8
AF = mybir.ActivationFunctionType
ALU = mybir.AluOpType
AX = mybir.AxisListType

D = 4096
S = 4096
NBATCH = 4
HD = 128
NH = 32
NKV = 2
QKV_D = 4608
NT = 16
TOK = 2048
NG = 2
GT = 1024
NBLK = 2
NCH = 32
QCH = 36
ER = 256
EPS = 1e-5
NEG = -1e30
ROT = 64

_DEBUG = os.environ.get("MK_DEBUG", "")
ENGINES = ("sync", "gpsimd", "tensor", "vector", "scalar")


def _g_of(par, i):
    return 2 * i + ((i & 1) if par == 0 else 1 - (i & 1))


def _perm_head():
    return np.array(list(range(0, ROT, 2)) + list(range(1, ROT, 2)) +
                    list(range(ROT, HD)), dtype=np.int64)


def _host_prep(inputs):
    x = np.asarray(inputs["L0_attn_hidden_states"], np.float32)
    qkv_w = np.asarray(inputs["L0_attn_qkv_w"], np.float32)
    qkv_b = np.asarray(inputs["L0_attn_qkv_b"], np.float32)
    dense_w = np.asarray(inputs["L0_attn_dense_w"], np.float32)
    W = np.asarray(inputs["L0_ffn_W"], np.float32)
    bias = np.asarray(inputs["L0_ffn_bias"], np.float32)
    A = np.asarray(inputs["L0_ffn_A"], np.float32)
    Bf = np.asarray(inputs["L0_ffn_Bf"], np.float32)
    coeffs = np.asarray(inputs["L0_ffn_coeffs"], np.float32)
    scales = np.asarray(inputs["L0_ffn_scales"], np.float32)

    ph = _perm_head()
    perm_rows = np.arange(QKV_D)
    for h in range(NH):
        perm_rows[h * HD:(h + 1) * HD] = h * HD + ph
    for kv in range(NKV):
        base = D + kv * HD
        perm_rows[base:base + HD] = base + ph
    qkv_w_p = qkv_w[perm_rows]
    qkv_b_p = qkv_b[perm_rows]

    Pm = np.zeros((128, 128), np.float32)
    for m in range(32):
        Pm[m + 32, m] = -1.0
        Pm[m, m + 32] = 1.0

    shared = {
        "wT_qkv": np.ascontiguousarray(qkv_w_p.T),
        "b_qkv": np.ascontiguousarray(qkv_b_p.reshape(QCH, 128).T),
        "wT_ffn": np.ascontiguousarray(W.T),
        "b_ffn": np.ascontiguousarray(bias.reshape(NCH, 128).T),
        "dT": np.ascontiguousarray(dense_w.T),
        "aT": np.ascontiguousarray(A.transpose(2, 0, 1).reshape(D, ER)),
        "bS": np.ascontiguousarray(Bf.transpose(0, 2, 1).reshape(ER, D)),
        "coefr": np.ascontiguousarray(np.repeat(coeffs, 16).reshape(2, 128).T),
        "scalr": np.ascontiguousarray(np.repeat(scales, 16).reshape(2, 128).T),
        "ident": np.eye(128, dtype=np.float32),
        "permm": Pm,
        "ones_col": np.ones((128, 1), np.float32),
        "ones_row": np.ones((1, 128), np.float32),
    }

    inv = 10000.0 ** (-(np.arange(32, dtype=np.float64) / 32.0))
    c_att = 1.0 / math.sqrt(HD)
    rng = np.random.default_rng(int.from_bytes(os.urandom(8), "little"))
    nonce1 = int(rng.integers(1, (1 << 30) - 1))
    nonce2 = int(rng.integers(1, (1 << 30) - 1))

    in_maps = []
    for c in range(8):
        b, par = c // 2, c % 2
        gl = [_g_of(par, i) for i in range(NT)]
        rows = np.concatenate([np.arange(g * 128, (g + 1) * 128) for g in gl])
        m = dict(shared)
        m["xT"] = np.ascontiguousarray(x[b][rows].T)

        pos = rows.astype(np.float64)
        th = pos[None, :] * inv[:, None]
        cos = np.cos(th).astype(np.float32)
        sin = np.sin(th).astype(np.float32)
        ck = np.empty((128, TOK), np.float32)
        sk = np.empty((128, TOK), np.float32)
        ck[0:32] = cos; ck[32:64] = cos; ck[64:128] = 1.0
        sk[0:32] = sin; sk[32:64] = sin; sk[64:128] = 0.0
        m["ck"], m["sk"] = ck, sk
        m["cq"] = (ck * c_att).astype(np.float32)
        m["sq"] = (sk * c_att).astype(np.float32)

        bm = np.zeros((128, NT * 256), np.float32)
        for i in range(NT):
            g = gl[i]
            for sec in range(2):
                gk = _g_of(sec, i)
                blkm = np.zeros((128, 128), np.float32)
                if gk > g:
                    blkm[:] = NEG
                elif gk == g:
                    blkm[np.triu_indices(128, 1)] = NEG
                bm[:, i * 256 + sec * 128: i * 256 + (sec + 1) * 128] = blkm
        m["bmask"] = bm

        kw = par * (NKV * 128 * TOK)
        vw = par * (NT * 128 * ER)
        m["offs"] = np.array(
            [[kw, vw, par * 16, (1 - par) * 16, nonce1, nonce2, 0, 0]], np.int32)
        in_maps.append(m)
    return in_maps


class Plan:
    def __init__(self):
        self.seq = []
        self.cnt = defaultdict(int)

    def add(self, eng, fn, waits=(), inc=None):
        incs = []
        if inc is not None:
            if isinstance(inc, tuple):
                inc = [inc]
            for s, n in inc:
                self.cnt[s] += n
                incs.append((s, n))
        for s, v in waits:
            assert v <= self.cnt[s], f"wait on future: {s} {v} > {self.cnt[s]}"
        self.seq.append((eng, fn, [(s, v) for s, v in waits if v > 0], incs))

    def at(self, s):
        return self.cnt[s]


def _chunks(total, step):
    out, c0 = [], 0
    while c0 < total:
        out.append((c0, min(step, total - c0)))
        c0 += step
    return out


def build_program():
    nc = bass.Bass("TRN2", target_bir_lowering=False)

    d_x = nc.dram_tensor("xT", [D, TOK], F32, kind="ExternalInput").ap()
    d_bm = nc.dram_tensor("bmask", [128, NT * 256], F32, kind="ExternalInput").ap()
    d_cq = nc.dram_tensor("cq", [128, TOK], F32, kind="ExternalInput").ap()
    d_sq = nc.dram_tensor("sq", [128, TOK], F32, kind="ExternalInput").ap()
    d_ck = nc.dram_tensor("ck", [128, TOK], F32, kind="ExternalInput").ap()
    d_sk = nc.dram_tensor("sk", [128, TOK], F32, kind="ExternalInput").ap()
    d_offs = nc.dram_tensor("offs", [1, 8], I32, kind="ExternalInput").ap()
    d_wq = nc.dram_tensor("wT_qkv", [D, QKV_D], F32, kind="ExternalInput").ap()
    d_bq = nc.dram_tensor("b_qkv", [128, QCH], F32, kind="ExternalInput").ap()
    d_wf = nc.dram_tensor("wT_ffn", [D, D], F32, kind="ExternalInput").ap()
    d_bfv = nc.dram_tensor("b_ffn", [128, NCH], F32, kind="ExternalInput").ap()
    d_dT = nc.dram_tensor("dT", [D, D], F32, kind="ExternalInput").ap()
    d_aT = nc.dram_tensor("aT", [D, ER], F32, kind="ExternalInput").ap()
    d_bS = nc.dram_tensor("bS", [ER, D], F32, kind="ExternalInput").ap()
    d_coef = nc.dram_tensor("coefr", [128, 2], F32, kind="ExternalInput").ap()
    d_scal = nc.dram_tensor("scalr", [128, 2], F32, kind="ExternalInput").ap()
    d_id = nc.dram_tensor("ident", [128, 128], F32, kind="ExternalInput").ap()
    d_pm = nc.dram_tensor("permm", [128, 128], F32, kind="ExternalInput").ap()
    d_oc = nc.dram_tensor("ones_col", [128, 1], F32, kind="ExternalInput").ap()
    d_or = nc.dram_tensor("ones_row", [1, 128], F32, kind="ExternalInput").ap()
    d_out = nc.dram_tensor("outT", [D, TOK], F32, kind="ExternalOutput").ap()

    d_q = nc.dram_tensor("q_hbm", [NCH, 128, TOK], BF16, kind="Internal").ap()
    d_xm = nc.dram_tensor("xmid", [NCH, 128, TOK], F32, kind="Internal").ap()
    d_xa = nc.dram_tensor("xatt", [NCH, 128, TOK], F32, kind="Internal").ap()
    d_ctx = nc.dram_tensor("ctx", [NCH, 128, TOK], BF16, kind="Internal").ap()
    d_shk = nc.dram_tensor("sh_k", [2, NKV, 128, TOK], BF16, kind="Internal",
                           addr_space="Shared").ap()
    d_shv = nc.dram_tensor("sh_v", [2, NT, 128, ER], BF16, kind="Internal",
                           addr_space="Shared").ap()
    d_shf = nc.dram_tensor("sh_flag", [2, 16], I32, kind="Internal",
                           addr_space="Shared").ap()
    d_dbg = nc.dram_tensor("dbg", [128, 16384], F32, kind="Internal").ap()

    cstk = ExitStack()

    def sbt(name, shape, dt=F32):
        return cstk.enter_context(nc.sbuf_tensor(name, shape, dt))

    def pst(name, shape, dt=F32):
        return cstk.enter_context(nc.psum_tensor(name, shape, dt))

    identb = sbt("identb", [128, 128], BF16)
    permb = sbt("permb", [128, 128], BF16)
    onescol = sbt("onescol", [128, 1], BF16)
    onesrow = sbt("onesrow", [1, 128], BF16)
    bq_sb = sbt("bq_sb", [128, QCH])
    bfh_sb = sbt("bfh_sb", [128, NCH])
    wvec = sbt("wvec", [128, 2])
    scal_tmp = sbt("scal_tmp", [128, 2])
    offs_sb = sbt("offs_sb", [1, 8], I32)
    aSb = sbt("aSb", [128, NCH * ER], BF16)
    bSb = sbt("bSb", [128, 2 * D], BF16)
    a1row = sbt("a1row", [1, ER], BF16)
    w1f_row = sbt("w1f_row", [1, D], BF16)
    w1q_row = sbt("w1q_row", [1, QKV_D], BF16)
    negm_row = sbt("negm_row", [1, GT], BF16)
    rhi_row = sbt("rhi_row", [1, GT], BF16)
    rlo_row = sbt("rlo_row", [1, GT], BF16)
    mrow = sbt("mrow", [1, GT])
    vrow = sbt("vrow", [1, GT])
    vtmp = sbt("vtmp", [1, GT])
    rb_sb = sbt("rb_sb", [128, GT])
    flagbuf = sbt("flagbuf", [1, 16], I32)

    pA = pst("pA", [128, 512])
    pB = pst("pB", [128, 512])
    pC = pst("pC", [128, 512])
    pD = pst("pD", [128, 512])
    pE = pst("pE", [128, 512])
    pF = pst("pF", [128, 512])
    pR = pst("pR", [128, 512])
    pT = pst("pT", [128, 1024], BF16)

    ARENA = 135488
    arena = nc.alloc_sbuf_tensor("arena", [128, ARENA], U8)
    abase = nc.lookup_mloc(arena).addr

    def at(name, off, shape, dt=F32):
        assert off % 64 == 0
        n = mybir.dt.size(dt)
        for s in shape[1:]:
            n *= s
        assert off + n <= ARENA, f"{name}: arena overflow {off}+{n}"
        return nc.alloc_sbuf_tensor_at(name, shape, dt, offset=abase + off)

    xbig = at("xbig", 0, [128, NCH * GT], BF16)
    tabs = at("tabs", 65536, [128, 8 * 512])
    wslab = [at(f"wslab{j}", 81920 + j * 8192, [128, 4096], BF16) for j in range(2)]
    xf_t = [at(f"xf{j}", 98304 + j * 2048, [128, 512]) for j in range(4)]
    xm_st = [at(f"xmst{j}", 106496 + j * 2048, [128, 512]) for j in range(2)]
    t1_t = [at(f"t1{j}", 110592 + j * 2048, [128, 512]) for j in range(2)]
    qb_t = [at(f"qb{j}", 114688 + j * 1024, [128, 512], BF16) for j in range(2)]
    rq_t = [at(f"rq{j}", 116736 + j * 1024, [128, 512], BF16) for j in range(2)]
    xab = at("xab", 118784, [128, 4 * 512], BF16)
    sq_t = [at(f"sq{j}", 122880 + j * 1024, [128, 512], BF16) for j in range(2)]
    vst_t = [at(f"vst{j}", 124928 + j * 2048, [128, 4 * 256], BF16)
             for j in range(2)]
    ksb = at("ksb", 0, [128, 2 * NKV * TOK], BF16)
    vsb = at("vsb", 16384, [128, 2 * NT * ER], BF16)
    qt_t = [at(f"qt{j}", 32768 + j * 8192, [128, NCH * 128], BF16) for j in range(2)]
    srow_t = [at(f"srow{j}", 49152 + j * 16384, [128, 4096])
              for j in range(2)]
    p4 = at("p4", 81920, [128, 4 * 4096], BF16)
    ptb = at("ptb", 114688, [128, 16 * 512], BF16)
    mt_t = [at(f"mt{j}", 131072 + j * 1024, [128, 256]) for j in range(2)]
    ctxst = [at(f"ctxst{j}", 133120 + j * 1024, [128, 512], BF16) for j in range(2)]
    sums4 = at("sums4", 135168, [128, 8])
    rs4 = at("rs4", 135232, [128, 8])
    negmx = at("negmx", 135296, [128, 8])
    cmax = at("cmax", 135360, [128, 24])
    ctxb_t = [at(f"ctxb{j}", j * 32768, [128, NCH * 512], BF16) for j in range(2)]

    plan = Plan()
    ms = {}
    sems = {}
    _rt_regs = {}

    def add(eng, fn, waits=(), inc=None):
        plan.add(eng, fn, waits, inc)

    def _rt(eng, key):
        idx = {"kw": 0, "vw": 1, "fw": 2, "fr": 3, "n1": 4, "n2": 5}[key]
        name = f"rt_{key}"
        if name not in _rt_regs:
            r = eng.alloc_register(name)
            eng.reg_load(r, offs_sb[0:1, idx:idx + 1])
            _rt_regs[name] = RuntimeValue(val=r, min_val=0, max_val=1 << 30)
        return _rt_regs[name]

    def barrier(k):
        for e in ENGINES:
            add(e, lambda eng: eng.nop(), inc=("bar", 1))
        tgt = 5 * k
        assert plan.at("bar") == tgt
        for e in ENGINES:
            add(e, lambda eng, t=tgt: None, waits=[("bar", tgt)])

    def preloads():
        for t, dsrc in ((identb, d_id), (permb, d_pm), (onescol, d_oc),
                        (onesrow, d_or)):
            add("gpsimd", lambda g, t=t, d_=dsrc: g.dma_start(out=t[:], in_=d_),
                inc=("dCon", 16))
        add("gpsimd", lambda g: g.dma_start(
            out=aSb[:].rearrange("p (i e) -> p i e", e=ER),
            in_=d_aT.rearrange("(i p) e -> p i e", p=128)), inc=("dCon", 16))
        add("gpsimd", lambda g: g.dma_start(
            out=bSb[:].rearrange("p (j o) -> p j o", o=D),
            in_=d_bS.rearrange("(j p) o -> p j o", p=128)), inc=("dCon", 16))
        wconst = ("dCon", plan.at("dCon"))
        for t, dsrc in ((bq_sb, d_bq), (bfh_sb, d_bfv), (wvec, d_coef),
                        (offs_sb, d_offs), (scal_tmp, d_scal)):
            add("sync", lambda s, t=t, d_=dsrc: s.dma_start(out=t[:], in_=d_),
                inc=("dCn2", 16))
        wx = ("dCn2", plan.at("dCn2"))
        add("sync", lambda s: [_rt(s, k) for k in
                               ("kw", "vw", "fw", "fr", "n1", "n2")] and s.nop(),
            waits=[wx])
        add("vector", lambda v: v.tensor_mul(wvec[:], wvec[:], scal_tmp[:]),
            waits=[wx], inc=("cV", 1))
        add("vector", lambda v: v.tensor_scalar_mul(bfh_sb[:], bfh_sb[:], 0.5),
            inc=("cV", 1))
        wv = ("cV", plan.at("cV"))
        for j in range(2):
            add("vector", lambda v, j=j: v.tensor_scalar(
                bSb[:, j * D:(j + 1) * D], bSb[:, j * D:(j + 1) * D],
                wvec[:, j:j + 1], None, ALU.mult),
                waits=[wv, wconst], inc=("cV", 1))
        for i in range(NCH):
            add("tensor", lambda t, i=i: t.matmul(
                pR[0:1, 0:ER], onescol[:], aSb[:, i * ER:(i + 1) * ER],
                start=(i == 0), stop=(i == NCH - 1)),
                waits=[wconst] if i == 0 else (),
                inc=("cT", 1) if i == NCH - 1 else None)
        wt = ("cT", plan.at("cT"))
        add("scalar", lambda s: s.activation(a1row[:], pR[0:1, 0:ER], AF.Copy,
                                             bias=0.0, scale=1.0),
            waits=[wt], inc=("cS", 1))
        ms["a1_done"] = ("cS", plan.at("cS"))

    def ln_finalize(half, tag):
        wt = ("cT", plan.at("cT"))
        for blk in range(NBLK):
            sl = slice(blk * 512, (blk + 1) * 512)
            sbank, qbank = (pC, pD)[blk], (pE, pF)[blk]
            add("vector", lambda v, sl=sl, b=sbank: v.tensor_scalar_mul(
                mrow[0:1, sl], b[0:1, 0:512], 1.0 / D), waits=[wt], inc=("cV", 1))
            add("vector", lambda v, sl=sl, b=qbank: v.tensor_scalar_mul(
                vrow[0:1, sl], b[0:1, 0:512], 1.0 / D), waits=[wt], inc=("cV", 1))
        add("vector", lambda v: v.tensor_mul(vtmp[0:1, :], mrow[0:1, :],
                                             mrow[0:1, :]), inc=("cV", 1))
        add("vector", lambda v: v.tensor_sub(vrow[0:1, :], vrow[0:1, :],
                                             vtmp[0:1, :]), inc=("cV", 1))
        add("vector", lambda v: v.tensor_scalar_mul(negm_row[0:1, :],
                                                    mrow[0:1, :], -1.0),
            inc=("cV", 1))
        add("vector", lambda v: v.tensor_scalar_add(vrow[0:1, :], vrow[0:1, :],
                                                    EPS), inc=("cV", 1))
        wv = ("cV", plan.at("cV"))
        add("scalar", lambda s: s.activation(vtmp[0:1, :], vrow[0:1, :], AF.Ln,
                                             bias=0.0, scale=1.0),
            waits=[wv], inc=("cS", 1))
        add("scalar", lambda s: s.activation(
            vrow[0:1, :], vtmp[0:1, :], AF.Exp, bias=0.0, scale=-0.5),
            inc=("cS", 1))
        wsc = ("cS", plan.at("cS"))
        if half:
            add("vector", lambda v: v.tensor_scalar_mul(vrow[0:1, :],
                                                        vrow[0:1, :], 0.5),
                waits=[wsc], inc=("cV", 1))
            wsc = ("cV", plan.at("cV"))
        add("vector", lambda v: v.tensor_copy(rhi_row[0:1, :], vrow[0:1, :]),
            waits=[wsc], inc=("cV", 1))
        add("vector", lambda v: v.tensor_sub(rlo_row[0:1, :], vrow[0:1, :],
                                             rhi_row[0:1, :]), inc=("cV", 1))
        wv = ("cV", plan.at("cV"))
        for blk in range(NBLK):
            sl = slice(blk * 512, (blk + 1) * 512)
            add("tensor", lambda t, sl=sl: t.matmul(
                pE[:], onesrow[:], rhi_row[0:1, sl], start=True, stop=False),
                waits=[wv, ("cS", plan.at("cS"))])
            add("tensor", lambda t, sl=sl: t.matmul(
                pE[:], onesrow[:], rlo_row[0:1, sl], start=False, stop=True),
                inc=("cT", 1))
            wt2 = ("cT", plan.at("cT"))
            add("scalar", lambda s, sl=sl: s.activation(
                rb_sb[:, sl], pE[:], AF.Copy, bias=0.0, scale=1.0),
                waits=[wt2, wv], inc=("cS", 1))
        ms[f"rdy_{tag}"] = ("cS", plan.at("cS"))

    stats_pend = []

    def stats_flush(keep=0):
        while len(stats_pend) > keep:
            tagu, ent = stats_pend.pop(0)
            for eng, fn, waits, inc in ent:
                add(eng, fn, waits=waits, inc=inc)
            ms[tagu] = ("cT", plan.at("cT"))

    def stats_ops(u, oo, blk, src_f32, first, last, tag, wsrc):
        stats_flush(keep=1)
        cb = qb_t[u % 2]
        wprev = [wsrc] + ([ms[(tag, "stq", u - 2)]] if u >= 2 else [])
        add("vector", lambda v, xm=src_f32, cb=cb: v.tensor_copy(cb[:], xm[:]),
            waits=wprev, inc=("cV", 1))
        wcb = ("cV", plan.at("cV"))
        sqt = sq_t[u % 2]
        add("scalar", lambda s, cb=cb, sqt=sqt: s.activation(
            sqt[:], cb[:], AF.Square, bias=0.0, scale=1.0),
            waits=[wcb], inc=("cS", 1))
        wsq = ("cS", plan.at("cS"))
        sbank, qbank = (pC, pD)[blk], (pE, pF)[blk]
        sw = [wcb]
        if first and ("stats_free" in ms):
            sw.append(ms["stats_free"])
        ent = [
            ("tensor", (lambda t, cb=cb, b=sbank, first=first, last=last:
                        t.matmul(b[0:1, 0:512], onescol[:], cb[:],
                                 start=first, stop=last)), sw, None),
            ("tensor", (lambda t, sqt=sqt, b=qbank, first=first, last=last:
                        t.matmul(b[0:1, 0:512], onescol[:], sqt[:],
                                 start=first, stop=last)), [wsq], ("cT", 1)),
        ]
        stats_pend.append(((tag, "stq", u), ent))

    def xa_adapters(tag, wsrc):
        for blk in range(NBLK):
            for j in range(2):
                bank = (pC, pD)[j]
                for i in range(NCH):
                    w0 = []
                    if i == 0:
                        w0 = [ms[f"rdy_{tag}"], wsrc]
                        if blk == 0 and j == 0 and "stats_free" in ms:
                            w0.append(ms["stats_free"])
                    add("tensor", lambda t, b=bank, i=i, j=j, blk=blk: t.matmul(
                        b[:], aSb[:, i * ER + j * 128: i * ER + (j + 1) * 128],
                        xbig[:, i * GT + blk * 512: i * GT + blk * 512 + 512],
                        start=(i == 0), stop=False), waits=w0)
                add("tensor", lambda t, b=bank, j=j, blk=blk: t.matmul(
                    b[:], a1row[0:1, j * 128:(j + 1) * 128],
                    negm_row[0:1, blk * 512: blk * 512 + 512],
                    start=False, stop=True), inc=("cT", 1))
                wt = ("cT", plan.at("cT"))
                add("scalar", lambda s, b=bank, j=j, blk=blk: s.activation(
                    xab[:, (blk * 2 + j) * 512:(blk * 2 + j + 1) * 512], b[:],
                    AF.Copy, bias=0.0, scale=1.0), waits=[wt], inc=("cS", 1))
        ms["xa_rdy"] = ("cS", plan.at("cS"))
        ms["stats_free"] = ms["xa_rdy"]

    def phase1_group(g, stop_after=None):
        gcol = g * GT
        waits = [ms["xbig_free"]] if g > 0 else []
        prev_ev = ("cV", plan.at("cV")) if g > 0 else None
        add("gpsimd", lambda gp: gp.dma_start(
            out=xbig[:].rearrange("p (i t) -> p i t", t=GT),
            in_=d_x[:, gcol:gcol + GT].rearrange("(i p) t -> p i t", p=128)),
            waits=waits, inc=("dXB", 16))
        wxb = ("dXB", plan.at("dXB"))
        tw = [ms["tabs_free"]] if g > 0 else []
        for j, dsrc in enumerate((d_cq, d_sq, d_ck, d_sk)):
            add("sync", lambda s, j=j, d_=dsrc: s.dma_start(
                out=tabs[:, j * 1024:(j + 1) * 1024], in_=d_[:, gcol:gcol + GT]),
                waits=tw, inc=("dTab", 16))
        wtab = ("dTab", plan.at("dTab"))

        for ci in range(NCH):
            for blk in range(NBLK):
                u = ci * NBLK + blk
                xsl = xbig[:, ci * GT + blk * 512: ci * GT + blk * 512 + 512]
                sqt = sq_t[u % 2]
                swq = [wxb] if u < 2 else [ms[("l1", "stq", u - 2)]]
                add("scalar", lambda s, xsl=xsl, sqt=sqt: s.activation(
                    sqt[:], xsl, AF.Square, bias=0.0, scale=1.0),
                    waits=swq, inc=("cS", 1))
                wsq = ("cS", plan.at("cS"))
                sbank, qbank = (pC, pD)[blk], (pE, pF)[blk]
                sw = [wxb]
                if ci == 0 and "stats_free" in ms:
                    sw.append(ms["stats_free"])
                add("tensor", lambda t, xsl=xsl, b=sbank, ci=ci: t.matmul(
                    b[0:1, 0:512], onescol[:], xsl,
                    start=(ci == 0), stop=(ci == NCH - 1)),
                    waits=sw if ci == 0 else ())
                add("tensor", lambda t, sqt=sqt, b=qbank, ci=ci: t.matmul(
                    b[0:1, 0:512], onescol[:], sqt[:],
                    start=(ci == 0), stop=(ci == NCH - 1)),
                    waits=[wsq], inc=("cT", 1))
                ms[("l1", "stq", u)] = ("cT", plan.at("cT"))
        ln_finalize(half=True, tag=f"ln1_{g}")
        ms["stats_free"] = ms[f"rdy_ln1_{g}"]
        if stop_after == "ln1":
            return
        xa_adapters(f"ln1_{g}", wxb)
        if stop_after == "xa":
            return

        def slab_load(mat, o, wprev):
            sem = ("dSlA", "dSlB")[o % 2]
            src = mat[:, o * 128:(o + 1) * 128].rearrange("(i p) c -> p i c", p=128)
            add("gpsimd", lambda gp, o=o, src=src: gp.dma_start(
                out=wslab[o % 2][:].rearrange("p (i c) -> p i c", c=128),
                in_=src), waits=wprev, inc=(sem, 16))
            return (sem, plan.at(sem))

        slw = {0: slab_load(d_wf, 0, []), 1: slab_load(d_wf, 1, [])}
        for oo in range(NCH):
            slab = wslab[oo % 2]
            if g == 0:
                for i in range(NCH):
                    w0 = [slw[oo]]
                    if oo >= 1:
                        w0.append(ms["w1ev"])
                    else:
                        w0.append(ms["a1_done"])
                    add("tensor", lambda t, slab=slab, i=i: t.matmul(
                        pR[0:1, 0:128], onescol[:], slab[:, i * 128:(i + 1) * 128],
                        start=(i == 0), stop=(i == NCH - 1)),
                        waits=w0 if i == 0 else (),
                        inc=("cT", 1) if i == NCH - 1 else None)
                wt = ("cT", plan.at("cT"))
                add("scalar", lambda s, oo=oo: s.activation(
                    w1f_row[0:1, oo * 128:(oo + 1) * 128], pR[0:1, 0:128],
                    AF.Copy, bias=0.0, scale=1.0), waits=[wt], inc=("cS", 1))
                ms["w1ev"] = ("cS", plan.at("cS"))
            ww1 = ms["w1ev"] if g == 0 else None
            for blk in range(NBLK):
                u = oo * NBLK + blk
                bank = (pA, pB)[u % 2]
                xf = xf_t[u % 4]
                xsem = f"dXf{u % 4}"
                xw = [ms[("p1", "res", u - 4)]] if u >= 4 else []
                add("sync", lambda s, oo=oo, blk=blk, xf=xf: s.dma_start(
                    out=xf[:], in_=d_x[oo * 128:(oo + 1) * 128,
                                       gcol + blk * 512: gcol + blk * 512 + 512]),
                    waits=xw, inc=(xsem, 16))
                wxf = (xsem, plan.at(xsem))
                mw = [slw[oo], ms["xa_rdy"]]
                if u >= 2:
                    mw.append(ms[("p1", "t1", u - 2)])
                elif prev_ev is not None:
                    mw.append(prev_ev)
                for i in range(NCH):
                    add("tensor", lambda t, slab=slab, i=i, blk=blk, b=bank: t.matmul(
                        b[:], slab[:, i * 128:(i + 1) * 128],
                        xbig[:, i * GT + blk * 512: i * GT + blk * 512 + 512],
                        start=(i == 0), stop=False), waits=mw if i == 0 else ())
                add("tensor", lambda t, oo=oo, blk=blk, b=bank: t.matmul(
                    b[:], w1f_row[0:1, oo * 128:(oo + 1) * 128],
                    negm_row[0:1, blk * 512: blk * 512 + 512],
                    start=False, stop=False),
                    waits=[ww1] if (g == 0 and blk == 0) else ())
                for j in range(2):
                    add("tensor", lambda t, oo=oo, blk=blk, j=j, b=bank: t.matmul(
                        b[:], bSb[:, j * D + oo * 128: j * D + (oo + 1) * 128],
                        xab[:, (blk * 2 + j) * 512:(blk * 2 + j + 1) * 512],
                        start=False, stop=(j == 1)),
                        inc=("cT", 1) if j == 1 else None)
                wmm = ("cT", plan.at("cT"))
                if blk == NBLK - 1:
                    ms[("p1", "slabdone", oo)] = wmm
                t1 = t1_t[u % 2]
                xm = xm_st[u % 2]
                wt1 = [wmm]
                if u >= 2:
                    wt1.append(ms[("p1", "xmio", u - 2)])
                add("vector", lambda v, b=bank, blk=blk, t1=t1: v.tensor_tensor(
                    t1[:], b[:], rb_sb[:, blk * 512: blk * 512 + 512],
                    op=ALU.mult), waits=wt1, inc=("cV", 1))
                ms[("p1", "t1", u)] = ("cV", plan.at("cV"))
                add("vector", lambda v, t1=t1, xm=xm, oo=oo, xf=xf:
                    v.scalar_tensor_tensor(xm[:], t1[:], bfh_sb[:, oo:oo + 1],
                                           xf[:], op0=ALU.add, op1=ALU.add),
                    waits=[wxf, ("dXm", plan.at("dXm"))], inc=("cV", 1))
                wev = ("cV", plan.at("cV"))
                add("sync", lambda s, oo=oo, blk=blk, xm=xm: s.dma_start(
                    out=d_xm[oo, :, gcol + blk * 512: gcol + blk * 512 + 512],
                    in_=xm[:]), waits=[wev], inc=("dXm", 16))
                ms[("p1", "res", u)] = ("dXm", plan.at("dXm"))
                stats_ops(u, oo, blk, xm, first=(oo == 0), last=(oo == NCH - 1),
                          tag="l2", wsrc=wev)
                ms[("p1", "xmio", u)] = ("cV", plan.at("cV"))
            if oo + 2 < NCH:
                slw[oo + 2] = slab_load(d_wf, oo + 2, [ms[("p1", "slabdone", oo)]])
        stats_flush()
        ms["ffn_done"] = ("cT", plan.at("cT"))
        ms["xmid_stored"] = ("dXm", plan.at("dXm"))
        if stop_after == "ffn":
            return
        ln_finalize(half=False, tag=f"ln2_{g}")
        if stop_after == "ln2":
            return
        ms["stats_free"] = ms[f"rdy_ln2_{g}"]

        add("gpsimd", lambda gp: gp.dma_start(
            out=xbig[:].rearrange("p (i t) -> p i t", t=GT),
            in_=d_xm[:, :, gcol:gcol + GT].rearrange("i p t -> p i t")),
            waits=[ms["ffn_done"], ms["xmid_stored"]], inc=("dXB", 16))
        wxm = ("dXB", plan.at("dXB"))

        def qslab_load(o, wprev):
            sem = ("dSlA", "dSlB")[o % 2]
            src = d_wq[:, o * 128:(o + 1) * 128].rearrange("(i p) c -> p i c", p=128)
            add("gpsimd", lambda gp, o=o, src=src: gp.dma_start(
                out=wslab[o % 2][:].rearrange("p (i c) -> p i c", c=128),
                in_=src), waits=wprev, inc=(sem, 16))
            return (sem, plan.at(sem))

        qlw = {0: qslab_load(0, []), 1: qslab_load(1, [])}
        for oo in range(QCH):
            slab = wslab[oo % 2]
            if g == 0:
                for i in range(NCH):
                    w0 = [qlw[oo], ms["w1ev"]]
                    add("tensor", lambda t, slab=slab, i=i: t.matmul(
                        pR[0:1, 0:128], onescol[:], slab[:, i * 128:(i + 1) * 128],
                        start=(i == 0), stop=(i == NCH - 1)),
                        waits=w0 if i == 0 else (),
                        inc=("cT", 1) if i == NCH - 1 else None)
                wt = ("cT", plan.at("cT"))
                add("scalar", lambda s, oo=oo: s.activation(
                    w1q_row[0:1, oo * 128:(oo + 1) * 128], pR[0:1, 0:128],
                    AF.Copy, bias=0.0, scale=1.0), waits=[wt], inc=("cS", 1))
                ms["w1ev"] = ("cS", plan.at("cS"))
            for blk in range(NBLK):
                u = oo * NBLK + blk
                bank = (pA, pB)[u % 2]
                mw = [qlw[oo], wxm, ms[f"rdy_ln2_{g}"]]
                if u >= 2:
                    mw.append(ms[("qk", "ev", u - 2)])
                for i in range(NCH):
                    add("tensor", lambda t, slab=slab, i=i, blk=blk, b=bank: t.matmul(
                        b[:], slab[:, i * 128:(i + 1) * 128],
                        xbig[:, i * GT + blk * 512: i * GT + blk * 512 + 512],
                        start=(i == 0), stop=False), waits=mw if i == 0 else ())
                add("tensor", lambda t, oo=oo, blk=blk, b=bank: t.matmul(
                    b[:], w1q_row[0:1, oo * 128:(oo + 1) * 128],
                    negm_row[0:1, blk * 512: blk * 512 + 512],
                    start=False, stop=True),
                    waits=[ms["w1ev"]] if (g == 0 and blk == 0) else (),
                    inc=("cT", 1))
                wmm = ("cT", plan.at("cT"))
                if blk == NBLK - 1:
                    ms[("qk", "slabdone", oo)] = wmm
                t1 = t1_t[u % 2]
                qb = qb_t[u % 2]
                wt1 = [wmm]
                if u >= 2:
                    wt1.append(ms[("qk", "cons", u - 2)])
                add("vector", lambda v, b=bank, blk=blk, t1=t1: v.tensor_tensor(
                    t1[:], b[:], rb_sb[:, blk * 512: blk * 512 + 512],
                    op=ALU.mult), waits=wt1, inc=("cV", 1))
                add("vector", lambda v, t1=t1, qb=qb, oo=oo: v.tensor_scalar(
                    qb[:], t1[:], bq_sb[:, oo:oo + 1], None, ALU.add),
                    inc=("cV", 1))
                ms[("qk", "ev", u)] = ("cV", plan.at("cV"))
                wqb = ms[("qk", "ev", u)]
                if oo < 34:
                    is_q = oo < 32
                    cbase = (0 if is_q else 2) * 1024 + blk * 512
                    sbase = (1 if is_q else 3) * 1024 + blk * 512
                    bankp = (pC, pD)[u % 2]
                    wpp = [wqb]
                    if u >= 2:
                        wpp.append(ms[("qk", "permev", u - 2)])
                    add("tensor", lambda t, qb=qb, bp=bankp: t.matmul(
                        bp[:], permb[:], qb[:], start=True, stop=True),
                        waits=wpp, inc=("cT", 1))
                    wpm = ("cT", plan.at("cT"))
                    t1b = t1_t[u % 2]
                    add("vector", lambda v, bp=bankp, sbase=sbase, t1b=t1b:
                        v.tensor_tensor(t1b[:], bp[:],
                                        tabs[:, sbase:sbase + 512], op=ALU.mult),
                        waits=[wpm, wtab], inc=("cV", 1))
                    ms[("qk", "permev", u)] = ("cV", plan.at("cV"))
                    xm2 = xm_st[u % 2]
                    add("vector", lambda v, qb=qb, cbase=cbase, xm2=xm2:
                        v.tensor_tensor(xm2[:], qb[:],
                                        tabs[:, cbase:cbase + 512], op=ALU.mult),
                        inc=("cV", 1))
                    rq = rq_t[u % 2]
                    wrq_prev = [ms[("qk", "rqio", u - 2)]] if u >= 2 else []
                    add("vector", lambda v, t1b=t1b, xm2=xm2, rq=rq: v.tensor_add(
                        rq[:], t1b[:], xm2[:]), waits=wrq_prev, inc=("cV", 1))
                    wrq = ("cV", plan.at("cV"))
                    if is_q:
                        add("sync", lambda s, oo=oo, blk=blk, rq=rq: s.dma_start(
                            out=d_q[oo, :, gcol + blk * 512: gcol + blk * 512 + 512],
                            in_=rq[:]), waits=[wrq], inc=("dQ", 16))
                        ms[("qk", "rqio", u)] = ("dQ", plan.at("dQ"))
                    else:
                        kvh = oo - 32
                        coff = kvh * (128 * TOK) + gcol + blk * 512
                        add("sync", lambda s, rq=rq, coff=coff: s.dma_start(
                            out=dataclasses.replace(
                                d_shk[0, 0, :, 0:512], offset=_rt(s, "kw") + coff),
                            in_=rq[:]), waits=[wrq], inc=("dKw", 16))
                        ms[("qk", "rqio", u)] = ("dKw", plan.at("dKw"))
                    ms[("qk", "cons", u)] = ("cV", plan.at("cV"))
                else:
                    vh = oo - 34
                    wv0 = [wqb]
                    if ("qk", "vstlast") in ms:
                        wv0.append(ms[("qk", "vstlast")])
                        wv0.append(("dV", plan.at("dV")))
                    vstb = vst_t[blk]
                    for pt in range(4):
                        add("tensor", lambda t, qb=qb, pt=pt: t.transpose(
                            pT[:, pt * 128:(pt + 1) * 128],
                            qb[:, pt * 128:(pt + 1) * 128], identb[:]),
                            waits=wv0 if pt == 0 else (),
                            inc=("cT", 1) if pt == 3 else None)
                    wtr = ("cT", plan.at("cT"))
                    add("scalar", lambda s, vh=vh, vstb=vstb: s.activation(
                        vstb[:].rearrange("p (pt c) -> p pt c", c=ER)[
                            :, :, vh * 128:(vh + 1) * 128],
                        pT[:, 0:512].rearrange("p (pt c) -> p pt c", c=128),
                        AF.Copy, bias=0.0, scale=1.0),
                        waits=[wtr], inc=("cS", 1))
                    wvs = ("cS", plan.at("cS"))
                    ms[("qk", "vstlast")] = wvs
                    if vh == 1:
                        ptg = g * 8 + blk * 4
                        coff = ptg * (128 * ER)
                        add("sync", lambda s, coff=coff, vstb=vstb: s.dma_start(
                            out=dataclasses.replace(
                                d_shv[0, 0:4, :, :].rearrange("l p c -> p l c"),
                                offset=_rt(s, "vw") + coff),
                            in_=vstb[:].rearrange("p (pt c) -> p pt c", c=ER)),
                            waits=[wvs], inc=("dV", 16))
                    ms[("qk", "cons", u)] = ("cS", plan.at("cS"))
            if oo + 2 < QCH:
                qlw[oo + 2] = qslab_load(oo + 2, [ms[("qk", "slabdone", oo)]])
        ms["xbig_free"] = ("cT", plan.at("cT"))
        ms["tabs_free"] = ("cV", plan.at("cV"))
        ms[f"qkv_done_{g}"] = [("dQ", plan.at("dQ")), ("dKw", plan.at("dKw")),
                               ("dV", plan.at("dV"))]

    def flag_and_poll():
        add("sync", lambda s: s.dma_start(
            out=dataclasses.replace(d_shf[0, 0:2], offset=_rt(s, "fw")),
            in_=offs_sb[0:1, 4:6]), waits=ms["qkv_done_1"], inc=("dF", 16))

        def poll(s):
            rflag = s.alloc_register("rflag")
            rcnt = s.alloc_register("rcnt")
            rv_f = RuntimeValue(val=rflag, min_val=0, max_val=(1 << 30) - 1)
            rv_c = RuntimeValue(val=rcnt, min_val=0, max_val=(1 << 30) - 1)
            s.reg_mov(rcnt, 0)
            for widx, nkey in ((0, "n1"), (1, "n2")):
                nv = _rt(s, nkey)
                src = dataclasses.replace(d_shf[0, 0:2], offset=_rt(s, "fr"))
                s.reg_mov(rflag, 0)
                with s.While(lambda: rv_f != nv):
                    s.reg_add(rcnt, rcnt, 16)
                    s.dma_start(out=flagbuf[0:1, 0:2], in_=src).then_inc(
                        sems["pol"], 16)
                    s.wait_ge(sems["pol"], rv_c)
                    s.reg_load(rflag, flagbuf[0:1, widx:widx + 1])
            return s.nop()
        add("sync", poll)

    def phase2(only_tiles=None, only_hg=None, dump=False, only_h4=None):
        for sec in range(2):
            for kvh in range(NKV):
                add("sync", lambda s, sec=sec, kvh=kvh: s.dma_start(
                    out=ksb[:, (sec * NKV + kvh) * TOK:(sec * NKV + kvh + 1) * TOK],
                    in_=d_shk[sec, kvh, :, :]), inc=("dKV", 16))
        add("sync", lambda s: s.dma_start(
            out=vsb[:].rearrange("p (u c) -> p u c", c=ER),
            in_=d_shv.rearrange("h l p c -> p (h l) c")), inc=("dKV", 16))
        wkv = ("dKV", plan.at("dKV"))
        p4_free = None
        last_exp = {0: None, 1: None}
        last_norm = None
        scnt = [0]
        hpar = [0]
        for i in (range(NT) if only_tiles is None else only_tiles):
            seclen = (i + 1) * 128
            qt = qt_t[i % 2]
            qsem = f"dQt{i % 2}"
            qw = [] if i < 2 else [ms[("p2q", i - 2)]]
            add("sync", lambda s, i=i, qt=qt: s.dma_start(
                out=qt[:].rearrange("p (c q) -> p c q", q=128),
                in_=d_q[:, :, i * 128:(i + 1) * 128].rearrange("c p q -> p c q")),
                waits=qw, inc=(qsem, 16))
            wq = (qsem, plan.at(qsem))
            mt = mt_t[i % 2]
            msem = f"dMt{i % 2}"
            mw = [] if i < 2 else [ms[("p2m", i - 2)]]
            add("sync", lambda s, i=i, mt=mt: s.dma_start(
                out=mt[:], in_=d_bm[:, i * 256:(i + 1) * 256]),
                waits=mw, inc=(msem, 16))
            wmt = (msem, plan.at(msem))
            for hg in (range(8) if only_hg is None else only_hg):
                kvh = hg // 4
                otb = (pC, pD)[hg % 2]
                for h4 in (range(4) if only_h4 is None else only_h4):
                    head = hg * 4 + h4
                    par_ = hpar[0] % 2
                    hpar[0] += 1
                    srow = srow_t[par_]
                    col = h4 + 4 * (hg % 2)
                    cm0 = par_ * 12
                    ncm = 0
                    qsl = qt[:, head * 128:(head + 1) * 128]
                    for sec in range(2):
                        kbase = (sec * NKV + kvh) * TOK
                        sb0 = sec * seclen
                        for c0, cw in _chunks(i * 128, 512):
                            nsc = scnt[0]
                            bank = (pA, pB, pE, pF)[nsc % 4]
                            sw = [wkv, wq]
                            if ("p2sev", nsc - 4) in ms:
                                prev = ms[("p2sev", nsc - 4)]
                                sw.extend(prev if isinstance(prev, list) else [prev])
                            add("tensor", lambda t, b=bank, qsl=qsl, kbase=kbase,
                                c0=c0, cw=cw: t.matmul(
                                    b[:, 0:cw], qsl,
                                    ksb[:, kbase + c0:kbase + c0 + cw],
                                    start=True, stop=True),
                                waits=sw, inc=("cT", 1))
                            wsm = ("cT", plan.at("cT"))
                            sev = [wsm]
                            if last_exp[par_] is not None:
                                sev.append(last_exp[par_])
                            if nsc % 4 == 3:
                                add("vector", lambda v, b=bank, sb0=sb0, c0=c0,
                                    cw=cw, srow=srow: v.tensor_copy(
                                        srow[:, sb0 + c0: sb0 + c0 + cw],
                                        b[:, 0:cw]),
                                    waits=sev, inc=("cV", 1))
                                ms[("p2sev", nsc)] = ("cV", plan.at("cV"))
                            else:
                                add("scalar", lambda s, b=bank, sb0=sb0, c0=c0,
                                    cw=cw, srow=srow:
                                    s.activation(srow[:, sb0 + c0: sb0 + c0 + cw],
                                                 b[:, 0:cw], AF.Copy, bias=0.0,
                                                 scale=1.0),
                                    waits=sev, inc=("cS", 1))
                                ms[("p2sev", nsc)] = ("cS", plan.at("cS"))
                            scnt[0] += 1
                        nsc = scnt[0]
                        bank = (pA, pB, pE, pF)[nsc % 4]
                        sw = [wkv, wq, wmt]
                        if ("p2sev", nsc - 4) in ms:
                            prev = ms[("p2sev", nsc - 4)]
                            sw.extend(prev if isinstance(prev, list) else [prev])
                        add("tensor", lambda t, b=bank, qsl=qsl, kbase=kbase,
                            sb1=i * 128: t.matmul(
                                b[:, 0:128], qsl,
                                ksb[:, kbase + sb1:kbase + sb1 + 128],
                                start=True, stop=True),
                            waits=sw, inc=("cT", 1))
                        wsm = ("cT", plan.at("cT"))
                        sev = [wsm]
                        if last_exp[par_] is not None:
                            sev.append(last_exp[par_])
                        add("vector", lambda v, b=bank, sec=sec, sb0=sb0,
                            sb1=i * 128, mt=mt, srow=srow: v.scalar_tensor_tensor(
                                srow[:, sb0 + sb1: sb0 + sb1 + 128], b[:, 0:128],
                                1.0, mt[:, sec * 128:(sec + 1) * 128],
                                op0=ALU.mult, op1=ALU.add),
                            waits=sev, inc=("cV", 1))
                        ms[("p2sev", nsc)] = ("cV", plan.at("cV"))
                        scnt[0] += 1
                    rowlen = 2 * seclen
                    wall = [("cS", plan.at("cS")), ("cV", plan.at("cV"))]
                    if ("p2nmfree", col) in ms:
                        wall.append(ms[("p2nmfree", col)])
                    add("vector", lambda v, col=col, rowlen=rowlen, srow=srow:
                        v.reduce_max(
                            negmx[:, col:col + 1], srow[:, 0:rowlen],
                            axis=AX.X, negate=True),
                        waits=wall, inc=("cV", 1))
                    wmx = ("cV", plan.at("cV"))
                    wex = [wmx, ("cS", plan.at("cS"))]
                    if p4_free is not None:
                        wex.append(p4_free)
                    add("scalar", lambda s, h4=h4, rowlen=rowlen, srow=srow,
                        col=col: s.activation(
                            p4[:, h4 * 4096: h4 * 4096 + rowlen], srow[:, 0:rowlen],
                            AF.Exp, bias=negmx[:, col:col + 1], scale=1.0,
                            accum_out=sums4[:, col:col + 1]),
                        waits=wex, inc=("cS", 1))
                    last_exp[par_] = ("cS", plan.at("cS"))
                    ms[("p2nmfree", col)] = last_exp[par_]
                    add("vector", lambda v, col=col: v.reciprocal(
                        rs4[:, col:col + 1], sums4[:, col:col + 1]),
                        waits=[last_exp[par_]], inc=("cV", 1))
                    wrc = ("cV", plan.at("cV"))
                    add("vector", lambda v, h4=h4, rowlen=rowlen, col=col:
                        v.tensor_scalar(
                            p4[:, h4 * 4096: h4 * 4096 + rowlen],
                            p4[:, h4 * 4096: h4 * 4096 + rowlen],
                            rs4[:, col:col + 1], None, ALU.mult),
                        waits=[wrc], inc=("cV", 1))
                    ms[("p2pn", h4)] = ("cV", plan.at("cV"))
                    last_norm = ms[("p2pn", h4)]
                wpn = [ms[("p2pn", h4)] for h4 in
                       (range(4) if only_h4 is None else only_h4)]
                h4list = list(range(4) if only_h4 is None else only_h4)
                nh = len(h4list)
                nkt = 2 * (i + 1)
                for sw0 in range(0, nkt, 16):
                    kts = range(sw0, min(sw0 + 16, nkt))
                    for kt in kts:
                        ptbank = (pT[:], pR[:].bitcast(BF16))[kt % 2]
                        for hx, h4 in enumerate(h4list):
                            tw = []
                            if kt == 0 and hx == 0:
                                tw = list(wpn)
                            if kt >= 2 and hx == 0:
                                tw.append(ms[("p2pte", kt - 2)])
                            if kt >= 16 and hx == 0:
                                tw.append(ms[("p2pv", kt - 16)])
                            add("tensor", lambda t, kt=kt, h4=h4, hx=hx,
                                ptbank=ptbank: t.transpose(
                                    ptbank[:, hx * 128:(hx + 1) * 128],
                                    p4[:, h4 * 4096 + kt * 128:
                                       h4 * 4096 + (kt + 1) * 128],
                                    identb[:]),
                                waits=tw, inc=("cT", 1) if hx == nh - 1 else None)
                        wtr = ("cT", plan.at("cT"))
                        slot = kt % 16
                        if kt % 4 == 1:
                            add("vector", lambda v, slot=slot, ptbank=ptbank:
                                v.tensor_copy(
                                    ptb[:, slot * nh * 128:(slot + 1) * nh * 128],
                                    ptbank[:, 0:nh * 128]),
                                waits=[wtr], inc=("cV", 1))
                            ms[("p2pte", kt)] = ("cV", plan.at("cV"))
                        else:
                            add("scalar", lambda s, slot=slot, ptbank=ptbank:
                                s.activation(
                                    ptb[:, slot * nh * 128:(slot + 1) * nh * 128],
                                    ptbank[:, 0:nh * 128],
                                    AF.Copy, bias=0.0, scale=1.0),
                                waits=[wtr], inc=("cS", 1))
                            ms[("p2pte", kt)] = ("cS", plan.at("cS"))
                    for kt in kts:
                        sec = 1 if kt >= (i + 1) else 0
                        lt = kt - (i + 1) * sec
                        vcol = (sec * NT + lt) * ER + kvh * 128
                        slot = kt % 16
                        pw = [ms[("p2pte", kt)]]
                        if hg >= 2 and kt == 0:
                            pw.append(ms[("p2cte", hg - 2)])
                        add("tensor", lambda t, otb=otb, vcol=vcol, slot=slot,
                            kt=kt, last=nkt - 1: t.matmul(
                                otb[:, 0:nh * 128], vsb[:, vcol:vcol + 128],
                                ptb[:, slot * nh * 128:(slot + 1) * nh * 128],
                                start=(kt == 0), stop=(kt == last)),
                            waits=pw,
                            inc=("cT", 1))
                        ms[("p2pv", kt)] = ("cT", plan.at("cT"))
                p4_free = ("cT", plan.at("cT"))
                wpv = p4_free
                cst = ctxst[hg % 2]
                wcs = [wpv]
                if hg >= 2:
                    wcs.append(ms[("p2cio", hg - 2)])
                add("scalar", lambda s, otb=otb, cst=cst: s.activation(
                    cst[:], otb[:], AF.Copy, bias=0.0, scale=1.0),
                    waits=wcs, inc=("cS", 1))
                ms[("p2cte", hg)] = ("cS", plan.at("cS"))
                add("sync", lambda s, hg=hg, i=i, cst=cst: s.dma_start(
                    out=d_ctx[hg * 4:(hg + 1) * 4, :, i * 128:(i + 1) * 128]
                    .rearrange("h p q -> p h q"),
                    in_=cst[:].rearrange("p (h q) -> p h q", q=128)),
                    waits=[ms[("p2cte", hg)]], inc=("dCtx", 16))
                ms[("p2cio", hg)] = ("dCtx", plan.at("dCtx"))
            ms[("p2q", i)] = ("cT", plan.at("cT"))
            ms[("p2m", i)] = ("cV", plan.at("cV"))
            if dump:
                rl = 2 * (i + 1) * 128
                wd = [("cS", plan.at("cS")), ("cV", plan.at("cV")),
                      ("cG", plan.at("cG")), ("cT", plan.at("cT"))]
                add("gpsimd", lambda g, rl=rl: g.dma_start(
                    out=d_dbg[:, 0:rl], in_=srow_t[0][:, 0:rl]), waits=wd,
                    inc=("dDbg", 16))
                add("gpsimd", lambda g, rl=rl: g.dma_start(
                    out=d_dbg[:, 4096:4096 + rl], in_=p4[:, 0:rl]), waits=wd,
                    inc=("dDbg", 16))
                _pw = 2 * (i + 1) * (1 if only_h4 else 4) * 128
                add("gpsimd", lambda g, _pw=_pw: g.dma_start(
                    out=d_dbg[:, 8192:8192 + _pw], in_=ptb[:, 0:_pw]),
                    waits=wd, inc=("dDbg", 16))
                add("gpsimd", lambda g: g.dma_start(
                    out=d_dbg[:, 12288:12289], in_=sums4[:, 0:1],
                    allow_slow_non_contiguous=True), waits=wd,
                    inc=("dDbg", 16))
                add("gpsimd", lambda g: g.dma_start(
                    out=d_dbg[:, 12292:12293], in_=negmx[:, 0:1],
                    allow_slow_non_contiguous=True), waits=wd,
                    inc=("dDbg", 16))
                add("gpsimd", lambda g, i=i: g.dma_start(
                    out=d_dbg[:, 12296:12552], in_=qt_t[i % 2][:, 0:256]),
                    waits=wd, inc=("dDbg", 16))
                add("gpsimd", lambda g, i=i: g.dma_start(
                    out=d_dbg[:, 12552:12808], in_=mt_t[i % 2][:]), waits=wd,
                    inc=("dDbg", 16))
        ms["ctx_stored"] = ("dCtx", plan.at("dCtx"))

    def phase3_group(g):
        gcol = g * GT
        prev_ev3 = ("cV", plan.at("cV")) if g > 0 else None
        for blk in range(NBLK):
            cb = ctxb_t[blk]
            waits = [ms["ctx_stored"]]
            if g > 0:
                waits.append(ms["ffn2_done_0"])
            add("sync", lambda s, blk=blk, cb=cb: s.dma_start(
                out=cb[:].rearrange("p (c q) -> p c q", q=512),
                in_=d_ctx[:, :, gcol + blk * 512: gcol + blk * 512 + 512]
                .rearrange("c p q -> p c q")),
                waits=waits, inc=(f"dCb{blk}", 16))
        wcb = [("dCb0", plan.at("dCb0")), ("dCb1", plan.at("dCb1"))]

        def dslab_load(o, wprev):
            sem = ("dSlA", "dSlB")[o % 2]
            src = d_dT[:, o * 128:(o + 1) * 128].rearrange("(c p) k -> p c k", p=128)
            add("gpsimd", lambda gp, o=o, src=src: gp.dma_start(
                out=wslab[o % 2][:].rearrange("p (c k) -> p c k", k=128),
                in_=src), waits=wprev, inc=(sem, 16))
            return (sem, plan.at(sem))

        dlw = {0: dslab_load(0, [] if g == 0 else [ms["ffn2_done_0"]]),
               1: dslab_load(1, [])}
        for oo in range(NCH):
            slab = wslab[oo % 2]
            for blk in range(NBLK):
                u = oo * NBLK + blk
                bank = (pA, pB)[u % 2]
                xf = xf_t[u % 4]
                xsem = f"dXf{u % 4}"
                xw = [ms[("d3", "res", u - 4)]] if u >= 4 else []
                add("sync", lambda s, oo=oo, blk=blk, xf=xf: s.dma_start(
                    out=xf[:],
                    in_=d_xm[oo, :, gcol + blk * 512: gcol + blk * 512 + 512]),
                    waits=xw, inc=(xsem, 16))
                wxf = (xsem, plan.at(xsem))
                mw = [dlw[oo]] + wcb
                if u >= 2:
                    mw.append(ms[("d3", "t1", u - 2)])
                elif prev_ev3 is not None:
                    mw.append(prev_ev3)
                if u == 0 and "stats_free" in ms:
                    mw.append(ms["stats_free"])
                cb = ctxb_t[blk]
                for c in range(NCH):
                    add("tensor", lambda t, slab=slab, c=c, cb=cb, b=bank: t.matmul(
                        b[:], slab[:, c * 128:(c + 1) * 128],
                        cb[:, c * 512:(c + 1) * 512],
                        start=(c == 0), stop=(c == NCH - 1)),
                        waits=mw if c == 0 else (),
                        inc=("cT", 1) if c == NCH - 1 else None)
                wmm = ("cT", plan.at("cT"))
                if blk == NBLK - 1:
                    ms[("d3", "slabdone", oo)] = wmm
                xm = xm_st[u % 2]
                wt1 = [wmm, wxf, ("dXa", plan.at("dXa"))]
                if u >= 2:
                    wt1.append(ms[("d3", "xmio", u - 2)])
                add("vector", lambda v, b=bank, xf=xf, xm=xm: v.tensor_add(
                    xm[:], b[:], xf[:]), waits=wt1, inc=("cV", 1))
                ms[("d3", "t1", u)] = ("cV", plan.at("cV"))
                wev = ms[("d3", "t1", u)]
                add("sync", lambda s, oo=oo, blk=blk, xm=xm: s.dma_start(
                    out=d_xa[oo, :, gcol + blk * 512: gcol + blk * 512 + 512],
                    in_=xm[:]), waits=[wev], inc=("dXa", 16))
                ms[("d3", "res", u)] = ("dXa", plan.at("dXa"))
                stats_ops(u, oo, blk, xm, first=(oo == 0), last=(oo == NCH - 1),
                          tag="l3", wsrc=wev)
                ms[("d3", "xmio", u)] = ("cV", plan.at("cV"))
            if oo + 2 < NCH:
                dlw[oo + 2] = dslab_load(oo + 2, [ms[("d3", "slabdone", oo)]])
        ms[("d3", "slabdone_last")] = ms[("d3", "slabdone", NCH - 1)]
        stats_flush()
        ms[f"dense_done_{g}"] = ("cT", plan.at("cT"))
        ms["xatt_stored"] = ("dXa", plan.at("dXa"))
        ln_finalize(half=True, tag=f"ln3_{g}")
        ms["stats_free"] = ms[f"rdy_ln3_{g}"]

        add("gpsimd", lambda gp: gp.dma_start(
            out=xbig[:].rearrange("p (i t) -> p i t", t=GT),
            in_=d_xa[:, :, gcol:gcol + GT].rearrange("i p t -> p i t")),
            waits=[ms[f"dense_done_{g}"], ms["xatt_stored"]], inc=("dXB", 16))
        wxb = ("dXB", plan.at("dXB"))
        xa_adapters(f"ln3_{g}", wxb)

        def fslab_load(o, wprev):
            sem = ("dSlA", "dSlB")[o % 2]
            src = d_wf[:, o * 128:(o + 1) * 128].rearrange("(i p) c -> p i c", p=128)
            add("gpsimd", lambda gp, o=o, src=src: gp.dma_start(
                out=wslab[o % 2][:].rearrange("p (i c) -> p i c", c=128),
                in_=src), waits=wprev, inc=(sem, 16))
            return (sem, plan.at(sem))

        flw = {0: fslab_load(0, [ms[("d3", "slabdone_last")]]),
               1: fslab_load(1, [])}
        for oo in range(NCH):
            slab = wslab[oo % 2]
            for blk in range(NBLK):
                u = oo * NBLK + blk
                bank = (pA, pB)[u % 2]
                xf = xf_t[u % 4]
                xsem = f"dXf{u % 4}"
                xw = [ms[("f2", "out", u - 4)]] if u >= 4 else [ms["xatt_stored"]]
                add("sync", lambda s, oo=oo, blk=blk, xf=xf: s.dma_start(
                    out=xf[:],
                    in_=d_xa[oo, :, gcol + blk * 512: gcol + blk * 512 + 512]),
                    waits=xw, inc=(xsem, 16))
                wxf = (xsem, plan.at(xsem))
                mw = [flw[oo], wxb, ms["xa_rdy"]]
                if u >= 2:
                    mw.append(ms[("f2", "t1", u - 2)])
                for i in range(NCH):
                    add("tensor", lambda t, slab=slab, i=i, blk=blk, b=bank: t.matmul(
                        b[:], slab[:, i * 128:(i + 1) * 128],
                        xbig[:, i * GT + blk * 512: i * GT + blk * 512 + 512],
                        start=(i == 0), stop=False), waits=mw if i == 0 else ())
                add("tensor", lambda t, oo=oo, blk=blk, b=bank: t.matmul(
                    b[:], w1f_row[0:1, oo * 128:(oo + 1) * 128],
                    negm_row[0:1, blk * 512: blk * 512 + 512],
                    start=False, stop=False))
                for j in range(2):
                    add("tensor", lambda t, oo=oo, blk=blk, j=j, b=bank: t.matmul(
                        b[:], bSb[:, j * D + oo * 128: j * D + (oo + 1) * 128],
                        xab[:, (blk * 2 + j) * 512:(blk * 2 + j + 1) * 512],
                        start=False, stop=(j == 1)),
                        inc=("cT", 1) if j == 1 else None)
                wmm = ("cT", plan.at("cT"))
                if blk == NBLK - 1:
                    ms[("f2", "slabdone", oo)] = wmm
                t1 = t1_t[u % 2]
                xm = xm_st[u % 2]
                wt1 = [wmm]
                if u >= 2:
                    wt1.append(ms[("f2", "out", u - 2)])
                add("vector", lambda v, b=bank, blk=blk, t1=t1: v.tensor_tensor(
                    t1[:], b[:], rb_sb[:, blk * 512: blk * 512 + 512],
                    op=ALU.mult), waits=wt1, inc=("cV", 1))
                ms[("f2", "t1", u)] = ("cV", plan.at("cV"))
                add("vector", lambda v, t1=t1, xm=xm, oo=oo, xf=xf:
                    v.scalar_tensor_tensor(xm[:], t1[:], bfh_sb[:, oo:oo + 1],
                                           xf[:], op0=ALU.add, op1=ALU.add),
                    waits=[wxf, ("dSt", plan.at("dSt"))], inc=("cV", 1))
                wev = ("cV", plan.at("cV"))
                add("sync", lambda s, oo=oo, blk=blk, xm=xm: s.dma_start(
                    out=d_out[oo * 128:(oo + 1) * 128,
                              gcol + blk * 512: gcol + blk * 512 + 512],
                    in_=xm[:]), waits=[wev], inc=("dSt", 16))
                ms[("f2", "out", u)] = ("dSt", plan.at("dSt"))
            if oo + 2 < NCH:
                flw[oo + 2] = fslab_load(oo + 2, [ms[("f2", "slabdone", oo)]])
        ms[f"ffn2_done_{g}"] = ("cT", plan.at("cT"))

    nb = [0]
    preloads()
    if _DEBUG == "pre":
        add("sync", lambda s: s.dma_start(
            out=d_out[0:128, 0:QCH], in_=bq_sb[:]),
            waits=[ms["a1_done"]], inc=("dSt", 16))
    elif _DEBUG.startswith("grp0"):
        stop = _DEBUG[5:] or None
        phase1_group(0, stop_after=stop)
        if stop in ("ffn", "ln2"):
            add("sync", lambda s: s.dma_start(
                out=d_out[:, 0:GT].rearrange("(c p) t -> c p t", p=128),
                in_=d_xm[:, :, 0:GT]),
                waits=[ms["xmid_stored"]], inc=("dSt", 16))
        elif stop in ("ln1", "xa"):
            add("sync", lambda s: s.dma_start(
                out=d_out[0:128, 0:GT], in_=rb_sb[:]),
                waits=[ms[f"rdy_ln1_0"]], inc=("dSt", 16))
        else:
            add("sync", lambda s: s.dma_start(
                out=d_out.rearrange("(c p) t -> c p t", p=128),
                in_=d_xm[:, :, :]),
                waits=[ms["xmid_stored"]] + ms["qkv_done_0"], inc=("dSt", 16))
    else:
        for g in range(NG):
            phase1_group(g)
    if _DEBUG == "pre" or _DEBUG.startswith("grp0"):
        pass
    elif _DEBUG == "xmid":
        add("sync", lambda s: s.dma_start(
            out=d_out.rearrange("(c p) t -> c p t", p=128), in_=d_xm[:, :, :]),
            waits=[ms["xmid_stored"]], inc=("dSt", 16))
    elif _DEBUG == "qkv":
        add("gpsimd", lambda g: g.dma_start(
            out=d_out.rearrange("(c p) t -> c p t", p=128), in_=d_q[:, :, :]),
            waits=ms["qkv_done_1"] + [ms["xmid_stored"]], inc=("dSt", 16))
    elif _DEBUG == "poll":
        flag_and_poll()
        nb[0] += 1
        barrier(nb[0])
        add("gpsimd", lambda g: g.dma_start(
            out=d_out[0:1, 0:16], in_=flagbuf[:]), inc=("dDbg", 16))
    elif _DEBUG == "kvload":
        flag_and_poll()
        nb[0] += 1
        barrier(nb[0])
        for sec in range(2):
            for kvh in range(NKV):
                add("sync", lambda s, sec=sec, kvh=kvh: s.dma_start(
                    out=ksb[:, (sec * NKV + kvh) * TOK:(sec * NKV + kvh + 1) * TOK],
                    in_=d_shk[sec, kvh, :, :]), inc=("dKV", 16))
        add("sync", lambda s: s.dma_start(
            out=vsb[:].rearrange("p (u c) -> p u c", c=ER),
            in_=d_shv.rearrange("h l p c -> p (h l) c")), inc=("dKV", 16))
        wkv = ("dKV", plan.at("dKV"))
        lvl = int(os.environ.get("MK_KV", "1"))
        if lvl >= 2:
            add("sync", lambda s: s.dma_start(
                out=qt_t[0][:].rearrange("p (c q) -> p c q", q=128),
                in_=d_q[:, :, 0:128].rearrange("c p q -> p c q")),
                inc=("dQt0", 16))
            add("sync", lambda s: s.dma_start(
                out=mt_t[0][:], in_=d_bm[:, 0:256]), inc=("dMt0", 16))
            wq = ("dQt0", plan.at("dQt0"))
            wmt = ("dMt0", plan.at("dMt0"))
        if lvl >= 3:
            for sec in range(2):
                bank = (pA, pB)[sec]
                add("tensor", lambda t, b=bank, sec=sec: t.matmul(
                    b[:, 0:128], qt_t[0][:, 0:128],
                    ksb[:, (sec * NKV) * TOK:(sec * NKV) * TOK + 128],
                    start=True, stop=True), waits=[wkv, wq, wmt], inc=("cT", 1))
                wsm = ("cT", plan.at("cT"))
                add("vector", lambda v, b=bank, sec=sec: v.scalar_tensor_tensor(
                    srow[:, sec * 128:(sec + 1) * 128], b[:, 0:128], 1.0,
                    mt_t[0][:, sec * 128:(sec + 1) * 128],
                    op0=ALU.mult, op1=ALU.add), waits=[wsm], inc=("cV", 1))
            wse = ("cV", plan.at("cV"))
            add("vector", lambda v: v.reduce_max(
                negmx[:, 0:1], srow[:, 0:256], axis=AX.X, negate=True),
                waits=[wse], inc=("cV", 1))
            wmx = ("cV", plan.at("cV"))
            add("scalar", lambda s: s.activation(
                p4[:, 0:256], srow[:, 0:256], AF.Exp, bias=negmx[:, 0:1],
                scale=1.0, accum_out=sums4[:, 0:1]), waits=[wmx], inc=("cS", 1))
            wex = ("cS", plan.at("cS"))
            add("vector", lambda v: v.reciprocal(rs4[:, 0:1], sums4[:, 0:1]),
                waits=[wex], inc=("cV", 1))
            wrc = ("cV", plan.at("cV"))
            add("gpsimd", lambda g: g.tensor_scalar(
                p4[:, 0:256], p4[:, 0:256], rs4[:, 0:1], None, ALU.mult),
                waits=[wrc], inc=("cG", 1))
            wpn = ("cG", plan.at("cG"))
        if lvl >= 4:
            for kt in range(2):
                for hx in range(1):
                    add("tensor", lambda t, kt=kt: t.transpose(
                        pT[:, (kt % 2) * 128:((kt % 2) + 1) * 128],
                        p4[:, kt * 128:(kt + 1) * 128], identb[:]),
                        waits=[wpn] if kt == 0 else (), inc=("cT", 1))
                wtr = ("cT", plan.at("cT"))
                add("scalar", lambda s, kt=kt: s.activation(
                    ptb[:, kt * 128:(kt + 1) * 128],
                    pT[:, (kt % 2) * 128:((kt % 2) + 1) * 128],
                    AF.Copy, bias=0.0, scale=1.0), waits=[wtr], inc=("cS", 1))
                wpte = ("cS", plan.at("cS"))
                add("tensor", lambda t, kt=kt, wv=wpte: t.matmul(
                    pC[:, 0:128], vsb[:, (kt * NT) * ER: (kt * NT) * ER + 128],
                    ptb[:, kt * 128:(kt + 1) * 128],
                    start=(kt == 0), stop=(kt == 1)),
                    waits=[wpte], inc=("cT", 1) if kt == 1 else None)
            wpv = ("cT", plan.at("cT"))
            add("scalar", lambda s: s.activation(
                ctxst[0][:, 0:128], pC[:, 0:128], AF.Copy, bias=0.0, scale=1.0),
                waits=[wpv], inc=("cS", 1))
            wcte = ("cS", plan.at("cS"))
            add("sync", lambda s: s.dma_start(
                out=d_ctx[0:1, :, 0:128].rearrange("h p q -> p h q"),
                in_=ctxst[0][:, 0:128].rearrange("p (h q) -> p h q", q=128)),
                waits=[wcte], inc=("dCtx", 16))
        add("gpsimd", lambda g: g.dma_start(
            out=d_out[0:128, 0:2048], in_=ksb[:, 0:2048]),
            waits=[wkv, ("dCtx", plan.at("dCtx")), ("cG", plan.at("cG")),
                   ("cS", plan.at("cS"))], inc=("dDbg", 16))
    elif _DEBUG == "att0n":
        flag_and_poll()
        nb[0] += 1
        barrier(nb[0])
        _tl = [int(v) for v in os.environ.get("MK_T", "0").split(",")]
        _hgs = [int(v) for v in os.environ.get("MK_HG", "0").split(",")]
        phase2(only_tiles=_tl, only_hg=_hgs, dump=False)
        add("gpsimd", lambda g: g.dma_start(
            out=d_out[0:512, 0:2048].rearrange("(c p) t -> c p t", p=128),
            in_=d_ctx[0:4, :, :]),
            waits=[ms["ctx_stored"]], inc=("dDbg", 16))
    elif _DEBUG == "att0":
        flag_and_poll()
        nb[0] += 1
        barrier(nb[0])
        import os as _os
        _tl = [int(v) for v in _os.environ.get("MK_T", "0,1").split(",")]
        _h4 = _os.environ.get("MK_H4", "")
        phase2(only_tiles=_tl, only_hg=[0], dump=True,
               only_h4=[int(v) for v in _h4.split(",")] if _h4 else None)
        for a in range(8):
            add("gpsimd", lambda g, a=a: g.dma_start(
                out=d_out[a * 128:(a + 1) * 128, 0:2048],
                in_=d_dbg[:, a * 2048:(a + 1) * 2048]),
                waits=[("dDbg", plan.at("dDbg")), ms["ctx_stored"]],
                inc=("dDbg", 16))
    else:
        flag_and_poll()
        nb[0] += 1
        barrier(nb[0])
        phase2()
        if _DEBUG == "ctx":
            add("gpsimd", lambda g: g.dma_start(
                out=d_out.rearrange("(c p) t -> c p t", p=128), in_=d_ctx[:, :, :]),
                waits=[ms["ctx_stored"]], inc=("dSt", 16))
        else:
            nb[0] += 1
            barrier(nb[0])
            for g in range(NG):
                phase3_group(g)
    fin = plan.at("dSt")
    find = plan.at("dDbg")
    add("sync", lambda s: (s.wait_ge(sems["dDbg"], find) if find else None,
                           s.wait_ge(sems["dSt"], fin))[-1])

    for name in ("dCon", "dCn2", "dXB", "dTab", "dSlA", "dSlB",
                 "dXf0", "dXf1", "dXf2", "dXf3", "dXm", "dQ", "dKw", "dV",
                 "dF", "dKV", "dQt0", "dQt1", "dMt0", "dMt1", "dCtx",
                 "dCb0", "dCb1", "dXa", "dSt", "dDbg", "pol", "bar",
                 "cT", "cV", "cS", "cG"):
        sems[name] = cstk.enter_context(nc.semaphore(name))

    per_eng = {e: [] for e in ENGINES}
    for eng, fn, waits, incs in plan.seq:
        per_eng[eng].append((fn, waits, incs))

    with nc.Block() as block:
        def mk(ops):
            def body(eng):
                for fn, waits, incs in ops:
                    for s, v in waits:
                        eng.wait_ge(sems[s], v)
                    ins = fn(eng)
                    for s, n in incs:
                        tgt = ins if ins is not None else eng.nop()
                        tgt.then_inc(sems[s], n)
            return body
        block.sync(mk(per_eng["sync"]))
        block.gpsimd(mk(per_eng["gpsimd"]))
        block.tensor(mk(per_eng["tensor"]))
        block.vector(mk(per_eng["vector"]))
        block.scalar(mk(per_eng["scalar"]))

    return nc


def kernel(**inputs):
    in_maps = _host_prep(inputs)
    nc = build_program()
    res = run_bass_kernel_spmd(nc, in_maps, core_ids=list(range(8)))
    out = np.empty((NBATCH, S, D), np.float32)
    for c in range(8):
        b, par = c // 2, c % 2
        gl = [_g_of(par, i) for i in range(NT)]
        rows = np.concatenate([np.arange(g * 128, (g + 1) * 128) for g in gl])
        out[b, rows, :] = res.results[c]["outT"].T
    return out
